# version 04
# baseline (speedup 1.0000x reference)
import os
import numpy as np
import ml_dtypes
import concourse.bass as bass
import concourse.mybir as mybir
from concourse.bass_utils import run_bass_kernel_spmd

F32 = mybir.dt.float32
BF16 = mybir.dt.bfloat16
AF = mybir.ActivationFunctionType
ALU = mybir.AluOpType
AX = mybir.AxisListType

D = 2048
NDC = 16
F = 1024
NFC = 8
E = 16
SH = 2
H, KVH, HD = 16, 4, 128
ST = 1024
HALO = 256
STT = ST + HALO
NTI = STT // 128
SCALE = float(1.0 / np.sqrt(128.0))
NEG = -1e9
_BIS = os.environ.get("KBISECT", "full")
_DO_P1 = _BIS not in ("p0", "ffnonly")
_N_PAIRS = 0 if _BIS in ("p0", "p1") else (1 if _BIS == "ffn1" else 9)
_N_KVH = KVH if _DO_P1 else 0
_NO_P4 = os.environ.get("KNOP4", "") == "1"
_NO_RT = os.environ.get("KNORT", "") == "1"
_N_TI = int(os.environ.get("KNTI", str(NTI)))
_NSTEP = int(os.environ.get("KNSTEP", "99"))


def build_program():
    nc = bass.Bass("TRN2", target_bir_lowering=False)

    dr = {}
    def di(name, shape, dtype=F32):
        dr[name] = nc.dram_tensor(name, list(shape), dtype, kind="ExternalInput")

    di("x_sl", (2304, D))
    di("wq_p", (H, 128, D))
    di("wk_p", (KVH, 128, D))
    di("wv_p", (16, 128, 512))
    di("wo_p", (16, KVH, 128, 512))
    di("bq_r", (1, D)); di("bk_r", (1, 512)); di("bv_r", (1, 512)); di("bo_r", (1, D))
    di("wa_n", (D, E))
    di("rg_p", (E, NFC, 128, D)); di("rw1_p", (E, NFC, 128, D))
    di("rw2_p", (E, 16, 128, F))
    di("sg_p", (SH, NFC, 128, D)); di("sw1_p", (SH, NFC, 128, D))
    di("sw2_p", (SH, 16, 128, F))
    di("rgb_p", (128, E * NFC)); di("rb1_p", (128, E * NFC))
    di("sgb_p", (128, SH * NFC)); di("sb1_p", (128, SH * NFC))
    di("rb2_n", (E, D)); di("sb2_n", (SH, D))
    di("masks", (3, 128, 384))
    di("id128", (128, 128))
    di("emat", (E, E * 128), BF16)
    di("onesr", (1, 512), BF16)
    di("ones2", (2, 1024), BF16)
    out_h = nc.dram_tensor("out", [2048, D], F32, kind="ExternalOutput")

    def sa(name, shape, dtype):
        return nc.alloc_sbuf_tensor(name, list(shape), dtype)

    xnT = sa("xnT", (128, NDC * STT), BF16)
    acc = sa("acc", (128, NDC * ST), BF16)
    ring = sa("ring", (128, 8 * 2048), BF16)
    arena = sa("arena", (128, 41472), mybir.dt.uint8)
    abase = nc.lookup_mloc(arena).addr
    def aa(name, shape, dtype, off):
        return nc.alloc_sbuf_tensor_at(name, list(shape), dtype, offset=abase + off)
    xstage = aa("xstage", (128, 2 * D), F32, 0)
    sqstage = aa("sqstage", (128, D), F32, 16384)
    xnstage = aa("xnstage", (128, 2 * D), F32, 24576)
    KTb = aa("KTb", (128, KVH * STT), BF16, 0)
    Vtm = aa("Vtm", (128, NTI * 512), BF16, 10240)
    QTb = aa("QTb", (128, 4 * ST), BF16, 20480)
    OTk = aa("OTk", (128, 4 * ST), BF16, 28672)
    Asb = aa("Asb", (128, 2 * 384), BF16, 36864)
    A2b = aa("A2b", (128, 2 * 384), BF16, 38400)
    ATb = aa("ATb", (128, 2 * 384), BF16, 39936)
    y0 = aa("y0", (128, NFC * ST), BF16, 0)
    y1 = aa("y1", (128, NFC * ST), BF16, 16384)
    ygc = aa("ygc", (128, 1024), BF16, 32768)
    ostage = aa("ostage", (128, 2 * D), F32, 16384)
    masksb = sa("masksb", (128, 3 * 384), BF16)
    combT = sa("combT", (16, ST), BF16)
    cbb = sa("cbb", (128, ST), BF16)
    wasb = sa("wasb", (128, NDC * E), BF16)
    emsb = sa("emsb", (16, E * 128), BF16)
    idbf = sa("idbf", (128, 128), BF16)
    idf32 = sa("idf32", (128, 128), F32)
    onesb = sa("onesb", (1, 512), BF16)
    ones2b = sa("ones2b", (2, 1024), BF16)
    bqr = sa("bqr", (1, D), BF16)
    bkr = sa("bkr", (1, 512), BF16)
    bvr = sa("bvr", (1, 512), BF16)
    bor = sa("bor", (1, D), BF16)
    rb2sb = sa("rb2sb", (16, D), BF16)
    sb2sb = sa("sb2sb", (2, D), BF16)
    rgbp = sa("rgbp", (128, E * NFC), F32)
    rb1p = sa("rb1p", (128, E * NFC), F32)
    sgbp = sa("sgbp", (128, SH * NFC), F32)
    sb1p = sa("sb1p", (128, SH * NFC), F32)
    lnsum = sa("lnsum", (128, 1), F32)
    lnm = sa("lnm", (128, 1), F32)
    lnsq = sa("lnsq", (128, 2), F32)
    lnva = sa("lnva", (128, 1), F32)
    lnrv = sa("lnrv", (128, 1), F32)
    lnrstd = sa("lnrstd", (128, 1), F32)
    sig = sa("sig", (128, 16), F32)
    rm1 = sa("rm1", (128, 1), F32)
    raff2 = sa("raff2", (128, 16), F32)
    rm2 = sa("rm2", (128, 1), F32)
    rge2 = sa("rge2", (128, 16), F32)
    rss = sa("rss", (128, 1), F32)
    rrs = sa("rrs", (128, 1), F32)
    rcb = sa("rcb", (128, 16), F32)
    smx = sa("smx", (128, 2), F32)
    srs = sa("srs", (128, 2), F32)
    srr = sa("srr", (128, 2), F32)

    psum_ctx = [nc.psum_tensor("pf%d" % i, [128, 512], F32) for i in range(8)]
    pf = [p.__enter__() for p in psum_ctx]
    PFN = ["pf%d" % i for i in range(8)]

    sem_names = ["sp", "sv", "ss", "swp", "sxa", "sxb", "soa", "sob"] + \
                ["sr%d" % i for i in range(8)]
    sem_ctx = [nc.semaphore(n) for n in sem_names]
    sems = dict(zip(sem_names, [s.__enter__() for s in sem_ctx]))
    C = {k: 0 for k in sems}
    plan = {"sync": [], "gpsimd": [], "tensor": [], "vector": [], "scalar": []}
    FREE = {}

    def em(eng, fn):
        plan[eng].append(fn)

    def wt(eng, sem, val):
        if val and val > 0:
            em(eng, lambda EE, s=sems[sem], v=val: EE.wait_ge(s, v))

    def bump(inc, n=1):
        C[inc] += n
        return C[inc]

    def mm(bank_ap, lhsT_ap, rhs_ap, start, stop, inc=None):
        cnt = bump(inc) if inc else None
        def fn(EE, o=bank_ap, l=lhsT_ap, r=rhs_ap, a=start, b=stop, s=(sems[inc] if inc else None)):
            ins = EE.matmul(o, l, r, start=a, stop=b)
            if s is not None:
                ins.then_inc(s, 1)
        em("tensor", fn)
        return cnt

    def mmtp(out_ap, in_ap, ident_ap):
        return mm(out_ap, in_ap, ident_ap, True, True, inc="sp")

    def vop(builder, inc="sv"):
        cnt = bump(inc)
        em("vector", lambda EE, b=builder, s=sems[inc]: b(EE).then_inc(s, 1))
        return cnt

    def sop(builder, inc="ss"):
        cnt = bump(inc)
        em("scalar", lambda EE, b=builder, s=sems[inc]: b(EE).then_inc(s, 1))
        return cnt

    def dma(eng, dst_ap, src_ap, inc):
        cnt = bump(inc, 16)
        em(eng, lambda EE, o=dst_ap, i=src_ap, s=sems[inc]: EE.dma_start(o, i).then_inc(s, 16))
        return cnt

    def bank_wait_and_claim(key):
        ent = FREE.pop(key, None)
        if ent:
            wt("tensor", ent[0], ent[1])

    def bank_release(key, sem, cnt):
        FREE[key] = (sem, cnt)

    def emit_jobs(jobs):
        for j0 in range(0, len(jobs), 2):
            pair = jobs[j0:j0 + 2]
            for jb in pair:
                for s, v in jb.get("waits", ()):
                    wt("tensor", s, v)
                bank_wait_and_claim(jb["bankkey"])
            nk = max(len(jb["steps"]) for jb in pair)
            for k in range(nk):
                for jb in pair:
                    if k >= len(jb["steps"]):
                        continue
                    l, r = jb["steps"][k]
                    last = k == len(jb["steps"]) - 1
                    c = mm(jb["bank"], l, r, k == 0, last, inc="sp" if last else None)
                    if last:
                        jb["done"] = c
            for jb in pair:
                bank_release(jb["bankkey"], *jb["evac"](jb))

    class Ring:
        def __init__(self, init_release):
            self.i = 0
            self.release = [init_release] * 8

        def take(self):
            s = self.i % 8
            self.i += 1
            return s

    def load_chunk(rg_ap, src_ap, slot_release, slot):
        wt("gpsimd", "sp", slot_release)
        c = dma("gpsimd", rg_ap, src_ap, "sr%d" % slot)
        return ("sr%d" % slot, c)

    def rreg(s, cols=2048):
        return ring[:, s * 2048:s * 2048 + cols]

    pre = [
        (masksb.rearrange("p (m t) -> p m t", t=384), dr["masks"].rearrange("m p t -> p m t")),
        (idbf[:, :], dr["id128"][:, :]),
        (idf32[:, :], dr["id128"][:, :]),
        (emsb[:, :], dr["emat"][:, :]),
        (onesb[:, :], dr["onesr"][:, :]),
        (ones2b[:, :], dr["ones2"][:, :]),
        (bqr[:, :], dr["bq_r"][:, :]),
        (bkr[:, :], dr["bk_r"][:, :]),
        (bvr[:, :], dr["bv_r"][:, :]),
        (bor[:, :], dr["bo_r"][:, :]),
        (rb2sb[:, :], dr["rb2_n"][:, :]),
        (sb2sb[:, :], dr["sb2_n"][:, :]),
        (rgbp[:, :], dr["rgb_p"][:, :]),
        (rb1p[:, :], dr["rb1_p"][:, :]),
        (sgbp[:, :], dr["sgb_p"][:, :]),
        (sb1p[:, :], dr["sb1_p"][:, :]),
    ]
    for o, i in pre:
        dma("gpsimd", o, i, "swp")
    for dc in range(NDC):
        dma("gpsimd", wasb[:, dc * E:(dc + 1) * E], dr["wa_n"][dc * 128:(dc + 1) * 128, :], "swp")
    SW_PRE = C["swp"]
    for eng in ("tensor", "vector", "scalar"):
        wt(eng, "swp", SW_PRE)

    for st in range(2):
        row0 = st * 1024

        if st > 0:
            for eng in ("vector", "scalar", "sync"):
                wt(eng, "sp", C["sp"])
                wt(eng, "soa", C["soa"])
                wt(eng, "sob", C["sob"])
            wt("vector", "ss", C["ss"])
            wt("gpsimd", "sp", C["sp"])

        evtile = {}
        for ti in range(_N_TI):
            buf = ti % 2
            xs = xstage[:, buf * D:(buf + 1) * D]
            xn = xnstage[:, buf * D:(buf + 1) * D]
            wt("sync", "sv", C["sv"])
            wt("sync", "ss", C["ss"])
            sxn = "sxa" if buf == 0 else "sxb"
            cx = dma("sync", xs, dr["x_sl"][row0 + ti * 128:row0 + ti * 128 + 128, :], sxn)
            if _NSTEP < 2:
                continue
            wt("vector", sxn, cx)
            c1 = vop(lambda EE, i=xs: EE.reduce_sum(lnsum[:, :], i, axis=AX.X))
            if _NSTEP < 3:
                continue
            wt("scalar", sxn, cx)
            sqa = lnsq[:, buf:buf + 1]
            c2 = sop(lambda EE, i=xs, a=sqa: EE.activation(sqstage[:, :], i, AF.Square, accum_out=a))
            if _NSTEP < 4:
                continue
            wt("vector", "sv", c1)
            c3 = vop(lambda EE: EE.tensor_scalar(lnm[:, :], lnsum[:, :], 1.0 / D, None, ALU.mult))
            wt("vector", "sv", c3)
            c4 = vop(lambda EE: EE.scalar_tensor_tensor(
                lnva[:, :], lnm[:, :], -1.0, lnm[:, :], ALU.mult, ALU.mult))
            wt("vector", "ss", c2)
            wt("vector", "sv", c4)
            c5 = vop(lambda EE, a=sqa: EE.tensor_scalar(
                lnva[:, :], a, 1.0 / D, lnva[:, 0:1], ALU.mult, ALU.add))
            wt("vector", "sv", c5)
            c6 = vop(lambda EE: EE.tensor_scalar(lnrv[:, :], lnva[:, :], 1e-5, None, ALU.add))
            wt("vector", "sv", c6)
            c7 = vop(lambda EE: EE.reciprocal(lnrv[:, :], lnrv[:, :]))
            wt("scalar", "sv", c7)
            c8 = sop(lambda EE: EE.activation(lnrstd[:, :], lnrv[:, :], AF.Sqrt))
            if _NSTEP < 5:
                continue
            wt("vector", "ss", c8)
            c9 = vop(lambda EE, i=xs, o=xn: EE.tensor_scalar(
                o, i, lnm[:, 0:1], lnrstd[:, 0:1], ALU.subtract, ALU.mult))
            if _NSTEP < 6:
                continue
            wt("tensor", "sv", c9)
            for dc in range(NDC):
                key = PFN[dc % 6]
                reg = pf[dc % 6][:, 0:128]
                bank_wait_and_claim(key)
                cc = mmtp(reg, xn[:, dc * 128:(dc + 1) * 128], idf32[:, :])
                dst = xnT[:, dc * STT + ti * 128:dc * STT + (ti + 1) * 128]
                wt("vector", "sp", cc)
                ce = vop(lambda EE, d=dst, r=reg: EE.tensor_copy(d, r))
                bank_release(key, "sv", ce)
            evtile[ti] = C["sv"]
            if ti >= 2 and not _NO_RT:
                tok0 = ti * 128 - HALO
                pa = pf[6 + ti % 2]
                key = PFN[6 + ti % 2]
                bank_wait_and_claim(key)
                wt("tensor", "sv", evtile[ti])
                ca = None
                for dc in range(NDC):
                    lhs = xnT[:, dc * STT + ti * 128:dc * STT + (ti + 1) * 128]
                    rhs = wasb[:, dc * E:(dc + 1) * E]
                    ca = mm(pa[:, 0:16], lhs, rhs, dc == 0, dc == NDC - 1,
                            inc="sp" if dc == NDC - 1 else None)
                wt("scalar", "sp", ca)
                cs = sop(lambda EE, p=pa: EE.activation(sig[:, :], p[:, 0:16], AF.Sigmoid))
                wt("vector", "ss", cs)
                c = vop(lambda EE: EE.reduce_max(rm1[:, :], sig[:, :], axis=AX.X))
                wt("vector", "sv", c)
                c = vop(lambda EE: EE.tensor_scalar(rge2[:, :], sig[:, :], rm1[:, 0:1], None, ALU.is_ge))
                wt("vector", "sv", c)
                c = vop(lambda EE: EE.scalar_tensor_tensor(
                    raff2[:, :], rge2[:, :], NEG, sig[:, :], ALU.mult, ALU.add))
                wt("vector", "sv", c)
                c = vop(lambda EE: EE.reduce_max(rm2[:, :], raff2[:, :], axis=AX.X))
                wt("vector", "sv", c)
                c = vop(lambda EE: EE.tensor_scalar(rge2[:, :], sig[:, :], rm2[:, 0:1], None, ALU.is_ge))
                c = vop(lambda EE: EE.tensor_add(rss[:, :], rm1[:, :], rm2[:, :]))
                wt("vector", "sv", c)
                c = vop(lambda EE: EE.reciprocal(rrs[:, :], rss[:, :]))
                wt("vector", "sv", c)
                c = vop(lambda EE: EE.tensor_mul(rcb[:, :], sig[:, :], rge2[:, :]))
                wt("vector", "sv", c)
                ccb = vop(lambda EE: EE.tensor_scalar(rcb[:, :], rcb[:, :], rrs[:, 0:1], None, ALU.mult))
                wt("tensor", "sv", ccb)
                key2 = PFN[6 + (ti + 1) % 2]
                pa2 = pf[6 + (ti + 1) % 2]
                bank_wait_and_claim(key2)
                ctp = mm(pa2[0:16, 0:128], rcb[:, 0:16], idf32[:, :], True, True, inc="sp")
                wt("vector", "sp", ctp)
                dst = combT[:, tok0:tok0 + 128]
                ce2 = vop(lambda EE, d=dst, p=pa2: EE.tensor_copy(d, p[0:16, 0:128]))
                bank_release(key2, "sv", ce2)
                bank_release(key, "ss", cs)

        fr = Ring(C["sp"])
        xnT_ok = C["sv"]
        KVHN = _N_KVH

        def cp_evac(dst_ap):
            def cb(jb):
                wt("vector", "sp", jb["done"])
                c = vop(lambda EE, d=dst_ap, b=jb["bank"]: EE.tensor_copy(d, b))
                return ("sv", c)
            return cb

        kjobs = []
        for kvh in range(KVHN):
            s = fr.take()
            ck = load_chunk(rreg(s), dr["wk_p"][kvh], fr.release[s], s)
            for tix, (t0, tw) in enumerate([(0, 512), (512, 512), (1024, 256)]):
                steps = [(rreg(s)[:, dc * 128:(dc + 1) * 128],
                          xnT[:, dc * STT + t0:dc * STT + t0 + tw]) for dc in range(NDC)]
                steps.append((bkr[0:1, kvh * 128:(kvh + 1) * 128], onesb[0:1, 0:tw]))
                bidx = (kvh * 3 + tix) % 4
                kjobs.append(dict(bankkey=PFN[bidx], bank=pf[bidx][:, 0:tw], steps=steps,
                                  waits=[ck, ("sv", xnT_ok)],
                                  evac=cp_evac(KTb[:, kvh * STT + t0:kvh * STT + t0 + tw]),
                                  slot=s))
        emit_jobs(kjobs)
        for kvh in range(KVHN):
            fr.release[kjobs[kvh * 3]["slot"]] = kjobs[kvh * 3 + 2]["done"]

        vslots = []
        for dc2 in range(8 if KVHN else 0):
            s = fr.take()
            src = dr["wv_p"][dc2 * 2:dc2 * 2 + 2].rearrange("c p m -> p c m")
            cv = load_chunk(rreg(s, 1024).rearrange("p (c m) -> p c m", m=512), src,
                            fr.release[s], s)
            vslots.append((s, cv))
        vjobs = []
        for tb in range(NTI if KVHN else 0):
            steps = []
            for dc in range(NDC):
                s, _ = vslots[dc // 2]
                steps.append((xnT[:, dc * STT + tb * 128:dc * STT + (tb + 1) * 128],
                              rreg(s)[:, (dc % 2) * 512:(dc % 2 + 1) * 512]))
            steps.append((onesb[0:1, 0:128], bvr[0:1, :]))
            vjobs.append(dict(bankkey=PFN[tb % 4], bank=pf[tb % 4][:, :], steps=steps,
                              waits=[w for _, w in vslots] + [("sv", xnT_ok)],
                              evac=cp_evac(Vtm[:, tb * 512:(tb + 1) * 512])))
        emit_jobs(vjobs)
        for s, _ in vslots:
            fr.release[s] = vjobs[-1]["done"]

        for kvh in range(KVHN):
            qslot = {}
            qck = {}
            for sh in range(4):
                s = fr.take()
                qslot[sh] = s
                qck[sh] = load_chunk(rreg(s), dr["wq_p"][kvh * 4 + sh], fr.release[s], s)

            def q_evac(dst_ap):
                def cb(jb):
                    wt("vector", "sp", jb["done"])
                    c = vop(lambda EE, d=dst_ap, b=jb["bank"]: EE.tensor_scalar(
                        d, b, SCALE, None, ALU.mult))
                    return ("sv", c)
                return cb

            qjobs = []
            for sh in range(4):
                for tt in range(2):
                    steps = [(rreg(qslot[sh])[:, dc * 128:(dc + 1) * 128],
                              xnT[:, dc * STT + HALO + tt * 512:dc * STT + HALO + (tt + 1) * 512])
                             for dc in range(NDC)]
                    hh = kvh * 4 + sh
                    steps.append((bqr[0:1, hh * 128:(hh + 1) * 128], onesb[0:1, 0:512]))
                    bidx = (sh * 2 + tt) % 4
                    qjobs.append(dict(bankkey=PFN[bidx], bank=pf[bidx][:, :], steps=steps,
                                      waits=[qck[sh]],
                                      evac=q_evac(QTb[:, sh * ST + tt * 512:sh * ST + (tt + 1) * 512])))
            emit_jobs(qjobs)
            for sh in range(4):
                fr.release[qslot[sh]] = qjobs[sh * 2 + 1]["done"]
            qt_ok = C["sv"]

            at_done = {}
            pending = None

            def emit_atav(p):
                it, dbl, qb, sh = p["it"], p["dbl"], p["qb"], p["sh"]
                wt("tensor", "sv", p["ca2"])
                at_ok = None
                for t3 in range(3):
                    key = PFN[2 + t3]
                    reg = pf[2 + t3][:, 0:128]
                    bank_wait_and_claim(key)
                    cc = mmtp(reg, A2b[:, dbl * 384 + t3 * 128:dbl * 384 + (t3 + 1) * 128],
                              idbf[:, :])
                    wt("vector", "sp", cc)
                    dst = ATb[:, dbl * 384 + t3 * 128:dbl * 384 + (t3 + 1) * 128]
                    ce = vop(lambda EE, d=dst, r=reg: EE.tensor_copy(d, r))
                    bank_release(key, "sv", ce)
                    at_ok = ce
                okey = PFN[dbl]
                pot = pf[dbl][:, 0:128]
                bank_wait_and_claim(okey)
                wt("tensor", "sv", at_ok)
                cot = None
                for t3 in range(3):
                    tbv = qb + t3
                    cot = mm(pot, Vtm[:, tbv * 512 + kvh * 128:tbv * 512 + (kvh + 1) * 128],
                             ATb[:, dbl * 384 + t3 * 128:dbl * 384 + (t3 + 1) * 128],
                             t3 == 0, t3 == 2, inc="sp" if t3 == 2 else None)
                at_done[it] = cot
                wt("vector", "sp", cot)
                dst = OTk[:, sh * ST + qb * 128:sh * ST + (qb + 1) * 128]
                ceo = vop(lambda EE, d=dst, p=pot: EE.tensor_copy(d, p))
                bank_release(okey, "sv", ceo)

            for qb in range(8):
                mi = 0 if (st == 0 and qb == 0) else (1 if (st == 0 and qb == 1) else 2)
                for sh in range(4):
                    it = qb * 4 + sh
                    dbl = it % 2
                    pkey = PFN[6 + dbl]
                    psc = pf[6 + dbl]
                    bank_wait_and_claim(pkey)
                    wt("tensor", "sv", qt_ok)
                    mm(psc[:, 0:384], QTb[:, sh * ST + qb * 128:sh * ST + (qb + 1) * 128],
                       KTb[:, kvh * STT + qb * 128:kvh * STT + qb * 128 + 384], True, False)
                    csc = mm(psc[:, 0:384], idbf[:, :], masksb[:, mi * 384:(mi + 1) * 384],
                             False, True, inc="sp")
                    wt("vector", "sp", csc)
                    cmx = vop(lambda EE, p=psc, d=dbl: EE.reduce_max(
                        smx[:, d:d + 1], p[:, 0:384], axis=AX.X, negate=True))
                    wt("scalar", "sv", cmx)
                    cex = sop(lambda EE, p=psc, d=dbl: EE.activation(
                        Asb[:, d * 384:(d + 1) * 384], p[:, 0:384], AF.Exp,
                        bias=smx[:, d:d + 1], accum_out=srs[:, d:d + 1]))
                    bank_release(pkey, "ss", cex)
                    wt("vector", "ss", cex)
                    crr = vop(lambda EE, d=dbl: EE.reciprocal(srr[:, d:d + 1], srs[:, d:d + 1]))
                    wt("vector", "sv", crr)
                    if it >= 2:
                        wt("vector", "sp", at_done.get(it - 2, 0))
                    ca2 = vop(lambda EE, d=dbl: EE.tensor_scalar(
                        A2b[:, d * 384:(d + 1) * 384], Asb[:, d * 384:(d + 1) * 384],
                        srr[:, d:d + 1], None, ALU.mult))
                    me = dict(it=it, dbl=dbl, qb=qb, sh=sh, ca2=ca2)
                    if pending is not None:
                        emit_atav(pending)
                    pending = me
            if pending is not None:
                emit_atav(pending)
                pending = None
            ot_ok = C["sv"]

            for dout in range(16):
                s = fr.take()
                src = dr["wo_p"][dout, kvh]
                cw = load_chunk(rreg(s, 512), src, fr.release[s], s)
                jobs = []
                for tt in range(2):
                    steps = [(rreg(s, 512)[:, sh * 128:(sh + 1) * 128],
                              OTk[:, sh * ST + tt * 512:sh * ST + (tt + 1) * 512])
                             for sh in range(4)]
                    if kvh == 0:
                        steps.append((bor[0:1, dout * 128:(dout + 1) * 128], onesb[0:1, 0:512]))
                    aslice = acc[:, dout * ST + tt * 512:dout * ST + (tt + 1) * 512]
                    if kvh == 0:
                        xslice = xnT[:, dout * STT + HALO + tt * 512:dout * STT + HALO + (tt + 1) * 512]
                        def acc_evac(jb, a=aslice, xx=xslice):
                            wt("vector", "sp", jb["done"])
                            c = vop(lambda EE, aa=a, b=jb["bank"], x2=xx:
                                    EE.tensor_tensor(aa, b, x2, ALU.add))
                            return ("sv", c)
                    else:
                        def acc_evac(jb, a=aslice):
                            wt("vector", "sp", jb["done"])
                            c = vop(lambda EE, aa=a, b=jb["bank"]:
                                    EE.tensor_tensor(aa, b, aa, ALU.add))
                            return ("sv", c)
                    bidx = (dout * 2 + tt) % 4
                    jobs.append(dict(bankkey=PFN[bidx], bank=pf[bidx][:, :], steps=steps,
                                     waits=[cw, ("sv", ot_ok)], evac=acc_evac))
                emit_jobs(jobs)
                fr.release[s] = jobs[1]["done"]

        def wsrc(kind, e, j):
            if kind == "g":
                t = dr["rg_p"][e, j] if e < E else dr["sg_p"][e - E, j]
            elif kind == "w1":
                t = dr["rw1_p"][e, j] if e < E else dr["sw1_p"][e - E, j]
            else:
                t = dr["rw2_p"][e, j] if e < E else dr["sw2_p"][e - E, j]
            return t

        prev_stt = {0: 0, 1: 0}
        for pe_i in range(_N_PAIRS):
            pair_es = [2 * pe_i, 2 * pe_i + 1]
            ybufs = {pair_es[0]: y0, pair_es[1]: y1}
            for e in pair_es:
                cb_ok = 0
                if e < E:
                    for tt in range(2):
                        key = PFN[6 + tt]
                        bank_wait_and_claim(key)
                        wt("tensor", "sv", C["sv"])
                        cm = mm(pf[6 + tt][:, :], emsb[:, e * 128:(e + 1) * 128],
                                combT[:, tt * 512:(tt + 1) * 512], True, True, inc="sp")
                        wt("vector", "sp", cm)
                        dst = cbb[:, tt * 512:(tt + 1) * 512]
                        ce = vop(lambda EE, d=dst, p=pf[6 + tt]: EE.tensor_copy(d, p[:, :]))
                        bank_release(key, "sv", ce)
                        cb_ok = ce
                if e < E:
                    bias_gb = rgbp[:, e * NFC:(e + 1) * NFC]
                    bias_b1 = rb1p[:, e * NFC:(e + 1) * NFC]
                else:
                    bias_gb = sgbp[:, (e - E) * NFC:(e - E + 1) * NFC]
                    bias_b1 = sb1p[:, (e - E) * NFC:(e - E + 1) * NFC]
                ybuf = ybufs[e]
                for fc in range(NFC):
                    sg_ = fr.take()
                    s1_ = fr.take()
                    cg = load_chunk(rreg(sg_), wsrc("g", e, fc), fr.release[sg_], sg_)
                    cw = load_chunk(rreg(s1_), wsrc("w1", e, fc), fr.release[s1_], s1_)
                    bsel = (fc % 3) * 2
                    banks = [bsel % 6, (bsel + 1) % 6, (bsel + 2) % 6, (bsel + 3) % 6]
                    jobs = []
                    for part, (slot, cload) in enumerate([(sg_, cg), (s1_, cw)]):
                        for tt in range(2):
                            steps = [(rreg(slot)[:, dc * 128:(dc + 1) * 128],
                                      xnT[:, dc * STT + HALO + tt * 512:
                                          dc * STT + HALO + (tt + 1) * 512])
                                     for dc in range(NDC)]
                            bidx = banks[part * 2 + tt]
                            gb_ap = bias_gb[:, fc:fc + 1]
                            b1_ap = bias_b1[:, fc:fc + 1]
                            yg_ap = ygc[:, tt * 512:(tt + 1) * 512]
                            ydst = ybuf[:, fc * ST + tt * 512:fc * ST + (tt + 1) * 512]
                            if part == 0:
                                def gelu_evac(jb, o=yg_ap, bb=gb_ap, pv=prev_stt, t=tt):
                                    wt("scalar", "sp", jb["done"])
                                    wt("scalar", "sv", pv[t])
                                    c = sop(lambda EE, oo=o, b=jb["bank"], b2=bb:
                                            EE.activation(oo, b, AF.Gelu, bias=b2[:, 0:1]))
                                    return ("ss", c)
                                jobs.append(dict(bankkey=PFN[bidx], bank=pf[bidx][:, :],
                                                 steps=steps, evac=gelu_evac,
                                                 waits=[cload, ("sv", xnT_ok)]))
                            else:
                                cbs_ap = cbb[:, tt * 512:(tt + 1) * 512]
                                def stt_evac(jb, o=ydst, bb=b1_ap, yg=yg_ap, ee=e, cbk=cb_ok,
                                             cba=cbs_ap, pv=prev_stt, t=tt):
                                    wt("vector", "sp", jb["done"])
                                    wt("vector", "ss", jb["gelu_ss"])
                                    c = vop(lambda EE, oo=o, b=jb["bank"], b2=bb, yy=yg:
                                            EE.scalar_tensor_tensor(
                                                oo, b, b2[:, 0:1], yy, ALU.add, ALU.mult))
                                    if ee < E:
                                        wt("vector", "sv", max(c, cbk))
                                        c = vop(lambda EE, oo=o, cc2=cba: EE.tensor_tensor(
                                            oo, oo, cc2, ALU.mult))
                                    pv[t] = c
                                    return ("sv", c)
                                jobs.append(dict(bankkey=PFN[bidx], bank=pf[bidx][:, :],
                                                 steps=steps, evac=stt_evac, tt=tt,
                                                 waits=[cload]))
                    emit_jobs(jobs[0:2])
                    jobs[2]["gelu_ss"] = C["ss"]
                    jobs[3]["gelu_ss"] = C["ss"]
                    emit_jobs(jobs[2:4])
                    fr.release[sg_] = jobs[1]["done"]
                    fr.release[s1_] = jobs[3]["done"]
            y_ok = C["sv"]

            for dout in range(16):
                slots = {}
                cls = {}
                for e in pair_es:
                    s = fr.take()
                    slots[e] = s
                    cls[e] = load_chunk(rreg(s, 1024), wsrc("w2", e, dout), fr.release[s], s)
                jobs = []
                for tt in range(2):
                    steps = []
                    for e in pair_es:
                        for fc in range(NFC):
                            steps.append((rreg(slots[e], 1024)[:, fc * 128:(fc + 1) * 128],
                                          ybufs[e][:, fc * ST + tt * 512:fc * ST + (tt + 1) * 512]))
                    if pe_i == 0:
                        steps.append((rb2sb[:, dout * 128:(dout + 1) * 128],
                                      combT[:, tt * 512:(tt + 1) * 512]))
                        steps.append((sb2sb[:, dout * 128:(dout + 1) * 128],
                                      ones2b[:, tt * 512:(tt + 1) * 512]))
                    aslice = acc[:, dout * ST + tt * 512:dout * ST + (tt + 1) * 512]
                    def add_evac(jb, a=aslice):
                        wt("vector", "sp", jb["done"])
                        c = vop(lambda EE, aa=a, b=jb["bank"]: EE.tensor_tensor(aa, b, aa, ALU.add))
                        return ("sv", c)
                    bidx = (dout * 2 + tt) % 4
                    jobs.append(dict(bankkey=PFN[bidx], bank=pf[bidx][:, :], steps=steps,
                                     waits=[cls[pair_es[0]], cls[pair_es[1]], ("sv", y_ok)],
                                     evac=add_evac))
                emit_jobs(jobs)
                for e in pair_es:
                    fr.release[slots[e]] = jobs[1]["done"]

        acc_ok = C["sv"]
        ost_cnt = {}
        for ti in range(0 if _NO_P4 else 8):
            buf = ti % 2
            xs = xstage[:, buf * D:(buf + 1) * D]
            osb = ostage[:, buf * D:(buf + 1) * D]
            wt("sync", "sv", C["sv"])
            wt("sync", "ss", C["ss"])
            wt("sync", "sp", C["sp"])
            sxn = "sxa" if buf == 0 else "sxb"
            cx = dma("sync", xs, dr["x_sl"][row0 + HALO + ti * 128:row0 + HALO + ti * 128 + 128, :],
                     sxn)
            wt("vector", sxn, cx)
            if ti >= 2:
                wt("vector", "soa" if buf == 0 else "sob", ost_cnt[ti - 2])
            wt("tensor", "sv", acc_ok)
            for dc in range(NDC):
                key = PFN[dc % 6]
                reg = pf[dc % 6][:, 0:128]
                bank_wait_and_claim(key)
                cc = mmtp(reg, acc[:, dc * ST + ti * 128:dc * ST + (ti + 1) * 128], idbf[:, :])
                wt("vector", "sp", cc)
                ce = vop(lambda EE, o=osb, d=dc, r=reg, i=xs: EE.tensor_tensor(
                    o[:, d * 128:(d + 1) * 128], r, i[:, d * 128:(d + 1) * 128], ALU.add))
                bank_release(key, "sv", ce)
            wt("sync", "sv", C["sv"])
            ost_cnt[ti] = dma("sync", out_h[st * 1024 + ti * 128:st * 1024 + (ti + 1) * 128, :],
                              osb, "soa" if buf == 0 else "sob")

    wt("sync", "soa", C["soa"])
    wt("sync", "sob", C["sob"])

    with nc.Block() as block:
        @block.sync
        def _(eng):
            for fn in plan["sync"]:
                fn(eng)

        @block.gpsimd
        def _(eng):
            for fn in plan["gpsimd"]:
                fn(eng)

        @block.tensor
        def _(eng):
            for fn in plan["tensor"]:
                fn(eng)

        @block.vector
        def _(eng):
            for fn in plan["vector"]:
                fn(eng)

        @block.scalar
        def _(eng):
            for fn in plan["scalar"]:
                fn(eng)

    for s in reversed(sem_ctx):
        s.__exit__(None, None, None)
    for p in reversed(psum_ctx):
        p.__exit__(None, None, None)
    return nc


def prep_inputs(L0_attn_x, L0_attn_Wq, L0_attn_bq, L0_attn_Wk, L0_attn_bk,
                L0_attn_Wv, L0_attn_bv, L0_attn_Wo, L0_attn_bo, L0_ffn_wa,
                L0_ffn_rg, L0_ffn_rgb, L0_ffn_rw1, L0_ffn_rb1, L0_ffn_rw2,
                L0_ffn_rb2, L0_ffn_sg, L0_ffn_sgb, L0_ffn_sw1, L0_ffn_sb1,
                L0_ffn_sw2, L0_ffn_sb2):
    f32 = np.float32
    asnp = lambda a: np.asarray(a, dtype=f32)
    x = asnp(L0_attn_x)
    sh = {}
    sh["wq_p"] = np.ascontiguousarray(
        asnp(L0_attn_Wq).reshape(16, 128, H, 128).transpose(2, 1, 0, 3).reshape(H, 128, D))
    sh["wk_p"] = np.ascontiguousarray(
        asnp(L0_attn_Wk).reshape(16, 128, KVH, 128).transpose(2, 1, 0, 3).reshape(KVH, 128, D))
    sh["wv_p"] = asnp(L0_attn_Wv).reshape(16, 128, 512)
    sh["wo_p"] = np.ascontiguousarray(
        asnp(L0_attn_Wo).reshape(KVH, 4, 128, 16, 128).transpose(3, 0, 2, 1, 4)
        .reshape(16, KVH, 128, 512))
    sh["bq_r"] = asnp(L0_attn_bq).reshape(1, D)
    sh["bk_r"] = asnp(L0_attn_bk).reshape(1, 512)
    sh["bv_r"] = asnp(L0_attn_bv).reshape(1, 512)
    sh["bo_r"] = asnp(L0_attn_bo).reshape(1, D)
    sh["wa_n"] = asnp(L0_ffn_wa)
    def dfp(w, ne):
        return np.ascontiguousarray(
            w.reshape(ne, 16, 128, NFC, 128).transpose(0, 3, 2, 1, 4).reshape(ne, NFC, 128, D))
    def fdp(w, ne):
        return np.ascontiguousarray(
            w.reshape(ne, NFC, 128, 16, 128).transpose(0, 3, 2, 1, 4).reshape(ne, 16, 128, F))
    sh["rg_p"] = dfp(asnp(L0_ffn_rg), E)
    sh["rw1_p"] = dfp(asnp(L0_ffn_rw1), E)
    sh["rw2_p"] = fdp(asnp(L0_ffn_rw2), E)
    sh["sg_p"] = dfp(asnp(L0_ffn_sg), SH)
    sh["sw1_p"] = fdp(asnp(L0_ffn_sw2), SH) if False else dfp(asnp(L0_ffn_sw1), SH)
    sh["sw2_p"] = fdp(asnp(L0_ffn_sw2), SH)
    sh["rgb_p"] = np.ascontiguousarray(
        asnp(L0_ffn_rgb).reshape(E, NFC, 128).transpose(2, 0, 1).reshape(128, E * NFC))
    sh["rb1_p"] = np.ascontiguousarray(
        asnp(L0_ffn_rb1).reshape(E, NFC, 128).transpose(2, 0, 1).reshape(128, E * NFC))
    sh["sgb_p"] = np.ascontiguousarray(
        asnp(L0_ffn_sgb).reshape(SH, NFC, 128).transpose(2, 0, 1).reshape(128, SH * NFC))
    sh["sb1_p"] = np.ascontiguousarray(
        asnp(L0_ffn_sb1).reshape(SH, NFC, 128).transpose(2, 0, 1).reshape(128, SH * NFC))
    sh["rb2_n"] = asnp(L0_ffn_rb2)
    sh["sb2_n"] = asnp(L0_ffn_sb2)
    sh["id128"] = np.eye(128, dtype=f32)
    em = np.zeros((E, E * 128), dtype=ml_dtypes.bfloat16)
    for e in range(E):
        em[e, e * 128:(e + 1) * 128] = 1
    sh["emat"] = em
    sh["onesr"] = np.ones((1, 512), dtype=ml_dtypes.bfloat16)
    sh["ones2"] = np.ones((2, 1024), dtype=ml_dtypes.bfloat16)

    qi = np.arange(128)[:, None]
    tj = np.arange(384)[None, :]
    band = (tj >= qi + 1) & (tj <= qi + 256)
    mk = lambda m: np.where(m, 0.0, NEG).astype(f32)
    m_band = mk(band)
    m_e0 = mk(band & (tj >= 256))
    m_e1 = mk(band & (tj >= 128))

    in_maps = []
    for c in range(8):
        b, h = c // 2, c % 2
        sl = np.zeros((2304, D), dtype=f32)
        lo = h * 2048 - 256
        src_lo = max(lo, 0)
        sl[src_lo - lo:, :] = x[b, src_lo:h * 2048 + 2048, :]
        m = dict(sh)
        m["x_sl"] = sl
        m["masks"] = (np.stack([m_e0, m_e1, m_band]) if h == 0
                      else np.stack([m_band, m_band, m_band]))
        in_maps.append(m)
    return in_maps


def kernel(**inputs):
    in_maps = prep_inputs(**inputs)
    nc = build_program()
    res = run_bass_kernel_spmd(nc, in_maps, core_ids=list(range(8)))
    out = np.empty((4, 4096, D), dtype=np.float32)
    for c in range(8):
        b, h = c // 2, c % 2
        out[b, h * 2048:(h + 1) * 2048, :] = res.results[c]["out"]
    return out


# version 05
# speedup vs baseline: 1.2741x; 1.2741x over previous
import os
import numpy as np
import ml_dtypes
import concourse.bass as bass
import concourse.mybir as mybir
from concourse.bass_utils import run_bass_kernel_spmd

F32 = mybir.dt.float32
BF16 = mybir.dt.bfloat16
AF = mybir.ActivationFunctionType
ALU = mybir.AluOpType
AX = mybir.AxisListType

D = 2048
NDC = 16
F = 1024
NFC = 8
E = 16
SH = 2
H, KVH, HD = 16, 4, 128
ST = 1024
HALO = 256
STT = ST + HALO
NTI = STT // 128
SCALE = float(1.0 / np.sqrt(128.0))
NEG = -1e9
_BIS = os.environ.get("KBISECT", "full")
_DO_P1 = _BIS not in ("p0", "ffnonly")
_N_PAIRS = 0 if _BIS in ("p0", "p1") else (1 if _BIS == "ffn1" else 9)
_N_KVH = KVH if _DO_P1 else 0
_NO_P4 = os.environ.get("KNOP4", "") == "1"
_NO_RT = os.environ.get("KNORT", "") == "1"
_N_TI = int(os.environ.get("KNTI", str(NTI)))
_NSTEP = int(os.environ.get("KNSTEP", "99"))


def build_program():
    nc = bass.Bass("TRN2", target_bir_lowering=False)

    dr = {}
    def di(name, shape, dtype=F32):
        dr[name] = nc.dram_tensor(name, list(shape), dtype, kind="ExternalInput")

    di("x_sl", (2304, D))
    di("wq_p", (H, 128, D))
    di("wk_p", (KVH, 128, D))
    di("wv_p", (16, 128, 512))
    di("wo_p", (16, KVH, 128, 512))
    di("bq_r", (1, D)); di("bk_r", (1, 512)); di("bv_r", (1, 512)); di("bo_r", (1, D))
    di("wa_n", (D, E))
    di("rg_p", (E, NFC, 128, D)); di("rw1_p", (E, NFC, 128, D))
    di("rw2_p", (E, 16, 128, F))
    di("sg_p", (SH, NFC, 128, D)); di("sw1_p", (SH, NFC, 128, D))
    di("sw2_p", (SH, 16, 128, F))
    di("rgb_p", (128, E * NFC)); di("rb1_p", (128, E * NFC))
    di("sgb_p", (128, SH * NFC)); di("sb1_p", (128, SH * NFC))
    di("rb2_n", (E, D)); di("sb2_n", (SH, D))
    di("masks", (3, 128, 384))
    di("id128", (128, 128))
    di("emat", (E, E * 128), BF16)
    di("onesr", (1, 512), BF16)
    di("ones2", (2, 1024), BF16)
    out_h = nc.dram_tensor("out", [2048, D], F32, kind="ExternalOutput")

    def sa(name, shape, dtype):
        return nc.alloc_sbuf_tensor(name, list(shape), dtype)

    xnT = sa("xnT", (128, NDC * STT), BF16)
    acc = sa("acc", (128, NDC * ST), BF16)
    ring = sa("ring", (128, 8 * 2048), BF16)
    arena = sa("arena", (128, 41472), mybir.dt.uint8)
    abase = nc.lookup_mloc(arena).addr
    def aa(name, shape, dtype, off):
        return nc.alloc_sbuf_tensor_at(name, list(shape), dtype, offset=abase + off)
    xstage = aa("xstage", (128, 2 * D), F32, 0)
    sqstage = aa("sqstage", (128, D), F32, 16384)
    xnstage = aa("xnstage", (128, 2 * D), F32, 24576)
    KTb = aa("KTb", (128, KVH * STT), BF16, 0)
    Vtm = aa("Vtm", (128, NTI * 512), BF16, 10240)
    QTb = aa("QTb", (128, 4 * ST), BF16, 20480)
    OTk = aa("OTk", (128, 4 * ST), BF16, 28672)
    Asb = aa("Asb", (128, 2 * 384), BF16, 36864)
    A2b = aa("A2b", (128, 2 * 384), BF16, 38400)
    ATb = aa("ATb", (128, 2 * 384), BF16, 39936)
    y0 = aa("y0", (128, NFC * ST), BF16, 0)
    y1 = aa("y1", (128, NFC * ST), BF16, 16384)
    ygc = aa("ygc", (128, 1024), BF16, 32768)
    ostage = aa("ostage", (128, 2 * D), F32, 16384)
    masksb = sa("masksb", (128, 3 * 384), BF16)
    combT = sa("combT", (16, ST), BF16)
    cbb = sa("cbb", (128, ST), BF16)
    wasb = sa("wasb", (128, NDC * E), BF16)
    emsb = sa("emsb", (16, E * 128), BF16)
    idbf = sa("idbf", (128, 128), BF16)
    idf32 = sa("idf32", (128, 128), F32)
    onesb = sa("onesb", (1, 512), BF16)
    ones2b = sa("ones2b", (2, 1024), BF16)
    bqr = sa("bqr", (1, D), BF16)
    bkr = sa("bkr", (1, 512), BF16)
    bvr = sa("bvr", (1, 512), BF16)
    bor = sa("bor", (1, D), BF16)
    rb2sb = sa("rb2sb", (16, D), BF16)
    sb2sb = sa("sb2sb", (2, D), BF16)
    rgbp = sa("rgbp", (128, E * NFC), F32)
    rb1p = sa("rb1p", (128, E * NFC), F32)
    sgbp = sa("sgbp", (128, SH * NFC), F32)
    sb1p = sa("sb1p", (128, SH * NFC), F32)
    lnsum = sa("lnsum", (128, 1), F32)
    lnm = sa("lnm", (128, 2), F32)
    lnsq = sa("lnsq", (128, 2), F32)
    lnva = sa("lnva", (128, 1), F32)
    lnrv = sa("lnrv", (128, 2), F32)
    lnrstd = sa("lnrstd", (128, 2), F32)
    sig = sa("sig", (128, 16), F32)
    rm1 = sa("rm1", (128, 1), F32)
    raff2 = sa("raff2", (128, 16), F32)
    rm2 = sa("rm2", (128, 1), F32)
    rge2 = sa("rge2", (128, 16), F32)
    rss = sa("rss", (128, 1), F32)
    rrs = sa("rrs", (128, 1), F32)
    rcb = sa("rcb", (128, 16), F32)
    smx = sa("smx", (128, 3), F32)
    srs = sa("srs", (128, 3), F32)
    srr = sa("srr", (128, 3), F32)

    psum_ctx = [nc.psum_tensor("pf%d" % i, [128, 512], F32) for i in range(8)]
    pf = [p.__enter__() for p in psum_ctx]
    PFN = ["pf%d" % i for i in range(8)]

    sem_names = ["sp", "sv", "ss", "swp", "sxa", "sxb", "soa", "sob"] + \
                ["sr%d" % i for i in range(8)]
    sem_ctx = [nc.semaphore(n) for n in sem_names]
    sems = dict(zip(sem_names, [s.__enter__() for s in sem_ctx]))
    C = {k: 0 for k in sems}
    plan = {"sync": [], "gpsimd": [], "tensor": [], "vector": [], "scalar": []}
    FREE = {}

    def em(eng, fn):
        plan[eng].append(fn)

    def wt(eng, sem, val):
        if val and val > 0:
            em(eng, lambda EE, s=sems[sem], v=val: EE.wait_ge(s, v))

    def bump(inc, n=1):
        C[inc] += n
        return C[inc]

    def mm(bank_ap, lhsT_ap, rhs_ap, start, stop, inc=None):
        cnt = bump(inc) if inc else None
        def fn(EE, o=bank_ap, l=lhsT_ap, r=rhs_ap, a=start, b=stop, s=(sems[inc] if inc else None)):
            ins = EE.matmul(o, l, r, start=a, stop=b)
            if s is not None:
                ins.then_inc(s, 1)
        em("tensor", fn)
        return cnt

    def mmtp(out_ap, in_ap, ident_ap):
        return mm(out_ap, in_ap, ident_ap, True, True, inc="sp")

    def vop(builder, inc="sv"):
        cnt = bump(inc)
        em("vector", lambda EE, b=builder, s=sems[inc]: b(EE).then_inc(s, 1))
        return cnt

    def sop(builder, inc="ss"):
        cnt = bump(inc)
        em("scalar", lambda EE, b=builder, s=sems[inc]: b(EE).then_inc(s, 1))
        return cnt

    def dma(eng, dst_ap, src_ap, inc):
        cnt = bump(inc, 16)
        em(eng, lambda EE, o=dst_ap, i=src_ap, s=sems[inc]: EE.dma_start(o, i).then_inc(s, 16))
        return cnt

    def bank_wait_and_claim(key):
        ent = FREE.pop(key, None)
        if ent:
            wt("tensor", ent[0], ent[1])

    def bank_release(key, sem, cnt):
        FREE[key] = (sem, cnt)

    def emit_jobs(jobs):
        for j0 in range(0, len(jobs), 2):
            pair = jobs[j0:j0 + 2]
            for jb in pair:
                for s, v in jb.get("waits", ()):
                    wt("tensor", s, v)
                bank_wait_and_claim(jb["bankkey"])
            nk = max(len(jb["steps"]) for jb in pair)
            for k in range(nk):
                for jb in pair:
                    if k >= len(jb["steps"]):
                        continue
                    l, r = jb["steps"][k]
                    last = k == len(jb["steps"]) - 1
                    c = mm(jb["bank"], l, r, k == 0, last, inc="sp" if last else None)
                    if last:
                        jb["done"] = c
            for jb in pair:
                bank_release(jb["bankkey"], *jb["evac"](jb))

    class Ring:
        def __init__(self, init_release):
            self.i = 0
            self.release = [init_release] * 8

        def take(self):
            s = self.i % 8
            self.i += 1
            return s

    def load_chunk(rg_ap, src_ap, slot_release, slot):
        wt("gpsimd", "sp", slot_release)
        c = dma("gpsimd", rg_ap, src_ap, "sr%d" % slot)
        return ("sr%d" % slot, c)

    def rreg(s, cols=2048):
        return ring[:, s * 2048:s * 2048 + cols]

    pre = [
        (masksb.rearrange("p (m t) -> p m t", t=384), dr["masks"].rearrange("m p t -> p m t")),
        (idbf[:, :], dr["id128"][:, :]),
        (idf32[:, :], dr["id128"][:, :]),
        (emsb[:, :], dr["emat"][:, :]),
        (onesb[:, :], dr["onesr"][:, :]),
        (ones2b[:, :], dr["ones2"][:, :]),
        (bqr[:, :], dr["bq_r"][:, :]),
        (bkr[:, :], dr["bk_r"][:, :]),
        (bvr[:, :], dr["bv_r"][:, :]),
        (bor[:, :], dr["bo_r"][:, :]),
        (rb2sb[:, :], dr["rb2_n"][:, :]),
        (sb2sb[:, :], dr["sb2_n"][:, :]),
        (rgbp[:, :], dr["rgb_p"][:, :]),
        (rb1p[:, :], dr["rb1_p"][:, :]),
        (sgbp[:, :], dr["sgb_p"][:, :]),
        (sb1p[:, :], dr["sb1_p"][:, :]),
    ]
    for o, i in pre:
        dma("gpsimd", o, i, "swp")
    for dc in range(NDC):
        dma("gpsimd", wasb[:, dc * E:(dc + 1) * E], dr["wa_n"][dc * 128:(dc + 1) * 128, :], "swp")
    SW_PRE = C["swp"]
    for eng in ("tensor", "vector", "scalar"):
        wt(eng, "swp", SW_PRE)

    for st in range(2):
        row0 = st * 1024

        if st > 0:
            for eng in ("vector", "scalar", "sync"):
                wt(eng, "sp", C["sp"])
                wt(eng, "soa", C["soa"])
                wt(eng, "sob", C["sob"])
            wt("vector", "ss", C["ss"])
            wt("gpsimd", "sp", C["sp"])

        evtile = {}

        def p0_stats(ti):
            buf = ti % 2
            xs = xstage[:, buf * D:(buf + 1) * D]
            wt("sync", "sv", C["sv"])
            wt("sync", "ss", C["ss"])
            sxn = "sxa" if buf == 0 else "sxb"
            cx = dma("sync", xs, dr["x_sl"][row0 + ti * 128:row0 + ti * 128 + 128, :], sxn)
            lm = lnm[:, buf:buf + 1]
            lrv = lnrv[:, buf:buf + 1]
            sqa = lnsq[:, buf:buf + 1]
            wt("vector", sxn, cx)
            c1 = vop(lambda EE, i=xs: EE.reduce_sum(lnsum[:, :], i, axis=AX.X))
            wt("scalar", sxn, cx)
            c2 = sop(lambda EE, i=xs, a=sqa: EE.activation(sqstage[:, :], i, AF.Square, accum_out=a))
            wt("vector", "sv", c1)
            c3 = vop(lambda EE, o=lm: EE.tensor_scalar(o, lnsum[:, :], 1.0 / D, None, ALU.mult))
            wt("vector", "sv", c3)
            c4 = vop(lambda EE, m=lm: EE.scalar_tensor_tensor(
                lnva[:, :], m, -1.0, m, ALU.mult, ALU.mult))
            wt("vector", "ss", c2)
            wt("vector", "sv", c4)
            c5 = vop(lambda EE, a=sqa: EE.tensor_scalar(
                lnva[:, :], a, 1.0 / D, lnva[:, 0:1], ALU.mult, ALU.add))
            wt("vector", "sv", c5)
            c6 = vop(lambda EE, o=lrv: EE.tensor_scalar(o, lnva[:, :], 1e-5, None, ALU.add))
            wt("vector", "sv", c6)
            c7 = vop(lambda EE, o=lrv: EE.reciprocal(o, o))
            wt("scalar", "sv", c7)
            c8 = sop(lambda EE, i=lrv, o=lnrstd[:, buf:buf + 1]: EE.activation(o, i, AF.Sqrt))
            return c8

        def p0_xn(ti, c8):
            buf = ti % 2
            xs = xstage[:, buf * D:(buf + 1) * D]
            xn = xnstage[:, buf * D:(buf + 1) * D]
            lm = lnm[:, buf:buf + 1]
            wt("vector", "ss", c8)
            c9 = vop(lambda EE, i=xs, o=xn, m=lm, r=lnrstd[:, buf:buf + 1]: EE.tensor_scalar(
                o, i, m, r, ALU.subtract, ALU.mult))
            wt("tensor", "sv", c9)
            for dc in range(NDC):
                key = PFN[dc % 6]
                reg = pf[dc % 6][:, 0:128]
                bank_wait_and_claim(key)
                cc = mmtp(reg, xn[:, dc * 128:(dc + 1) * 128], idf32[:, :])
                dst = xnT[:, dc * STT + ti * 128:dc * STT + (ti + 1) * 128]
                wt("vector", "sp", cc)
                ce = vop(lambda EE, d=dst, r=reg: EE.tensor_copy(d, r))
                bank_release(key, "sv", ce)
            evtile[ti] = C["sv"]
            if ti >= 2 and not _NO_RT:
                tok0 = ti * 128 - HALO
                pa = pf[6 + ti % 2]
                key = PFN[6 + ti % 2]
                bank_wait_and_claim(key)
                wt("tensor", "sv", evtile[ti])
                ca = None
                for dc in range(NDC):
                    lhs = xnT[:, dc * STT + ti * 128:dc * STT + (ti + 1) * 128]
                    rhs = wasb[:, dc * E:(dc + 1) * E]
                    ca = mm(pa[:, 0:16], lhs, rhs, dc == 0, dc == NDC - 1,
                            inc="sp" if dc == NDC - 1 else None)
                wt("scalar", "sp", ca)
                cs = sop(lambda EE, p=pa: EE.activation(sig[:, :], p[:, 0:16], AF.Sigmoid))
                wt("vector", "ss", cs)
                c = vop(lambda EE: EE.reduce_max(rm1[:, :], sig[:, :], axis=AX.X))
                wt("vector", "sv", c)
                c = vop(lambda EE: EE.tensor_scalar(rge2[:, :], sig[:, :], rm1[:, 0:1], None, ALU.is_ge))
                wt("vector", "sv", c)
                c = vop(lambda EE: EE.scalar_tensor_tensor(
                    raff2[:, :], rge2[:, :], NEG, sig[:, :], ALU.mult, ALU.add))
                wt("vector", "sv", c)
                c = vop(lambda EE: EE.reduce_max(rm2[:, :], raff2[:, :], axis=AX.X))
                wt("vector", "sv", c)
                c = vop(lambda EE: EE.tensor_scalar(rge2[:, :], sig[:, :], rm2[:, 0:1], None, ALU.is_ge))
                c = vop(lambda EE: EE.tensor_add(rss[:, :], rm1[:, :], rm2[:, :]))
                wt("vector", "sv", c)
                c = vop(lambda EE: EE.reciprocal(rrs[:, :], rss[:, :]))
                wt("vector", "sv", c)
                c = vop(lambda EE: EE.tensor_mul(rcb[:, :], sig[:, :], rge2[:, :]))
                wt("vector", "sv", c)
                ccb = vop(lambda EE: EE.tensor_scalar(rcb[:, :], rcb[:, :], rrs[:, 0:1], None, ALU.mult))
                wt("tensor", "sv", ccb)
                key2 = PFN[6 + (ti + 1) % 2]
                pa2 = pf[6 + (ti + 1) % 2]
                bank_wait_and_claim(key2)
                ctp = mm(pa2[0:16, 0:128], rcb[:, 0:16], idf32[:, :], True, True, inc="sp")
                wt("vector", "sp", ctp)
                dst = combT[:, tok0:tok0 + 128]
                ce2 = vop(lambda EE, d=dst, p=pa2: EE.tensor_copy(d, p[0:16, 0:128]))
                bank_release(key2, "sv", ce2)
                bank_release(key, "ss", cs)

        p0_pend = None
        for ti in range(_N_TI):
            c8t = p0_stats(ti)
            if p0_pend is not None:
                p0_xn(*p0_pend)
            p0_pend = (ti, c8t)
        if p0_pend is not None:
            p0_xn(*p0_pend)
            p0_pend = None

        fr = Ring(C["sp"])
        xnT_ok = C["sv"]
        KVHN = _N_KVH

        def cp_evac(dst_ap):
            def cb(jb):
                wt("vector", "sp", jb["done"])
                c = vop(lambda EE, d=dst_ap, b=jb["bank"]: EE.tensor_copy(d, b))
                return ("sv", c)
            return cb

        kjobs = []
        for kvh in range(KVHN):
            s = fr.take()
            ck = load_chunk(rreg(s), dr["wk_p"][kvh], fr.release[s], s)
            for tix, (t0, tw) in enumerate([(0, 512), (512, 512), (1024, 256)]):
                steps = [(rreg(s)[:, dc * 128:(dc + 1) * 128],
                          xnT[:, dc * STT + t0:dc * STT + t0 + tw]) for dc in range(NDC)]
                steps.append((bkr[0:1, kvh * 128:(kvh + 1) * 128], onesb[0:1, 0:tw]))
                bidx = (kvh * 3 + tix) % 4
                kjobs.append(dict(bankkey=PFN[bidx], bank=pf[bidx][:, 0:tw], steps=steps,
                                  waits=[ck, ("sv", xnT_ok)],
                                  evac=cp_evac(KTb[:, kvh * STT + t0:kvh * STT + t0 + tw]),
                                  slot=s))
        emit_jobs(kjobs)
        for kvh in range(KVHN):
            fr.release[kjobs[kvh * 3]["slot"]] = kjobs[kvh * 3 + 2]["done"]

        vslots = []
        for dc2 in range(8 if KVHN else 0):
            s = fr.take()
            src = dr["wv_p"][dc2 * 2:dc2 * 2 + 2].rearrange("c p m -> p c m")
            cv = load_chunk(rreg(s, 1024).rearrange("p (c m) -> p c m", m=512), src,
                            fr.release[s], s)
            vslots.append((s, cv))
        vjobs = []
        for tb in range(NTI if KVHN else 0):
            steps = []
            for dc in range(NDC):
                s, _ = vslots[dc // 2]
                steps.append((xnT[:, dc * STT + tb * 128:dc * STT + (tb + 1) * 128],
                              rreg(s)[:, (dc % 2) * 512:(dc % 2 + 1) * 512]))
            steps.append((onesb[0:1, 0:128], bvr[0:1, :]))
            vjobs.append(dict(bankkey=PFN[tb % 4], bank=pf[tb % 4][:, :], steps=steps,
                              waits=[w for _, w in vslots] + [("sv", xnT_ok)],
                              evac=cp_evac(Vtm[:, tb * 512:(tb + 1) * 512])))
        emit_jobs(vjobs)
        for s, _ in vslots:
            fr.release[s] = vjobs[-1]["done"]

        for kvh in range(KVHN):
            qslot = {}
            qck = {}
            for sh in range(4):
                s = fr.take()
                qslot[sh] = s
                qck[sh] = load_chunk(rreg(s), dr["wq_p"][kvh * 4 + sh], fr.release[s], s)

            def q_evac(dst_ap):
                def cb(jb):
                    wt("vector", "sp", jb["done"])
                    c = vop(lambda EE, d=dst_ap, b=jb["bank"]: EE.tensor_scalar(
                        d, b, SCALE, None, ALU.mult))
                    return ("sv", c)
                return cb

            qjobs = []
            for sh in range(4):
                for tt in range(2):
                    steps = [(rreg(qslot[sh])[:, dc * 128:(dc + 1) * 128],
                              xnT[:, dc * STT + HALO + tt * 512:dc * STT + HALO + (tt + 1) * 512])
                             for dc in range(NDC)]
                    hh = kvh * 4 + sh
                    steps.append((bqr[0:1, hh * 128:(hh + 1) * 128], onesb[0:1, 0:512]))
                    bidx = (sh * 2 + tt) % 4
                    qjobs.append(dict(bankkey=PFN[bidx], bank=pf[bidx][:, :], steps=steps,
                                      waits=[qck[sh]],
                                      evac=q_evac(QTb[:, sh * ST + tt * 512:sh * ST + (tt + 1) * 512])))
            emit_jobs(qjobs)
            for sh in range(4):
                fr.release[qslot[sh]] = qjobs[sh * 2 + 1]["done"]
            qt_ok = C["sv"]

            at_done = {}
            pending = None

            def emit_atav(p):
                it, dbl, qb, sh = p["it"], p["dbl"], p["qb"], p["sh"]
                wt("tensor", "sv", p["ca2"])
                at_ok = None
                for t3 in range(3):
                    key = PFN[2 + t3]
                    reg = pf[2 + t3][:, 0:128]
                    bank_wait_and_claim(key)
                    cc = mmtp(reg, A2b[:, dbl * 384 + t3 * 128:dbl * 384 + (t3 + 1) * 128],
                              idbf[:, :])
                    wt("vector", "sp", cc)
                    dst = ATb[:, dbl * 384 + t3 * 128:dbl * 384 + (t3 + 1) * 128]
                    ce = vop(lambda EE, d=dst, r=reg: EE.tensor_copy(d, r))
                    bank_release(key, "sv", ce)
                    at_ok = ce
                okey = PFN[dbl]
                pot = pf[dbl][:, 0:128]
                bank_wait_and_claim(okey)
                wt("tensor", "sv", at_ok)
                cot = None
                for t3 in range(3):
                    tbv = qb + t3
                    cot = mm(pot, Vtm[:, tbv * 512 + kvh * 128:tbv * 512 + (kvh + 1) * 128],
                             ATb[:, dbl * 384 + t3 * 128:dbl * 384 + (t3 + 1) * 128],
                             t3 == 0, t3 == 2, inc="sp" if t3 == 2 else None)
                at_done[it] = cot
                wt("vector", "sp", cot)
                dst = OTk[:, sh * ST + qb * 128:sh * ST + (qb + 1) * 128]
                ceo = vop(lambda EE, d=dst, p=pot: EE.tensor_copy(d, p))
                bank_release(okey, "sv", ceo)

            for qb in range(8):
                mi = 0 if (st == 0 and qb == 0) else (1 if (st == 0 and qb == 1) else 2)
                for sh in range(4):
                    it = qb * 4 + sh
                    dbl = it % 2
                    tri = it % 3
                    pkey = PFN[5 + tri]
                    psc = pf[5 + tri]
                    bank_wait_and_claim(pkey)
                    wt("tensor", "sv", qt_ok)
                    mm(psc[:, 0:384], QTb[:, sh * ST + qb * 128:sh * ST + (qb + 1) * 128],
                       KTb[:, kvh * STT + qb * 128:kvh * STT + qb * 128 + 384], True, False)
                    csc = mm(psc[:, 0:384], idbf[:, :], masksb[:, mi * 384:(mi + 1) * 384],
                             False, True, inc="sp")
                    wt("vector", "sp", csc)
                    cmx = vop(lambda EE, p=psc, d=tri: EE.reduce_max(
                        smx[:, d:d + 1], p[:, 0:384], axis=AX.X, negate=True))
                    wt("scalar", "sv", cmx)
                    cex = sop(lambda EE, p=psc, d=tri, d2=dbl: EE.activation(
                        Asb[:, d2 * 384:(d2 + 1) * 384], p[:, 0:384], AF.Exp,
                        bias=smx[:, d:d + 1], accum_out=srs[:, d:d + 1]))
                    bank_release(pkey, "ss", cex)
                    wt("vector", "ss", cex)
                    crr = vop(lambda EE, d=tri: EE.reciprocal(srr[:, d:d + 1], srs[:, d:d + 1]))
                    wt("vector", "sv", crr)
                    if it >= 2:
                        wt("vector", "sp", at_done.get(it - 2, 0))
                    ca2 = vop(lambda EE, d=tri, d2=dbl: EE.tensor_scalar(
                        A2b[:, d2 * 384:(d2 + 1) * 384], Asb[:, d2 * 384:(d2 + 1) * 384],
                        srr[:, d:d + 1], None, ALU.mult))
                    me = dict(it=it, dbl=dbl, qb=qb, sh=sh, ca2=ca2)
                    if pending is not None:
                        emit_atav(pending)
                    pending = me
            if pending is not None:
                emit_atav(pending)
                pending = None
            ot_ok = C["sv"]

            for dout in range(16):
                s = fr.take()
                src = dr["wo_p"][dout, kvh]
                cw = load_chunk(rreg(s, 512), src, fr.release[s], s)
                jobs = []
                for tt in range(2):
                    steps = [(rreg(s, 512)[:, sh * 128:(sh + 1) * 128],
                              OTk[:, sh * ST + tt * 512:sh * ST + (tt + 1) * 512])
                             for sh in range(4)]
                    if kvh == 0:
                        steps.append((bor[0:1, dout * 128:(dout + 1) * 128], onesb[0:1, 0:512]))
                    aslice = acc[:, dout * ST + tt * 512:dout * ST + (tt + 1) * 512]
                    if kvh == 0:
                        xslice = xnT[:, dout * STT + HALO + tt * 512:dout * STT + HALO + (tt + 1) * 512]
                        def acc_evac(jb, a=aslice, xx=xslice):
                            wt("vector", "sp", jb["done"])
                            c = vop(lambda EE, aa=a, b=jb["bank"], x2=xx:
                                    EE.tensor_tensor(aa, b, x2, ALU.add))
                            return ("sv", c)
                    else:
                        def acc_evac(jb, a=aslice):
                            wt("vector", "sp", jb["done"])
                            c = vop(lambda EE, aa=a, b=jb["bank"]:
                                    EE.tensor_tensor(aa, b, aa, ALU.add))
                            return ("sv", c)
                    bidx = (dout * 2 + tt) % 4
                    jobs.append(dict(bankkey=PFN[bidx], bank=pf[bidx][:, :], steps=steps,
                                     waits=[cw, ("sv", ot_ok)], evac=acc_evac))
                emit_jobs(jobs)
                fr.release[s] = jobs[1]["done"]

        comb_ok = C["sv"]
        def wsrc(kind, e, j):
            if kind == "g":
                t = dr["rg_p"][e, j] if e < E else dr["sg_p"][e - E, j]
            elif kind == "w1":
                t = dr["rw1_p"][e, j] if e < E else dr["sw1_p"][e - E, j]
            else:
                t = dr["rw2_p"][e, j] if e < E else dr["sw2_p"][e - E, j]
            return t

        prev_stt = {0: 0, 1: 0}
        for pe_i in range(_N_PAIRS):
            pair_es = [2 * pe_i, 2 * pe_i + 1]
            ybufs = {pair_es[0]: y0, pair_es[1]: y1}
            for e in pair_es:
                cb_ok = 0
                if e < E:
                    for tt in range(2):
                        key = PFN[6 + tt]
                        bank_wait_and_claim(key)
                        wt("tensor", "sv", comb_ok)
                        cm = mm(pf[6 + tt][:, :], emsb[:, e * 128:(e + 1) * 128],
                                combT[:, tt * 512:(tt + 1) * 512], True, True, inc="sp")
                        wt("vector", "sp", cm)
                        dst = cbb[:, tt * 512:(tt + 1) * 512]
                        ce = vop(lambda EE, d=dst, p=pf[6 + tt]: EE.tensor_copy(d, p[:, :]))
                        bank_release(key, "sv", ce)
                        cb_ok = ce
                if e < E:
                    bias_gb = rgbp[:, e * NFC:(e + 1) * NFC]
                    bias_b1 = rb1p[:, e * NFC:(e + 1) * NFC]
                else:
                    bias_gb = sgbp[:, (e - E) * NFC:(e - E + 1) * NFC]
                    bias_b1 = sb1p[:, (e - E) * NFC:(e - E + 1) * NFC]
                ybuf = ybufs[e]
                for fc in range(NFC):
                    sg_ = fr.take()
                    s1_ = fr.take()
                    cg = load_chunk(rreg(sg_), wsrc("g", e, fc), fr.release[sg_], sg_)
                    cw = load_chunk(rreg(s1_), wsrc("w1", e, fc), fr.release[s1_], s1_)
                    bsel = (fc % 2) * 4
                    banks = [bsel, bsel + 1, bsel + 2, bsel + 3]
                    jobs = []
                    for part, (slot, cload) in enumerate([(sg_, cg), (s1_, cw)]):
                        for tt in range(2):
                            steps = [(rreg(slot)[:, dc * 128:(dc + 1) * 128],
                                      xnT[:, dc * STT + HALO + tt * 512:
                                          dc * STT + HALO + (tt + 1) * 512])
                                     for dc in range(NDC)]
                            bidx = banks[part * 2 + tt]
                            gb_ap = bias_gb[:, fc:fc + 1]
                            b1_ap = bias_b1[:, fc:fc + 1]
                            yg_ap = ygc[:, tt * 512:(tt + 1) * 512]
                            ydst = ybuf[:, fc * ST + tt * 512:fc * ST + (tt + 1) * 512]
                            if part == 0:
                                def gelu_evac(jb, o=yg_ap, bb=gb_ap, pv=prev_stt, t=tt):
                                    wt("scalar", "sp", jb["done"])
                                    wt("scalar", "sv", pv[t])
                                    c = sop(lambda EE, oo=o, b=jb["bank"], b2=bb:
                                            EE.activation(oo, b, AF.Gelu, bias=b2[:, 0:1]))
                                    return ("ss", c)
                                jobs.append(dict(bankkey=PFN[bidx], bank=pf[bidx][:, :],
                                                 steps=steps, evac=gelu_evac,
                                                 waits=[cload, ("sv", xnT_ok)]))
                            else:
                                cbs_ap = cbb[:, tt * 512:(tt + 1) * 512]
                                def stt_evac(jb, o=ydst, bb=b1_ap, yg=yg_ap, ee=e, cbk=cb_ok,
                                             cba=cbs_ap, pv=prev_stt, t=tt):
                                    wt("vector", "sp", jb["done"])
                                    wt("vector", "ss", jb["gelu_ss"])
                                    c = vop(lambda EE, oo=o, b=jb["bank"], b2=bb, yy=yg:
                                            EE.scalar_tensor_tensor(
                                                oo, b, b2[:, 0:1], yy, ALU.add, ALU.mult))
                                    if ee < E:
                                        wt("vector", "sv", max(c, cbk))
                                        c = vop(lambda EE, oo=o, cc2=cba: EE.tensor_tensor(
                                            oo, oo, cc2, ALU.mult))
                                    pv[t] = c
                                    return ("sv", c)
                                jobs.append(dict(bankkey=PFN[bidx], bank=pf[bidx][:, :],
                                                 steps=steps, evac=stt_evac, tt=tt,
                                                 waits=[cload]))
                    emit_jobs(jobs[0:2])
                    jobs[2]["gelu_ss"] = C["ss"]
                    jobs[3]["gelu_ss"] = C["ss"]
                    emit_jobs(jobs[2:4])
                    fr.release[sg_] = jobs[1]["done"]
                    fr.release[s1_] = jobs[3]["done"]
            y_ok = C["sv"]

            for dout in range(16):
                slots = {}
                cls = {}
                for e in pair_es:
                    s = fr.take()
                    slots[e] = s
                    cls[e] = load_chunk(rreg(s, 1024), wsrc("w2", e, dout), fr.release[s], s)
                jobs = []
                for tt in range(2):
                    steps = []
                    for e in pair_es:
                        for fc in range(NFC):
                            steps.append((rreg(slots[e], 1024)[:, fc * 128:(fc + 1) * 128],
                                          ybufs[e][:, fc * ST + tt * 512:fc * ST + (tt + 1) * 512]))
                    if pe_i == 0:
                        steps.append((rb2sb[:, dout * 128:(dout + 1) * 128],
                                      combT[:, tt * 512:(tt + 1) * 512]))
                        steps.append((sb2sb[:, dout * 128:(dout + 1) * 128],
                                      ones2b[:, tt * 512:(tt + 1) * 512]))
                    aslice = acc[:, dout * ST + tt * 512:dout * ST + (tt + 1) * 512]
                    def add_evac(jb, a=aslice):
                        wt("vector", "sp", jb["done"])
                        c = vop(lambda EE, aa=a, b=jb["bank"]: EE.tensor_tensor(aa, b, aa, ALU.add))
                        return ("sv", c)
                    bidx = (dout * 2 + tt) % 4
                    jobs.append(dict(bankkey=PFN[bidx], bank=pf[bidx][:, :], steps=steps,
                                     waits=[cls[pair_es[0]], cls[pair_es[1]], ("sv", y_ok)],
                                     evac=add_evac))
                emit_jobs(jobs)
                for e in pair_es:
                    fr.release[slots[e]] = jobs[1]["done"]

        acc_ok = C["sv"]
        ost_cnt = {}
        for ti in range(0 if _NO_P4 else 8):
            buf = ti % 2
            xs = xstage[:, buf * D:(buf + 1) * D]
            osb = ostage[:, buf * D:(buf + 1) * D]
            wt("sync", "sv", C["sv"])
            wt("sync", "ss", C["ss"])
            wt("sync", "sp", C["sp"])
            sxn = "sxa" if buf == 0 else "sxb"
            cx = dma("sync", xs, dr["x_sl"][row0 + HALO + ti * 128:row0 + HALO + ti * 128 + 128, :],
                     sxn)
            wt("vector", sxn, cx)
            if ti >= 2:
                wt("vector", "soa" if buf == 0 else "sob", ost_cnt[ti - 2])
            wt("tensor", "sv", acc_ok)
            for dc in range(NDC):
                key = PFN[dc % 6]
                reg = pf[dc % 6][:, 0:128]
                bank_wait_and_claim(key)
                cc = mmtp(reg, acc[:, dc * ST + ti * 128:dc * ST + (ti + 1) * 128], idbf[:, :])
                wt("vector", "sp", cc)
                ce = vop(lambda EE, o=osb, d=dc, r=reg, i=xs: EE.tensor_tensor(
                    o[:, d * 128:(d + 1) * 128], r, i[:, d * 128:(d + 1) * 128], ALU.add))
                bank_release(key, "sv", ce)
            wt("sync", "sv", C["sv"])
            ost_cnt[ti] = dma("sync", out_h[st * 1024 + ti * 128:st * 1024 + (ti + 1) * 128, :],
                              osb, "soa" if buf == 0 else "sob")

    wt("sync", "soa", C["soa"])
    wt("sync", "sob", C["sob"])

    with nc.Block() as block:
        @block.sync
        def _(eng):
            for fn in plan["sync"]:
                fn(eng)

        @block.gpsimd
        def _(eng):
            for fn in plan["gpsimd"]:
                fn(eng)

        @block.tensor
        def _(eng):
            for fn in plan["tensor"]:
                fn(eng)

        @block.vector
        def _(eng):
            for fn in plan["vector"]:
                fn(eng)

        @block.scalar
        def _(eng):
            for fn in plan["scalar"]:
                fn(eng)

    for s in reversed(sem_ctx):
        s.__exit__(None, None, None)
    for p in reversed(psum_ctx):
        p.__exit__(None, None, None)
    return nc


def prep_inputs(L0_attn_x, L0_attn_Wq, L0_attn_bq, L0_attn_Wk, L0_attn_bk,
                L0_attn_Wv, L0_attn_bv, L0_attn_Wo, L0_attn_bo, L0_ffn_wa,
                L0_ffn_rg, L0_ffn_rgb, L0_ffn_rw1, L0_ffn_rb1, L0_ffn_rw2,
                L0_ffn_rb2, L0_ffn_sg, L0_ffn_sgb, L0_ffn_sw1, L0_ffn_sb1,
                L0_ffn_sw2, L0_ffn_sb2):
    f32 = np.float32
    asnp = lambda a: np.asarray(a, dtype=f32)
    x = asnp(L0_attn_x)
    sh = {}
    sh["wq_p"] = np.ascontiguousarray(
        asnp(L0_attn_Wq).reshape(16, 128, H, 128).transpose(2, 1, 0, 3).reshape(H, 128, D))
    sh["wk_p"] = np.ascontiguousarray(
        asnp(L0_attn_Wk).reshape(16, 128, KVH, 128).transpose(2, 1, 0, 3).reshape(KVH, 128, D))
    sh["wv_p"] = asnp(L0_attn_Wv).reshape(16, 128, 512)
    sh["wo_p"] = np.ascontiguousarray(
        asnp(L0_attn_Wo).reshape(KVH, 4, 128, 16, 128).transpose(3, 0, 2, 1, 4)
        .reshape(16, KVH, 128, 512))
    sh["bq_r"] = asnp(L0_attn_bq).reshape(1, D)
    sh["bk_r"] = asnp(L0_attn_bk).reshape(1, 512)
    sh["bv_r"] = asnp(L0_attn_bv).reshape(1, 512)
    sh["bo_r"] = asnp(L0_attn_bo).reshape(1, D)
    sh["wa_n"] = asnp(L0_ffn_wa)
    def dfp(w, ne):
        return np.ascontiguousarray(
            w.reshape(ne, 16, 128, NFC, 128).transpose(0, 3, 2, 1, 4).reshape(ne, NFC, 128, D))
    def fdp(w, ne):
        return np.ascontiguousarray(
            w.reshape(ne, NFC, 128, 16, 128).transpose(0, 3, 2, 1, 4).reshape(ne, 16, 128, F))
    sh["rg_p"] = dfp(asnp(L0_ffn_rg), E)
    sh["rw1_p"] = dfp(asnp(L0_ffn_rw1), E)
    sh["rw2_p"] = fdp(asnp(L0_ffn_rw2), E)
    sh["sg_p"] = dfp(asnp(L0_ffn_sg), SH)
    sh["sw1_p"] = fdp(asnp(L0_ffn_sw2), SH) if False else dfp(asnp(L0_ffn_sw1), SH)
    sh["sw2_p"] = fdp(asnp(L0_ffn_sw2), SH)
    sh["rgb_p"] = np.ascontiguousarray(
        asnp(L0_ffn_rgb).reshape(E, NFC, 128).transpose(2, 0, 1).reshape(128, E * NFC))
    sh["rb1_p"] = np.ascontiguousarray(
        asnp(L0_ffn_rb1).reshape(E, NFC, 128).transpose(2, 0, 1).reshape(128, E * NFC))
    sh["sgb_p"] = np.ascontiguousarray(
        asnp(L0_ffn_sgb).reshape(SH, NFC, 128).transpose(2, 0, 1).reshape(128, SH * NFC))
    sh["sb1_p"] = np.ascontiguousarray(
        asnp(L0_ffn_sb1).reshape(SH, NFC, 128).transpose(2, 0, 1).reshape(128, SH * NFC))
    sh["rb2_n"] = asnp(L0_ffn_rb2)
    sh["sb2_n"] = asnp(L0_ffn_sb2)
    sh["id128"] = np.eye(128, dtype=f32)
    em = np.zeros((E, E * 128), dtype=ml_dtypes.bfloat16)
    for e in range(E):
        em[e, e * 128:(e + 1) * 128] = 1
    sh["emat"] = em
    sh["onesr"] = np.ones((1, 512), dtype=ml_dtypes.bfloat16)
    sh["ones2"] = np.ones((2, 1024), dtype=ml_dtypes.bfloat16)

    qi = np.arange(128)[:, None]
    tj = np.arange(384)[None, :]
    band = (tj >= qi + 1) & (tj <= qi + 256)
    mk = lambda m: np.where(m, 0.0, NEG).astype(f32)
    m_band = mk(band)
    m_e0 = mk(band & (tj >= 256))
    m_e1 = mk(band & (tj >= 128))

    in_maps = []
    for c in range(8):
        b, h = c // 2, c % 2
        sl = np.zeros((2304, D), dtype=f32)
        lo = h * 2048 - 256
        src_lo = max(lo, 0)
        sl[src_lo - lo:, :] = x[b, src_lo:h * 2048 + 2048, :]
        m = dict(sh)
        m["x_sl"] = sl
        m["masks"] = (np.stack([m_e0, m_e1, m_band]) if h == 0
                      else np.stack([m_band, m_band, m_band]))
        in_maps.append(m)
    return in_maps


def kernel(**inputs):
    in_maps = prep_inputs(**inputs)
    nc = build_program()
    res = run_bass_kernel_spmd(nc, in_maps, core_ids=list(range(8)))
    out = np.empty((4, 4096, D), dtype=np.float32)
    for c in range(8):
        b, h = c // 2, c % 2
        out[b, h * 2048:(h + 1) * 2048, :] = res.results[c]["out"]
    return out


# version 07
# speedup vs baseline: 1.5276x; 1.1990x over previous
import os
import numpy as np
import ml_dtypes
import concourse.bass as bass
import concourse.mybir as mybir
from concourse.bass_utils import run_bass_kernel_spmd

F32 = mybir.dt.float32
BF16 = mybir.dt.bfloat16
AF = mybir.ActivationFunctionType
ALU = mybir.AluOpType
AX = mybir.AxisListType

D = 2048
NDC = 16
F = 1024
NFC = 8
E = 16
SH = 2
H, KVH, HD = 16, 4, 128
ST = 1024
HALO = 256
STT = ST + HALO
NTI = STT // 128
SCALE = float(1.0 / np.sqrt(128.0))
NEG = -1e9
CAP = 256
BIGF = 1.0e5
_BIS = os.environ.get("KBISECT", "full")
_DO_P1 = _BIS not in ("p0", "ffnonly")
_N_PAIRS = 0 if _BIS in ("p0", "p1") else (1 if _BIS == "ffn1" else 9)
_N_KVH = KVH if _DO_P1 else 0
_NO_P4 = os.environ.get("KNOP4", "") == "1"
_NO_RT = os.environ.get("KNORT", "") == "1"
_N_TI = int(os.environ.get("KNTI", str(NTI)))
_NSTEP = int(os.environ.get("KNSTEP", "99"))


def build_program():
    nc = bass.Bass("TRN2", target_bir_lowering=False)

    dr = {}
    def di(name, shape, dtype=F32):
        dr[name] = nc.dram_tensor(name, list(shape), dtype, kind="ExternalInput")

    di("x_sl", (2304, D))
    di("wq_p", (H, 128, D))
    di("wk_p", (KVH, 128, D))
    di("wv_p", (16, 128, 512))
    di("wo_p", (16, KVH, 128, 512))
    di("bq_r", (1, D)); di("bk_r", (1, 512)); di("bv_r", (1, 512)); di("bo_r", (1, D))
    di("wa_n", (D, E))
    di("rg_p", (E, NFC, 128, D)); di("rw1_p", (E, NFC, 128, D))
    di("rw2_p", (E, 16, 128, F))
    di("sg_p", (SH, NFC, 128, D)); di("sw1_p", (SH, NFC, 128, D))
    di("sw2_p", (SH, 16, 128, F))
    di("rgb_p", (128, E * NFC)); di("rb1_p", (128, E * NFC))
    di("sgb_p", (128, SH * NFC)); di("sb1_p", (128, SH * NFC))
    di("rb2_n", (E, D)); di("sb2_n", (SH, D))
    di("masks", (3, 128, 384))
    di("id128", (128, 128))
    di("emat", (E, E * 128), BF16)
    di("ecap_c", (16, 1))
    di("iotatok", (128, 8), mybir.dt.int32)
    di("onesr", (1, 512), BF16)
    di("ones2", (2, 1024), BF16)
    out_h = nc.dram_tensor("out", [2048, D], F32, kind="ExternalOutput")
    xncb_d = nc.dram_tensor("xncb_d", [1024, 2064], BF16)
    xg_d = nc.dram_tensor("xg_d", [16 * CAP, 2064], BF16)
    idx_d = nc.dram_tensor("idx_d", [16 * CAP, 1], mybir.dt.int32)
    racc_d = nc.dram_tensor("racc_d", [1024, D], BF16)

    def sa(name, shape, dtype):
        return nc.alloc_sbuf_tensor(name, list(shape), dtype)

    xnT = sa("xnT", (128, NDC * STT), BF16)
    acc = sa("acc", (128, NDC * ST), BF16)
    ring = sa("ring", (128, 8 * 2048), BF16)
    arena = sa("arena", (128, 41472), mybir.dt.uint8)
    abase = nc.lookup_mloc(arena).addr
    def aa(name, shape, dtype, off):
        return nc.alloc_sbuf_tensor_at(name, list(shape), dtype, offset=abase + off)
    xstage = aa("xstage", (128, 2 * D), F32, 0)
    sqstage = aa("sqstage", (128, D), F32, 16384)
    xnstage = aa("xnstage", (128, 2 * D), F32, 24576)
    KTb = aa("KTb", (128, KVH * STT), BF16, 0)
    Vtm = aa("Vtm", (128, NTI * 512), BF16, 10240)
    QTb = aa("QTb", (128, 4 * ST), BF16, 20480)
    OTk = aa("OTk", (128, 4 * ST), BF16, 28672)
    Asb = aa("Asb", (128, 2 * 384), BF16, 36864)
    A2b = aa("A2b", (128, 2 * 384), BF16, 38400)
    ATb = aa("ATb", (128, 2 * 384), BF16, 39936)
    y0 = aa("y0", (128, NFC * ST), BF16, 0)
    y1 = aa("y1", (128, NFC * ST), BF16, 16384)
    ygc = aa("ygc", (128, 1024), BF16, 32768)
    xgsb = aa("xgsb", (128, 2 * 2064), BF16, 0)
    xgT = aa("xgT", (128, 16 * CAP), BF16, 8256)
    ysp = aa("ysp", (128, NFC * CAP), BF16, 16448)
    ztm = aa("ztm", (128, 2 * D), BF16, 20544)
    zsb = aa("zsb", (128, 2 * CAP), BF16, 28736)
    ostage = aa("ostage", (128, 2 * D), F32, 16384)
    rstage = aa("rstage", (128, 2 * D), BF16, 32768)
    masksb = sa("masksb", (128, 3 * 384), BF16)
    combT = sa("combT", (16, ST), BF16)
    cbb = sa("cbb", (128, ST), BF16)
    wasb = sa("wasb", (128, NDC * E), BF16)
    emsb = sa("emsb", (16, E * 128), BF16)
    idbf = sa("idbf", (128, 128), BF16)
    idf32 = sa("idf32", (128, 128), F32)
    onesb = sa("onesb", (1, 512), BF16)
    ones2b = sa("ones2b", (2, 1024), BF16)
    bqr = sa("bqr", (1, D), BF16)
    bkr = sa("bkr", (1, 512), BF16)
    bvr = sa("bvr", (1, 512), BF16)
    bor = sa("bor", (1, D), BF16)
    rb2sb = sa("rb2sb", (16, D), BF16)
    sb2sb = sa("sb2sb", (2, D), BF16)
    rgbp = sa("rgbp", (128, E * NFC), F32)
    rb1p = sa("rb1p", (128, E * NFC), F32)
    sgbp = sa("sgbp", (128, SH * NFC), F32)
    sb1p = sa("sb1p", (128, SH * NFC), F32)
    lnsum = sa("lnsum", (128, 1), F32)
    lnm = sa("lnm", (128, 2), F32)
    lnsq = sa("lnsq", (128, 2), F32)
    lnva = sa("lnva", (128, 1), F32)
    lnrv = sa("lnrv", (128, 2), F32)
    lnrstd = sa("lnrstd", (128, 2), F32)
    sig = sa("sig", (128, 16), F32)
    rm1 = sa("rm1", (128, 1), F32)
    raff2 = sa("raff2", (128, 16), F32)
    rm2 = sa("rm2", (128, 1), F32)
    rge2 = sa("rge2", (128, 16), F32)
    rss = sa("rss", (128, 1), F32)
    rrs = sa("rrs", (128, 1), F32)
    rcb = sa("rcb", (128, 16), F32)
    smx = sa("smx", (128, 3), F32)
    srs = sa("srs", (128, 3), F32)
    srr = sa("srr", (128, 3), F32)
    xncb = sa("xncb", (128, 2 * 2064), BF16)
    cs0 = sa("cs0", (16, 1024), F32)
    cs1 = sa("cs1", (16, 1024), F32)
    qb0 = sa("qb0", (16, 1024), F32)
    qtm = sa("qtm", (128, 8 * 16), F32)
    p12f = sa("p12f", (128, 2), F32)
    p12i = sa("p12i", (128, 16), mybir.dt.int32)
    mm16 = sa("mm16", (128, 16), F32)
    zerosb = sa("zerosb", (128, 2064), BF16)
    bigisb = sa("bigisb", (128, 32), mybir.dt.int32)
    iotsb = sa("iotsb", (128, 8), mybir.dt.int32)
    idxesb = sa("idxesb", (128, 2 * 2), mybir.dt.int32)
    cbvsb = sa("cbvsb", (128, 2), F32)
    ecapsb = sa("ecapsb", (16, 1), F32)

    psum_ctx = [nc.psum_tensor("pf%d" % i, [128, 512], F32) for i in range(8)]
    pf = [p.__enter__() for p in psum_ctx]
    PFN = ["pf%d" % i for i in range(8)]

    sem_names = ["sp", "sv", "ss", "swp", "sxa", "sxb", "soa", "sob",
                 "sxc", "ssc", "szs", "sla", "slb", "ssd", "sse", "sxg", "sie", "sif",
                 "sra", "srb"] + \
                ["sr%d" % i for i in range(8)]
    sem_ctx = [nc.semaphore(n) for n in sem_names]
    sems = dict(zip(sem_names, [s.__enter__() for s in sem_ctx]))
    C = {k: 0 for k in sems}
    plan = {"sync": [], "gpsimd": [], "tensor": [], "vector": [], "scalar": []}
    FREE = {}

    def em(eng, fn):
        plan[eng].append(fn)

    def wt(eng, sem, val):
        if val and val > 0:
            em(eng, lambda EE, s=sems[sem], v=val: EE.wait_ge(s, v))

    def bump(inc, n=1):
        C[inc] += n
        return C[inc]

    def mm(bank_ap, lhsT_ap, rhs_ap, start, stop, inc=None):
        cnt = bump(inc) if inc else None
        def fn(EE, o=bank_ap, l=lhsT_ap, r=rhs_ap, a=start, b=stop, s=(sems[inc] if inc else None)):
            ins = EE.matmul(o, l, r, start=a, stop=b)
            if s is not None:
                ins.then_inc(s, 1)
        em("tensor", fn)
        return cnt

    def mmtp(out_ap, in_ap, ident_ap):
        return mm(out_ap, in_ap, ident_ap, True, True, inc="sp")

    def vop(builder, inc="sv"):
        cnt = bump(inc)
        em("vector", lambda EE, b=builder, s=sems[inc]: b(EE).then_inc(s, 1))
        return cnt

    def sop(builder, inc="ss"):
        cnt = bump(inc)
        em("scalar", lambda EE, b=builder, s=sems[inc]: b(EE).then_inc(s, 1))
        return cnt

    def dma(eng, dst_ap, src_ap, inc):
        cnt = bump(inc, 16)
        em(eng, lambda EE, o=dst_ap, i=src_ap, s=sems[inc]: EE.dma_start(o, i).then_inc(s, 16))
        return cnt

    def bank_wait_and_claim(key):
        ent = FREE.pop(key, None)
        if ent:
            wt("tensor", ent[0], ent[1])

    def bank_release(key, sem, cnt):
        FREE[key] = (sem, cnt)

    def emit_jobs(jobs):
        for j0 in range(0, len(jobs), 2):
            pair = jobs[j0:j0 + 2]
            for jb in pair:
                for s, v in jb.get("waits", ()):
                    wt("tensor", s, v)
                bank_wait_and_claim(jb["bankkey"])
            nk = max(len(jb["steps"]) for jb in pair)
            for k in range(nk):
                for jb in pair:
                    if k >= len(jb["steps"]):
                        continue
                    l, r = jb["steps"][k]
                    last = k == len(jb["steps"]) - 1
                    c = mm(jb["bank"], l, r, k == 0, last, inc="sp" if last else None)
                    if last:
                        jb["done"] = c
            for jb in pair:
                bank_release(jb["bankkey"], *jb["evac"](jb))

    class Ring:
        def __init__(self, init_release):
            self.i = 0
            self.release = [init_release] * 8

        def take(self):
            s = self.i % 8
            self.i += 1
            return s

    def load_chunk(rg_ap, src_ap, slot_release, slot):
        wt("gpsimd", "sp", slot_release)
        c = dma("gpsimd", rg_ap, src_ap, "sr%d" % slot)
        return ("sr%d" % slot, c)

    def rreg(s, cols=2048):
        return ring[:, s * 2048:s * 2048 + cols]

    rb_cap = nc.gpsimd.alloc_register("rb_cap")
    rb_tok = nc.gpsimd.alloc_register("rb_tok")
    em("gpsimd", lambda EE: EE.reg_mov(rb_cap, 16 * CAP - 1))
    em("gpsimd", lambda EE: EE.reg_mov(rb_tok, 1023))

    pre = [
        (masksb.rearrange("p (m t) -> p m t", t=384), dr["masks"].rearrange("m p t -> p m t")),
        (idbf[:, :], dr["id128"][:, :]),
        (idf32[:, :], dr["id128"][:, :]),
        (emsb[:, :], dr["emat"][:, :]),
        (onesb[:, :], dr["onesr"][:, :]),
        (ones2b[:, :], dr["ones2"][:, :]),
        (bqr[:, :], dr["bq_r"][:, :]),
        (bkr[:, :], dr["bk_r"][:, :]),
        (bvr[:, :], dr["bv_r"][:, :]),
        (bor[:, :], dr["bo_r"][:, :]),
        (rb2sb[:, :], dr["rb2_n"][:, :]),
        (sb2sb[:, :], dr["sb2_n"][:, :]),
        (rgbp[:, :], dr["rgb_p"][:, :]),
        (rb1p[:, :], dr["rb1_p"][:, :]),
        (sgbp[:, :], dr["sgb_p"][:, :]),
        (sb1p[:, :], dr["sb1_p"][:, :]),
    ]
    for o, i in pre:
        dma("gpsimd", o, i, "swp")
    for dc in range(NDC):
        dma("gpsimd", wasb[:, dc * E:(dc + 1) * E], dr["wa_n"][dc * 128:(dc + 1) * 128, :], "swp")
    dma("gpsimd", iotsb[:, :], dr["iotatok"][:, :], "swp")
    dma("gpsimd", ecapsb[:, :], dr["ecap_c"][:, :], "swp")
    SW_PRE = C["swp"]
    for eng in ("tensor", "vector", "scalar"):
        wt(eng, "swp", SW_PRE)
    cz1 = vop(lambda EE: EE.memset(zerosb[:, :], 0.0))
    cz2 = vop(lambda EE: EE.memset(bigisb[:, :], 100000))
    CZ_OK = cz2
    wt("sync", "sv", CZ_OK)
    for zi in range(32):
        dma("sync", xg_d[zi * 128:(zi + 1) * 128, :], zerosb[:, :], "sxc")

    for st in range(2):
        row0 = st * 1024

        if st > 0:
            for eng in ("vector", "scalar", "sync"):
                wt(eng, "sp", C["sp"])
                wt(eng, "soa", C["soa"])
                wt(eng, "sob", C["sob"])
            wt("vector", "ss", C["ss"])
            wt("gpsimd", "sp", C["sp"])

        evtile = {}
        xncb_st = {}

        def p0_stats(ti):
            buf = ti % 2
            xs = xstage[:, buf * D:(buf + 1) * D]
            wt("sync", "sv", C["sv"])
            wt("sync", "ss", C["ss"])
            sxn = "sxa" if buf == 0 else "sxb"
            cx = dma("sync", xs, dr["x_sl"][row0 + ti * 128:row0 + ti * 128 + 128, :], sxn)
            lm = lnm[:, buf:buf + 1]
            lrv = lnrv[:, buf:buf + 1]
            sqa = lnsq[:, buf:buf + 1]
            wt("vector", sxn, cx)
            c1 = vop(lambda EE, i=xs: EE.reduce_sum(lnsum[:, :], i, axis=AX.X))
            wt("scalar", sxn, cx)
            c2 = sop(lambda EE, i=xs, a=sqa: EE.activation(sqstage[:, :], i, AF.Square, accum_out=a))
            wt("vector", "sv", c1)
            c3 = vop(lambda EE, o=lm: EE.tensor_scalar(o, lnsum[:, :], 1.0 / D, None, ALU.mult))
            wt("vector", "sv", c3)
            c4 = vop(lambda EE, m=lm: EE.scalar_tensor_tensor(
                lnva[:, :], m, -1.0, m, ALU.mult, ALU.mult))
            wt("vector", "ss", c2)
            wt("vector", "sv", c4)
            c5 = vop(lambda EE, a=sqa: EE.tensor_scalar(
                lnva[:, :], a, 1.0 / D, lnva[:, 0:1], ALU.mult, ALU.add))
            wt("vector", "sv", c5)
            c6 = vop(lambda EE, o=lrv: EE.tensor_scalar(o, lnva[:, :], 1e-5, None, ALU.add))
            wt("vector", "sv", c6)
            c7 = vop(lambda EE, o=lrv: EE.reciprocal(o, o))
            wt("scalar", "sv", c7)
            c8 = sop(lambda EE, i=lrv, o=lnrstd[:, buf:buf + 1]: EE.activation(o, i, AF.Sqrt))
            return c8

        def p0_xn(ti, c8):
            buf = ti % 2
            xs = xstage[:, buf * D:(buf + 1) * D]
            xn = xnstage[:, buf * D:(buf + 1) * D]
            lm = lnm[:, buf:buf + 1]
            wt("vector", "ss", c8)
            c9 = vop(lambda EE, i=xs, o=xn, m=lm, r=lnrstd[:, buf:buf + 1]: EE.tensor_scalar(
                o, i, m, r, ALU.subtract, ALU.mult))
            wt("tensor", "sv", c9)
            for dc in range(NDC):
                key = PFN[dc % 6]
                reg = pf[dc % 6][:, 0:128]
                bank_wait_and_claim(key)
                cc = mmtp(reg, xn[:, dc * 128:(dc + 1) * 128], idf32[:, :])
                dst = xnT[:, dc * STT + ti * 128:dc * STT + (ti + 1) * 128]
                wt("vector", "sp", cc)
                ce = vop(lambda EE, d=dst, r=reg: EE.tensor_copy(d, r))
                bank_release(key, "sv", ce)
            evtile[ti] = C["sv"]
            if ti >= 2 and not _NO_RT:
                tok0 = ti * 128 - HALO
                pa = pf[6 + ti % 2]
                key = PFN[6 + ti % 2]
                bank_wait_and_claim(key)
                wt("tensor", "sv", evtile[ti])
                ca = None
                for dc in range(NDC):
                    lhs = xnT[:, dc * STT + ti * 128:dc * STT + (ti + 1) * 128]
                    rhs = wasb[:, dc * E:(dc + 1) * E]
                    ca = mm(pa[:, 0:16], lhs, rhs, dc == 0, dc == NDC - 1,
                            inc="sp" if dc == NDC - 1 else None)
                wt("scalar", "sp", ca)
                cs = sop(lambda EE, p=pa: EE.activation(sig[:, :], p[:, 0:16], AF.Sigmoid))
                wt("vector", "ss", cs)
                c = vop(lambda EE: EE.reduce_max(rm1[:, :], sig[:, :], axis=AX.X))
                wt("vector", "sv", c)
                c = vop(lambda EE: EE.tensor_scalar(rge2[:, :], sig[:, :], rm1[:, 0:1], None, ALU.is_ge))
                wt("vector", "sv", c)
                c = vop(lambda EE: EE.scalar_tensor_tensor(
                    raff2[:, :], rge2[:, :], NEG, sig[:, :], ALU.mult, ALU.add))
                wt("vector", "sv", c)
                c = vop(lambda EE: EE.reduce_max(rm2[:, :], raff2[:, :], axis=AX.X))
                wt("vector", "sv", c)
                c = vop(lambda EE: EE.tensor_scalar(rge2[:, :], sig[:, :], rm2[:, 0:1], None, ALU.is_ge))
                c = vop(lambda EE: EE.tensor_add(rss[:, :], rm1[:, :], rm2[:, :]))
                wt("vector", "sv", c)
                c = vop(lambda EE: EE.reciprocal(rrs[:, :], rss[:, :]))
                wt("vector", "sv", c)
                c = vop(lambda EE: EE.tensor_mul(rcb[:, :], sig[:, :], rge2[:, :]))
                wt("vector", "sv", c)
                ccb = vop(lambda EE: EE.tensor_scalar(rcb[:, :], rcb[:, :], rrs[:, 0:1], None, ALU.mult))
                wt("tensor", "sv", ccb)
                key2 = PFN[6 + (ti + 1) % 2]
                pa2 = pf[6 + (ti + 1) % 2]
                bank_wait_and_claim(key2)
                ctp = mm(pa2[0:16, 0:128], rcb[:, 0:16], idf32[:, :], True, True, inc="sp")
                wt("vector", "sp", ctp)
                dst = combT[:, tok0:tok0 + 128]
                ce2 = vop(lambda EE, d=dst, p=pa2: EE.tensor_copy(d, p[0:16, 0:128]))
                bank_release(key2, "sv", ce2)
                bank_release(key, "ss", cs)
                xcb = xncb[:, buf * 2064:(buf + 1) * 2064]
                sost = "soa" if buf == 0 else "sob"
                wt("vector", sost, xncb_st.get(buf, 0))
                cxa = vop(lambda EE, o=xcb, i=xn: EE.tensor_copy(o[:, 0:2048], i))
                cxb = vop(lambda EE, o=xcb: EE.tensor_copy(o[:, 2048:2064], rcb[:, :]))
                wt("sync", "sv", cxb)
                xncb_st[buf] = dma("sync", xncb_d[tok0:tok0 + 128, :], xcb, sost)

        p0_pend = None
        for ti in range(_N_TI):
            c8t = p0_stats(ti)
            if p0_pend is not None:
                p0_xn(*p0_pend)
            p0_pend = (ti, c8t)
        if p0_pend is not None:
            p0_xn(*p0_pend)
            p0_pend = None

        fr = Ring(C["sp"])
        xnT_ok = C["sv"]
        KVHN = _N_KVH

        def cp_evac(dst_ap):
            def cb(jb):
                wt("vector", "sp", jb["done"])
                c = vop(lambda EE, d=dst_ap, b=jb["bank"]: EE.tensor_copy(d, b))
                return ("sv", c)
            return cb

        kjobs = []
        for kvh in range(KVHN):
            s = fr.take()
            ck = load_chunk(rreg(s), dr["wk_p"][kvh], fr.release[s], s)
            for tix, (t0, tw) in enumerate([(0, 512), (512, 512), (1024, 256)]):
                steps = [(rreg(s)[:, dc * 128:(dc + 1) * 128],
                          xnT[:, dc * STT + t0:dc * STT + t0 + tw]) for dc in range(NDC)]
                steps.append((bkr[0:1, kvh * 128:(kvh + 1) * 128], onesb[0:1, 0:tw]))
                bidx = (kvh * 3 + tix) % 4
                kjobs.append(dict(bankkey=PFN[bidx], bank=pf[bidx][:, 0:tw], steps=steps,
                                  waits=[ck, ("sv", xnT_ok)],
                                  evac=cp_evac(KTb[:, kvh * STT + t0:kvh * STT + t0 + tw]),
                                  slot=s))
        emit_jobs(kjobs)
        for kvh in range(KVHN):
            fr.release[kjobs[kvh * 3]["slot"]] = kjobs[kvh * 3 + 2]["done"]

        vslots = []
        for dc2 in range(8 if KVHN else 0):
            s = fr.take()
            src = dr["wv_p"][dc2 * 2:dc2 * 2 + 2].rearrange("c p m -> p c m")
            cv = load_chunk(rreg(s, 1024).rearrange("p (c m) -> p c m", m=512), src,
                            fr.release[s], s)
            vslots.append((s, cv))
        vjobs = []
        for tb in range(NTI if KVHN else 0):
            steps = []
            for dc in range(NDC):
                s, _ = vslots[dc // 2]
                steps.append((xnT[:, dc * STT + tb * 128:dc * STT + (tb + 1) * 128],
                              rreg(s)[:, (dc % 2) * 512:(dc % 2 + 1) * 512]))
            steps.append((onesb[0:1, 0:128], bvr[0:1, :]))
            vjobs.append(dict(bankkey=PFN[tb % 4], bank=pf[tb % 4][:, :], steps=steps,
                              waits=[w for _, w in vslots] + [("sv", xnT_ok)],
                              evac=cp_evac(Vtm[:, tb * 512:(tb + 1) * 512])))
        emit_jobs(vjobs)
        for s, _ in vslots:
            fr.release[s] = vjobs[-1]["done"]

        wt("sync", "szs", C["szs"])
        wt("sync", "sra", C["sra"])
        wt("sync", "srb", C["srb"])
        wt("sync", "sv", CZ_OK)
        cini = dma("sync", idx_d.rearrange("(a b) x -> a (b x)", a=128), bigisb[:, :], "sxc")
        for tc8 in range(8):
            cini = dma("sync", racc_d[tc8 * 128:(tc8 + 1) * 128, :], zerosb[:, 0:2048], "sxc")
        SXC_INIT = C["sxc"]

        comb_ok0 = C["sv"]
        c = vop(lambda EE: EE.tensor_scalar(cs0[:, :], combT[:, :], 0.0, None, ALU.is_gt))
        wt("vector", "sv", c)
        cm1 = vop(lambda EE: EE.tensor_scalar(
            cs1[:, :], cs0[:, :], -1.0, 1.0, ALU.mult, ALU.add))
        cur, oth = cs0, qb0
        step = 1
        while step < 1024:
            wt("vector", "sv", C["sv"])
            ca_ = vop(lambda EE, a=oth, b=cur, st_=step: EE.tensor_tensor(
                a[:, st_:1024], b[:, st_:1024], b[:, 0:1024 - st_], ALU.add))
            cb_ = vop(lambda EE, a=oth, b=cur, st_=step: EE.tensor_copy(
                a[:, 0:st_], b[:, 0:st_]))
            cur, oth = oth, cur
            step *= 2
        wt("vector", "sv", C["sv"])
        cq = vop(lambda EE, a=cur: EE.tensor_scalar(
            a[:, :], a[:, :], -1.0, ecapsb[:, 0:1], ALU.add, ALU.add))
        wt("vector", "sv", cq)
        cg3 = vop(lambda EE, a=cur: EE.scalar_tensor_tensor(
            a[:, :], cs1[:, :], BIGF, a[:, :], ALU.mult, ALU.add))
        wt("vector", "sv", cg3)
        cg_ = vop(lambda EE, a=cur: EE.tensor_scalar(
            cs1[:, :], a[:, :], ecapsb[:, 0:1], CAP - 0.5, ALU.subtract, ALU.is_gt))
        wt("vector", "sv", cg_)
        cg2 = vop(lambda EE, a=cur: EE.scalar_tensor_tensor(
            a[:, :], cs1[:, :], BIGF, a[:, :], ALU.mult, ALU.add))
        q_ok = cg2
        for tc8 in range(8):
            key = PFN[tc8 % 2]
            bank_wait_and_claim(key)
            wt("tensor", "sv", q_ok)
            ctq = mm(pf[tc8 % 2][0:128, 0:16], cur[0:16, tc8 * 128:(tc8 + 1) * 128],
                     idf32[0:16, 0:16], True, True, inc="sp")
            wt("vector", "sp", ctq)
            ceq = vop(lambda EE, t=tc8, p=pf[tc8 % 2]: EE.tensor_copy(
                qtm[:, t * 16:(t + 1) * 16], p[0:128, 0:16]))
            bank_release(key, "sv", ceq)
        for tc8 in range(8):
            sl = qtm[:, tc8 * 16:(tc8 + 1) * 16]
            wt("vector", "sv", C["sv"])
            c1_ = vop(lambda EE, s2=sl: EE.tensor_reduce(p12f[:, 0:1], s2, AX.X, ALU.min))
            wt("vector", "sv", c1_)
            c2_ = vop(lambda EE, s2=sl: EE.tensor_scalar(
                mm16[:, :], s2, p12f[:, 0:1], None, ALU.is_le))
            wt("vector", "sv", c2_)
            c3_ = vop(lambda EE, s2=sl: EE.scalar_tensor_tensor(
                mm16[:, :], mm16[:, :], BIGF, s2, ALU.mult, ALU.add))
            wt("vector", "sv", c3_)
            c4_ = vop(lambda EE: EE.tensor_reduce(p12f[:, 1:2], mm16[:, :], AX.X, ALU.min))
            wt("vector", "sv", c4_)
            c5_ = vop(lambda EE, t=tc8: EE.tensor_copy(p12i[:, 2 * t:2 * t + 2], p12f[:, 0:2]))
        p12_ok = C["sv"]

        scat_last = {}
        for tc8 in range(8):
            buf2 = tc8 % 2
            sln = "sla" if buf2 == 0 else "slb"
            ssn = "ssd" if buf2 == 0 else "sse"
            xcb = xncb[:, buf2 * 2064:(buf2 + 1) * 2064]
            if tc8 >= 2:
                wt("gpsimd", ssn, scat_last[tc8 - 2])
            wt("gpsimd", "soa", C["soa"])
            wt("gpsimd", "sob", C["sob"])
            cl_ = dma("gpsimd", xcb, xncb_d[tc8 * 128:(tc8 + 1) * 128, :], sln)
            wt("gpsimd", sln, cl_)
            wt("gpsimd", "sv", p12_ok)
            wt("gpsimd", "sxc", SXC_INIT)
            last = None
            for k2 in range(2):
                idxap = p12i[:, 2 * tc8 + k2:2 * tc8 + k2 + 1]
                c1_ = bump(ssn, 16)
                em("gpsimd", lambda EE, o=xg_d, ix=idxap, i=xcb, s2=sems[ssn]:
                   EE.indirect_dma_start(o[:, :], bass.IndirectOffsetOnAxis(ix, 0), i, None,
                                         bounds_check=rb_cap, oob_is_err=False
                                         ).then_inc(s2, 16))
                wt("gpsimd", ssn, c1_)
                last = bump(ssn, 16)
                em("gpsimd", lambda EE, o=idx_d, ix=idxap, i=iotsb[:, tc8:tc8 + 1],
                   s2=sems[ssn]:
                   EE.indirect_dma_start(o[:, :], bass.IndirectOffsetOnAxis(ix, 0), i, None,
                                         bounds_check=rb_cap, oob_is_err=False
                                         ).then_inc(s2, 16))
                wt("gpsimd", ssn, last)
            scat_last[tc8] = last
        SSD_ALL = (C["ssd"], C["sse"])

        for kvh in range(KVHN):
            qslot = {}
            qck = {}
            for sh in range(4):
                s = fr.take()
                qslot[sh] = s
                qck[sh] = load_chunk(rreg(s), dr["wq_p"][kvh * 4 + sh], fr.release[s], s)

            def q_evac(dst_ap):
                def cb(jb):
                    wt("vector", "sp", jb["done"])
                    c = vop(lambda EE, d=dst_ap, b=jb["bank"]: EE.tensor_scalar(
                        d, b, SCALE, None, ALU.mult))
                    return ("sv", c)
                return cb

            qjobs = []
            for sh in range(4):
                for tt in range(2):
                    steps = [(rreg(qslot[sh])[:, dc * 128:(dc + 1) * 128],
                              xnT[:, dc * STT + HALO + tt * 512:dc * STT + HALO + (tt + 1) * 512])
                             for dc in range(NDC)]
                    hh = kvh * 4 + sh
                    steps.append((bqr[0:1, hh * 128:(hh + 1) * 128], onesb[0:1, 0:512]))
                    bidx = (sh * 2 + tt) % 4
                    qjobs.append(dict(bankkey=PFN[bidx], bank=pf[bidx][:, :], steps=steps,
                                      waits=[qck[sh]],
                                      evac=q_evac(QTb[:, sh * ST + tt * 512:sh * ST + (tt + 1) * 512])))
            emit_jobs(qjobs)
            for sh in range(4):
                fr.release[qslot[sh]] = qjobs[sh * 2 + 1]["done"]
            qt_ok = C["sv"]

            at_done = {}
            pending = None

            def emit_atav(p):
                it, dbl, qb, sh = p["it"], p["dbl"], p["qb"], p["sh"]
                wt("tensor", "sv", p["ca2"])
                at_ok = None
                for t3 in range(3):
                    key = PFN[2 + t3]
                    reg = pf[2 + t3][:, 0:128]
                    bank_wait_and_claim(key)
                    cc = mmtp(reg, A2b[:, dbl * 384 + t3 * 128:dbl * 384 + (t3 + 1) * 128],
                              idbf[:, :])
                    wt("vector", "sp", cc)
                    dst = ATb[:, dbl * 384 + t3 * 128:dbl * 384 + (t3 + 1) * 128]
                    ce = vop(lambda EE, d=dst, r=reg: EE.tensor_copy(d, r))
                    bank_release(key, "sv", ce)
                    at_ok = ce
                okey = PFN[dbl]
                pot = pf[dbl][:, 0:128]
                bank_wait_and_claim(okey)
                wt("tensor", "sv", at_ok)
                cot = None
                for t3 in range(3):
                    tbv = qb + t3
                    cot = mm(pot, Vtm[:, tbv * 512 + kvh * 128:tbv * 512 + (kvh + 1) * 128],
                             ATb[:, dbl * 384 + t3 * 128:dbl * 384 + (t3 + 1) * 128],
                             t3 == 0, t3 == 2, inc="sp" if t3 == 2 else None)
                at_done[it] = cot
                wt("vector", "sp", cot)
                dst = OTk[:, sh * ST + qb * 128:sh * ST + (qb + 1) * 128]
                ceo = vop(lambda EE, d=dst, p=pot: EE.tensor_copy(d, p))
                bank_release(okey, "sv", ceo)

            for qb in range(8):
                mi = 0 if (st == 0 and qb == 0) else (1 if (st == 0 and qb == 1) else 2)
                for sh in range(4):
                    it = qb * 4 + sh
                    dbl = it % 2
                    tri = it % 3
                    pkey = PFN[5 + tri]
                    psc = pf[5 + tri]
                    bank_wait_and_claim(pkey)
                    wt("tensor", "sv", qt_ok)
                    mm(psc[:, 0:384], QTb[:, sh * ST + qb * 128:sh * ST + (qb + 1) * 128],
                       KTb[:, kvh * STT + qb * 128:kvh * STT + qb * 128 + 384], True, False)
                    csc = mm(psc[:, 0:384], idbf[:, :], masksb[:, mi * 384:(mi + 1) * 384],
                             False, True, inc="sp")
                    wt("vector", "sp", csc)
                    cmx = vop(lambda EE, p=psc, d=tri: EE.reduce_max(
                        smx[:, d:d + 1], p[:, 0:384], axis=AX.X, negate=True))
                    wt("scalar", "sv", cmx)
                    cex = sop(lambda EE, p=psc, d=tri, d2=dbl: EE.activation(
                        Asb[:, d2 * 384:(d2 + 1) * 384], p[:, 0:384], AF.Exp,
                        bias=smx[:, d:d + 1], accum_out=srs[:, d:d + 1]))
                    bank_release(pkey, "ss", cex)
                    wt("vector", "ss", cex)
                    crr = vop(lambda EE, d=tri: EE.reciprocal(srr[:, d:d + 1], srs[:, d:d + 1]))
                    wt("vector", "sv", crr)
                    if it >= 2:
                        wt("vector", "sp", at_done.get(it - 2, 0))
                    ca2 = vop(lambda EE, d=tri, d2=dbl: EE.tensor_scalar(
                        A2b[:, d2 * 384:(d2 + 1) * 384], Asb[:, d2 * 384:(d2 + 1) * 384],
                        srr[:, d:d + 1], None, ALU.mult))
                    me = dict(it=it, dbl=dbl, qb=qb, sh=sh, ca2=ca2)
                    if pending is not None:
                        emit_atav(pending)
                    pending = me
            if pending is not None:
                emit_atav(pending)
                pending = None
            ot_ok = C["sv"]

            for dout in range(16):
                s = fr.take()
                src = dr["wo_p"][dout, kvh]
                cw = load_chunk(rreg(s, 512), src, fr.release[s], s)
                jobs = []
                for tt in range(2):
                    steps = [(rreg(s, 512)[:, sh * 128:(sh + 1) * 128],
                              OTk[:, sh * ST + tt * 512:sh * ST + (tt + 1) * 512])
                             for sh in range(4)]
                    if kvh == 0:
                        steps.append((bor[0:1, dout * 128:(dout + 1) * 128], onesb[0:1, 0:512]))
                    aslice = acc[:, dout * ST + tt * 512:dout * ST + (tt + 1) * 512]
                    if kvh == 0:
                        xslice = xnT[:, dout * STT + HALO + tt * 512:dout * STT + HALO + (tt + 1) * 512]
                        def acc_evac(jb, a=aslice, xx=xslice):
                            wt("vector", "sp", jb["done"])
                            c = vop(lambda EE, aa=a, b=jb["bank"], x2=xx:
                                    EE.tensor_tensor(aa, b, x2, ALU.add))
                            return ("sv", c)
                    else:
                        def acc_evac(jb, a=aslice):
                            wt("vector", "sp", jb["done"])
                            c = vop(lambda EE, aa=a, b=jb["bank"]:
                                    EE.tensor_tensor(aa, b, aa, ALU.add))
                            return ("sv", c)
                    bidx = (dout * 2 + tt) % 4
                    jobs.append(dict(bankkey=PFN[bidx], bank=pf[bidx][:, :], steps=steps,
                                     waits=[cw, ("sv", ot_ok)], evac=acc_evac))
                emit_jobs(jobs)
                fr.release[s] = jobs[1]["done"]

        comb_ok = C["sv"]
        def wsrc(kind, e, j):
            if kind == "g":
                t = dr["rg_p"][e, j] if e < E else dr["sg_p"][e - E, j]
            elif kind == "w1":
                t = dr["rw1_p"][e, j] if e < E else dr["sw1_p"][e - E, j]
            else:
                t = dr["rw2_p"][e, j] if e < E else dr["sw2_p"][e - E, j]
            return t

        prev_stt = {0: 0, 1: 0}

        tp_in_done = {-1: 0}
        zsc_last = {-2: 0, -1: 0}
        sie_cnt = {}
        sxg_cnt = {}
        otp_done = {}
        if _N_PAIRS > 0:
            for e in range(E):
                wt("gpsimd", "szs", zsc_last[e - 2])
                if e == 0:
                    wt("gpsimd", "ssd", SSD_ALL[0])
                    wt("gpsimd", "sse", SSD_ALL[1])
                sie_cnt[e] = (("sie" if e % 2 == 0 else "sif"),
                    dma("gpsimd", idxesb[:, (e % 2) * 2:(e % 2) * 2 + 2],
                        idx_d[e * CAP:(e + 1) * CAP].rearrange("(p c) x -> p (c x)", p=128),
                        "sie" if e % 2 == 0 else "sif"))
                wt("gpsimd", sie_cnt[e][0], sie_cnt[e][1])
                wt("gpsimd", "sp", tp_in_done[e - 1])
                sxg_cnt[e] = dma("gpsimd", xgsb[:, :].rearrange("p (c m) -> p c m", m=2064),
                                 xg_d[e * CAP:(e + 1) * CAP, :].rearrange(
                                     "(p c) m -> p c m", p=128), "sxg")
                wt("vector", "sxg", sxg_cnt[e])
                ccb1 = vop(lambda EE, e_=e: EE.tensor_copy(
                    cbvsb[:, 0:1], xgsb[:, 0 * 2064 + 2048 + e_:0 * 2064 + 2048 + e_ + 1]))
                ccb2 = vop(lambda EE, e_=e: EE.tensor_copy(
                    cbvsb[:, 1:2], xgsb[:, 1 * 2064 + 2048 + e_:1 * 2064 + 2048 + e_ + 1]))
                wt("tensor", "sxg", sxg_cnt[e])
                tpi = 0
                last_tp = 0
                for cc in range(2):
                    for dc in range(NDC):
                        key = PFN[tpi % 6]
                        reg = pf[tpi % 6][:, 0:128]
                        bank_wait_and_claim(key)
                        cc_t = mmtp(reg, xgsb[:, cc * 2064 + dc * 128:cc * 2064 + (dc + 1) * 128],
                                    idbf[:, :])
                        wt("vector", "sp", cc_t)
                        dst = xgT[:, dc * CAP + cc * 128:dc * CAP + (cc + 1) * 128]
                        ce_t = vop(lambda EE, d=dst, r=reg: EE.tensor_copy(d, r))
                        bank_release(key, "sv", ce_t)
                        tpi += 1
                        last_tp = cc_t
                tp_in_done[e] = last_tp
                xgT_ok = C["sv"]
                bias_gb = rgbp[:, e * NFC:(e + 1) * NFC]
                bias_b1 = rb1p[:, e * NFC:(e + 1) * NFC]
                for fc in range(NFC):
                    sg_ = fr.take()
                    s1_ = fr.take()
                    cg = load_chunk(rreg(sg_), wsrc("g", e, fc), fr.release[sg_], sg_)
                    cw = load_chunk(rreg(s1_), wsrc("w1", e, fc), fr.release[s1_], s1_)
                    bsel = (fc % 2) * 2
                    gb_ap = bias_gb[:, fc:fc + 1]
                    b1_ap = bias_b1[:, fc:fc + 1]
                    ydst = ysp[:, fc * CAP:(fc + 1) * CAP]

                    def gelu_evac(jb, bb=gb_ap, pv=prev_stt):
                        wt("scalar", "sp", jb["done"])
                        wt("scalar", "sv", pv[0])
                        c = sop(lambda EE, b=jb["bank"], b2=bb:
                                EE.activation(ygc[:, 0:CAP], b, AF.Gelu, bias=b2[:, 0:1]))
                        return ("ss", c)

                    def stt_evac(jb, o=ydst, bb=b1_ap, pv=prev_stt):
                        wt("vector", "sp", jb["done"])
                        wt("vector", "ss", C["ss"])
                        c = vop(lambda EE, oo=o, b=jb["bank"], b2=bb:
                                EE.scalar_tensor_tensor(
                                    oo, b, b2[:, 0:1], ygc[:, 0:CAP], ALU.add, ALU.mult))
                        pv[0] = c
                        return ("sv", c)

                    jobs = [
                        dict(bankkey=PFN[bsel], bank=pf[bsel][:, 0:CAP],
                             steps=[(rreg(sg_)[:, dc * 128:(dc + 1) * 128],
                                     xgT[:, dc * CAP:(dc + 1) * CAP]) for dc in range(NDC)],
                             evac=gelu_evac, waits=[cg, ("sv", xgT_ok)]),
                        dict(bankkey=PFN[bsel + 1], bank=pf[bsel + 1][:, 0:CAP],
                             steps=[(rreg(s1_)[:, dc * 128:(dc + 1) * 128],
                                     xgT[:, dc * CAP:(dc + 1) * CAP]) for dc in range(NDC)],
                             evac=stt_evac, waits=[cw]),
                    ]
                    emit_jobs(jobs)
                    fr.release[sg_] = jobs[0]["done"]
                    fr.release[s1_] = jobs[1]["done"]
                y_ok = C["sv"]

                wt("vector", "szs", zsc_last[e - 1])
                otp_pend = None

                def emit_otp(items):
                    for (dout, cc2, zap, zev) in items:
                        key = PFN[(dout * 2 + cc2) % 4]
                        reg = pf[(dout * 2 + cc2) % 4][:, 0:128]
                        bank_wait_and_claim(key)
                        wt("tensor", "sv", zev)
                        ct2 = mmtp(reg, zap, idbf[:, :])
                        otp_done[dout % 2] = ct2
                        wt("vector", "sp", ct2)
                        dst = ztm[:, cc2 * D + dout * 128:cc2 * D + (dout + 1) * 128]
                        cz2 = vop(lambda EE, d=dst, r=reg: EE.tensor_copy(d, r))
                        bank_release(key, "sv", cz2)

                for dp in range(8):
                    d0, d1 = 2 * dp, 2 * dp + 1
                    jobs = []
                    for dout in (d0, d1):
                        sw2 = fr.take()
                        cw2 = load_chunk(rreg(sw2, 1024), wsrc("w2", e, dout),
                                         fr.release[sw2], sw2)
                        zslice = zsb[:, (dout % 2) * CAP:(dout % 2) * CAP + CAP]

                        def z_evac(jb, zz=zslice, dd=dout):
                            wt("vector", "sp", jb["done"])
                            wt("vector", "sp", otp_done.get(dd % 2, 0))
                            c = vop(lambda EE, o=zz, b=jb["bank"]: EE.tensor_copy(o, b))
                            jb["zev"] = c
                            return ("sv", c)

                        jobs.append(dict(bankkey=PFN[4 + dout % 2],
                                         bank=pf[4 + dout % 2][:, 0:CAP],
                                         steps=[(rreg(sw2, 1024)[:, fc * 128:(fc + 1) * 128],
                                                 ysp[:, fc * CAP:(fc + 1) * CAP])
                                                for fc in range(NFC)],
                                         evac=z_evac, waits=[cw2, ("sv", y_ok)],
                                         slot=sw2, dout=dout))
                    if otp_pend is not None:
                        emit_otp(otp_pend)
                    emit_jobs(jobs)
                    for jb in jobs:
                        fr.release[jb["slot"]] = jb["done"]
                    otp_pend = []
                    for jb in jobs:
                        dd = jb["dout"]
                        otp_pend.append((dd, 0, zsb[:, (dd % 2) * CAP:(dd % 2) * CAP + 128],
                                         jb["zev"]))
                        otp_pend.append((dd, 1, zsb[:, (dd % 2) * CAP + 128:(dd % 2) * CAP + 256],
                                         jb["zev"]))
                if otp_pend is not None:
                    emit_otp(otp_pend)
                wt("vector", "sv", C["sv"])
                wt("vector", "sv", max(ccb1, ccb2))
                csc1 = vop(lambda EE: EE.tensor_scalar(
                    ztm[:, 0:D], ztm[:, 0:D], cbvsb[:, 0:1], None, ALU.mult))
                wt("vector", "sv", csc1)
                csc2 = vop(lambda EE: EE.tensor_scalar(
                    ztm[:, D:2 * D], ztm[:, D:2 * D], cbvsb[:, 1:2], None, ALU.mult))
                wt("gpsimd", "sv", csc2)
                wt("gpsimd", sie_cnt[e][0], sie_cnt[e][1])
                wt("gpsimd", "sxc", SXC_INIT)
                zl = 0
                for cc in range(2):
                    zl = bump("szs", 16)
                    wt("gpsimd", "szs", zl - 16)
                    em("gpsimd", lambda EE, i=ztm[:, cc * D:(cc + 1) * D],
                       ix=idxesb[:, (e % 2) * 2 + cc:(e % 2) * 2 + cc + 1],
                       s2=sems["szs"]:
                       EE.indirect_dma_start(racc_d[:, :], bass.IndirectOffsetOnAxis(ix, 0),
                                             i, None, bounds_check=rb_tok, oob_is_err=False,
                                             compute_op=ALU.add).then_inc(s2, 16))
                zsc_last[e] = zl
        Z_ALL = C["szs"]

        if _N_PAIRS > 0:
            wt("vector", "szs", C["szs"])
            wt("scalar", "szs", C["szs"])
            pair_es = [E, E + 1]
            ybufs = {E: y0, E + 1: y1}
            for e in pair_es:
                bias_gb = sgbp[:, (e - E) * NFC:(e - E + 1) * NFC]
                bias_b1 = sb1p[:, (e - E) * NFC:(e - E + 1) * NFC]
                ybuf = ybufs[e]
                for fc in range(NFC):
                    sg_ = fr.take()
                    s1_ = fr.take()
                    cg = load_chunk(rreg(sg_), wsrc("g", e, fc), fr.release[sg_], sg_)
                    cw = load_chunk(rreg(s1_), wsrc("w1", e, fc), fr.release[s1_], s1_)
                    bsel = (fc % 2) * 4
                    banks = [bsel, bsel + 1, bsel + 2, bsel + 3]
                    jobs = []
                    for part, (slot, cload) in enumerate([(sg_, cg), (s1_, cw)]):
                        for tt in range(2):
                            steps = [(rreg(slot)[:, dc * 128:(dc + 1) * 128],
                                      xnT[:, dc * STT + HALO + tt * 512:
                                          dc * STT + HALO + (tt + 1) * 512])
                                     for dc in range(NDC)]
                            bidx = banks[part * 2 + tt]
                            gb_ap = bias_gb[:, fc:fc + 1]
                            b1_ap = bias_b1[:, fc:fc + 1]
                            yg_ap = ygc[:, tt * 512:(tt + 1) * 512]
                            ydst = ybuf[:, fc * ST + tt * 512:fc * ST + (tt + 1) * 512]
                            if part == 0:
                                def gelu_evac(jb, o=yg_ap, bb=gb_ap, pv=prev_stt, t=tt):
                                    wt("scalar", "sp", jb["done"])
                                    wt("scalar", "sv", pv[t])
                                    c = sop(lambda EE, oo=o, b=jb["bank"], b2=bb:
                                            EE.activation(oo, b, AF.Gelu, bias=b2[:, 0:1]))
                                    return ("ss", c)
                                jobs.append(dict(bankkey=PFN[bidx], bank=pf[bidx][:, :],
                                                 steps=steps, evac=gelu_evac,
                                                 waits=[cload, ("sv", xnT_ok)]))
                            else:
                                def stt_evac(jb, o=ydst, bb=b1_ap, yg=yg_ap, pv=prev_stt, t=tt):
                                    wt("vector", "sp", jb["done"])
                                    wt("vector", "ss", jb["gelu_ss"])
                                    c = vop(lambda EE, oo=o, b=jb["bank"], b2=bb, yy=yg:
                                            EE.scalar_tensor_tensor(
                                                oo, b, b2[:, 0:1], yy, ALU.add, ALU.mult))
                                    pv[t] = c
                                    return ("sv", c)
                                jobs.append(dict(bankkey=PFN[bidx], bank=pf[bidx][:, :],
                                                 steps=steps, evac=stt_evac, tt=tt,
                                                 waits=[cload]))
                    emit_jobs(jobs[0:2])
                    jobs[2]["gelu_ss"] = C["ss"]
                    jobs[3]["gelu_ss"] = C["ss"]
                    emit_jobs(jobs[2:4])
                    fr.release[sg_] = jobs[1]["done"]
                    fr.release[s1_] = jobs[3]["done"]
            y_ok = C["sv"]

            for dout in range(16):
                slots = {}
                cls = {}
                for e in pair_es:
                    s2_ = fr.take()
                    slots[e] = s2_
                    cls[e] = load_chunk(rreg(s2_, 1024), wsrc("w2", e, dout),
                                        fr.release[s2_], s2_)
                jobs = []
                for tt in range(2):
                    steps = []
                    for e in pair_es:
                        for fc in range(NFC):
                            steps.append((rreg(slots[e], 1024)[:, fc * 128:(fc + 1) * 128],
                                          ybufs[e][:, fc * ST + tt * 512:fc * ST + (tt + 1) * 512]))
                    steps.append((rb2sb[:, dout * 128:(dout + 1) * 128],
                                  combT[:, tt * 512:(tt + 1) * 512]))
                    steps.append((sb2sb[:, dout * 128:(dout + 1) * 128],
                                  ones2b[:, tt * 512:(tt + 1) * 512]))
                    aslice = acc[:, dout * ST + tt * 512:dout * ST + (tt + 1) * 512]
                    def add_evac(jb, a=aslice):
                        wt("vector", "sp", jb["done"])
                        c = vop(lambda EE, aa=a, b=jb["bank"]: EE.tensor_tensor(aa, b, aa, ALU.add))
                        return ("sv", c)
                    bidx = (dout * 2 + tt) % 4
                    jobs.append(dict(bankkey=PFN[bidx], bank=pf[bidx][:, :], steps=steps,
                                     waits=[cls[pair_es[0]], cls[pair_es[1]], ("sv", y_ok)],
                                     evac=add_evac))
                emit_jobs(jobs)
                for e in pair_es:
                    fr.release[slots[e]] = jobs[1]["done"]

        acc_ok = C["sv"]
        ost_cnt = {}
        for ti in range(0 if _NO_P4 else 8):
            buf = ti % 2
            xs = xstage[:, buf * D:(buf + 1) * D]
            osb = ostage[:, buf * D:(buf + 1) * D]
            wt("sync", "sv", C["sv"])
            wt("sync", "ss", C["ss"])
            wt("sync", "sp", C["sp"])
            sxn = "sxa" if buf == 0 else "sxb"
            cx = dma("sync", xs, dr["x_sl"][row0 + HALO + ti * 128:row0 + HALO + ti * 128 + 128, :],
                     sxn)
            rsb = rstage[:, buf * D:(buf + 1) * D]
            wt("sync", "szs", Z_ALL if _N_PAIRS > 0 else 0)
            srn = "sra" if buf == 0 else "srb"
            crc = dma("sync", rsb, racc_d[ti * 128:(ti + 1) * 128, :], srn)
            wt("vector", sxn, cx)
            wt("vector", srn, crc)
            if ti >= 2:
                wt("vector", "soa" if buf == 0 else "sob", ost_cnt[ti - 2])
            cra = vop(lambda EE, o=osb, i=xs, r=rsb: EE.tensor_tensor(o, i, r, ALU.add))
            wt("vector", "sv", cra)
            wt("tensor", "sv", acc_ok)
            for dc in range(NDC):
                key = PFN[dc % 6]
                reg = pf[dc % 6][:, 0:128]
                bank_wait_and_claim(key)
                cc = mmtp(reg, acc[:, dc * ST + ti * 128:dc * ST + (ti + 1) * 128], idbf[:, :])
                wt("vector", "sp", cc)
                ce = vop(lambda EE, o=osb, d=dc, r=reg: EE.tensor_tensor(
                    o[:, d * 128:(d + 1) * 128], r, o[:, d * 128:(d + 1) * 128], ALU.add))
                bank_release(key, "sv", ce)
            wt("sync", "sv", C["sv"])
            ost_cnt[ti] = dma("sync", out_h[st * 1024 + ti * 128:st * 1024 + (ti + 1) * 128, :],
                              osb, "soa" if buf == 0 else "sob")

    wt("sync", "soa", C["soa"])
    wt("sync", "sob", C["sob"])

    with nc.Block() as block:
        @block.sync
        def _(eng):
            for fn in plan["sync"]:
                fn(eng)

        @block.gpsimd
        def _(eng):
            for fn in plan["gpsimd"]:
                fn(eng)

        @block.tensor
        def _(eng):
            for fn in plan["tensor"]:
                fn(eng)

        @block.vector
        def _(eng):
            for fn in plan["vector"]:
                fn(eng)

        @block.scalar
        def _(eng):
            for fn in plan["scalar"]:
                fn(eng)

    for s in reversed(sem_ctx):
        s.__exit__(None, None, None)
    for p in reversed(psum_ctx):
        p.__exit__(None, None, None)
    return nc


def prep_inputs(L0_attn_x, L0_attn_Wq, L0_attn_bq, L0_attn_Wk, L0_attn_bk,
                L0_attn_Wv, L0_attn_bv, L0_attn_Wo, L0_attn_bo, L0_ffn_wa,
                L0_ffn_rg, L0_ffn_rgb, L0_ffn_rw1, L0_ffn_rb1, L0_ffn_rw2,
                L0_ffn_rb2, L0_ffn_sg, L0_ffn_sgb, L0_ffn_sw1, L0_ffn_sb1,
                L0_ffn_sw2, L0_ffn_sb2):
    f32 = np.float32
    asnp = lambda a: np.asarray(a, dtype=f32)
    x = asnp(L0_attn_x)
    sh = {}
    sh["wq_p"] = np.ascontiguousarray(
        asnp(L0_attn_Wq).reshape(16, 128, H, 128).transpose(2, 1, 0, 3).reshape(H, 128, D))
    sh["wk_p"] = np.ascontiguousarray(
        asnp(L0_attn_Wk).reshape(16, 128, KVH, 128).transpose(2, 1, 0, 3).reshape(KVH, 128, D))
    sh["wv_p"] = asnp(L0_attn_Wv).reshape(16, 128, 512)
    sh["wo_p"] = np.ascontiguousarray(
        asnp(L0_attn_Wo).reshape(KVH, 4, 128, 16, 128).transpose(3, 0, 2, 1, 4)
        .reshape(16, KVH, 128, 512))
    sh["bq_r"] = asnp(L0_attn_bq).reshape(1, D)
    sh["bk_r"] = asnp(L0_attn_bk).reshape(1, 512)
    sh["bv_r"] = asnp(L0_attn_bv).reshape(1, 512)
    sh["bo_r"] = asnp(L0_attn_bo).reshape(1, D)
    sh["wa_n"] = asnp(L0_ffn_wa)
    def dfp(w, ne):
        return np.ascontiguousarray(
            w.reshape(ne, 16, 128, NFC, 128).transpose(0, 3, 2, 1, 4).reshape(ne, NFC, 128, D))
    def fdp(w, ne):
        return np.ascontiguousarray(
            w.reshape(ne, NFC, 128, 16, 128).transpose(0, 3, 2, 1, 4).reshape(ne, 16, 128, F))
    sh["rg_p"] = dfp(asnp(L0_ffn_rg), E)
    sh["rw1_p"] = dfp(asnp(L0_ffn_rw1), E)
    sh["rw2_p"] = fdp(asnp(L0_ffn_rw2), E)
    sh["sg_p"] = dfp(asnp(L0_ffn_sg), SH)
    sh["sw1_p"] = fdp(asnp(L0_ffn_sw2), SH) if False else dfp(asnp(L0_ffn_sw1), SH)
    sh["sw2_p"] = fdp(asnp(L0_ffn_sw2), SH)
    sh["rgb_p"] = np.ascontiguousarray(
        asnp(L0_ffn_rgb).reshape(E, NFC, 128).transpose(2, 0, 1).reshape(128, E * NFC))
    sh["rb1_p"] = np.ascontiguousarray(
        asnp(L0_ffn_rb1).reshape(E, NFC, 128).transpose(2, 0, 1).reshape(128, E * NFC))
    sh["sgb_p"] = np.ascontiguousarray(
        asnp(L0_ffn_sgb).reshape(SH, NFC, 128).transpose(2, 0, 1).reshape(128, SH * NFC))
    sh["sb1_p"] = np.ascontiguousarray(
        asnp(L0_ffn_sb1).reshape(SH, NFC, 128).transpose(2, 0, 1).reshape(128, SH * NFC))
    sh["rb2_n"] = asnp(L0_ffn_rb2)
    sh["sb2_n"] = asnp(L0_ffn_sb2)
    sh["id128"] = np.eye(128, dtype=f32)
    em = np.zeros((E, E * 128), dtype=ml_dtypes.bfloat16)
    for e in range(E):
        em[e, e * 128:(e + 1) * 128] = 1
    sh["emat"] = em
    sh["onesr"] = np.ones((1, 512), dtype=ml_dtypes.bfloat16)
    sh["ecap_c"] = (np.arange(16) * CAP).reshape(16, 1).astype(f32)
    sh["iotatok"] = (np.arange(8)[None, :] * 128 + np.arange(128)[:, None]).astype(np.int32)
    sh["ones2"] = np.ones((2, 1024), dtype=ml_dtypes.bfloat16)

    qi = np.arange(128)[:, None]
    tj = np.arange(384)[None, :]
    band = (tj >= qi + 1) & (tj <= qi + 256)
    mk = lambda m: np.where(m, 0.0, NEG).astype(f32)
    m_band = mk(band)
    m_e0 = mk(band & (tj >= 256))
    m_e1 = mk(band & (tj >= 128))

    in_maps = []
    for c in range(8):
        b, h = c // 2, c % 2
        sl = np.zeros((2304, D), dtype=f32)
        lo = h * 2048 - 256
        src_lo = max(lo, 0)
        sl[src_lo - lo:, :] = x[b, src_lo:h * 2048 + 2048, :]
        m = dict(sh)
        m["x_sl"] = sl
        m["masks"] = (np.stack([m_e0, m_e1, m_band]) if h == 0
                      else np.stack([m_band, m_band, m_band]))
        in_maps.append(m)
    return in_maps


def kernel(**inputs):
    in_maps = prep_inputs(**inputs)
    nc = build_program()
    res = run_bass_kernel_spmd(nc, in_maps, core_ids=list(range(8)))
    out = np.empty((4, 4096, D), dtype=np.float32)
    for c in range(8):
        b, h = c // 2, c % 2
        out[b, h * 2048:(h + 1) * 2048, :] = res.results[c]["out"]
    return out


# version 08 (final)
# speedup vs baseline: 1.5322x; 1.0030x over previous
import os
import numpy as np
import ml_dtypes
import concourse.bass as bass
import concourse.mybir as mybir
from concourse.bass_utils import run_bass_kernel_spmd

F32 = mybir.dt.float32
BF16 = mybir.dt.bfloat16
AF = mybir.ActivationFunctionType
ALU = mybir.AluOpType
AX = mybir.AxisListType

D = 2048
NDC = 16
F = 1024
NFC = 8
E = 16
SH = 2
H, KVH, HD = 16, 4, 128
ST = 1024
HALO = 256
STT = ST + HALO
NTI = STT // 128
SCALE = float(1.0 / np.sqrt(128.0))
NEG = -1e9
CAP = 256
BIGF = 1.0e5
_BIS = os.environ.get("KBISECT", "full")
_DO_P1 = _BIS not in ("p0", "ffnonly")
_N_PAIRS = 0 if _BIS in ("p0", "p1") else (1 if _BIS == "ffn1" else 9)
_N_KVH = KVH if _DO_P1 else 0
_NO_P4 = os.environ.get("KNOP4", "") == "1"
_NO_RT = os.environ.get("KNORT", "") == "1"
_N_TI = int(os.environ.get("KNTI", str(NTI)))
_NSTEP = int(os.environ.get("KNSTEP", "99"))


def build_program():
    nc = bass.Bass("TRN2", target_bir_lowering=False)

    dr = {}
    def di(name, shape, dtype=F32):
        dr[name] = nc.dram_tensor(name, list(shape), dtype, kind="ExternalInput")

    di("x_sl", (2304, D))
    di("wq_p", (H, 128, D))
    di("wk_p", (KVH, 128, D))
    di("wv_p", (16, 128, 512))
    di("wo_p", (16, KVH, 128, 512))
    di("bq_r", (1, D)); di("bk_r", (1, 512)); di("bv_r", (1, 512)); di("bo_r", (1, D))
    di("wa_n", (D, E))
    di("rg_p", (E, NFC, 128, D)); di("rw1_p", (E, NFC, 128, D))
    di("rw2_p", (E, 16, 128, F))
    di("sg_p", (SH, NFC, 128, D)); di("sw1_p", (SH, NFC, 128, D))
    di("sw2_p", (SH, 16, 128, F))
    di("rgb_p", (128, E * NFC)); di("rb1_p", (128, E * NFC))
    di("sgb_p", (128, SH * NFC)); di("sb1_p", (128, SH * NFC))
    di("rb2_n", (E, D)); di("sb2_n", (SH, D))
    di("masks", (3, 128, 384))
    di("id128", (128, 128))
    di("emat", (E, E * 128), BF16)
    di("ecap_c", (16, 1))
    di("iotatok", (128, 8), mybir.dt.int32)
    di("onesr", (1, 512), BF16)
    di("ones2", (2, 1024), BF16)
    out_h = nc.dram_tensor("out", [2048, D], F32, kind="ExternalOutput")
    xncb_d = nc.dram_tensor("xncb_d", [1024, 2064], BF16)
    xg_d = nc.dram_tensor("xg_d", [16 * CAP, 2064], BF16)
    idx_d = nc.dram_tensor("idx_d", [16 * CAP, 1], mybir.dt.int32)
    racc_d = nc.dram_tensor("racc_d", [1024, D], BF16)

    def sa(name, shape, dtype):
        return nc.alloc_sbuf_tensor(name, list(shape), dtype)

    xnT = sa("xnT", (128, NDC * STT), BF16)
    acc = sa("acc", (128, NDC * ST), BF16)
    ring = sa("ring", (128, 8 * 2048), BF16)
    arena = sa("arena", (128, 41472), mybir.dt.uint8)
    abase = nc.lookup_mloc(arena).addr
    def aa(name, shape, dtype, off):
        return nc.alloc_sbuf_tensor_at(name, list(shape), dtype, offset=abase + off)
    xstage = aa("xstage", (128, 2 * D), F32, 0)
    sqstage = aa("sqstage", (128, D), F32, 16384)
    xnstage = aa("xnstage", (128, 2 * D), F32, 24576)
    KTb = aa("KTb", (128, KVH * STT), BF16, 0)
    Vtm = aa("Vtm", (128, NTI * 512), BF16, 10240)
    QTb = aa("QTb", (128, 4 * ST), BF16, 20480)
    OTk = aa("OTk", (128, 4 * ST), BF16, 28672)
    Asb = aa("Asb", (128, 2 * 384), BF16, 36864)
    A2b = aa("A2b", (128, 2 * 384), BF16, 38400)
    ATb = aa("ATb", (128, 2 * 384), BF16, 39936)
    y0 = aa("y0", (128, NFC * ST), BF16, 0)
    y1 = aa("y1", (128, NFC * ST), BF16, 16384)
    ygc = aa("ygc", (128, 1024), BF16, 32768)
    xgsb = aa("xgsb", (128, 2 * 2064), BF16, 0)
    xgT = aa("xgT", (128, 16 * CAP), BF16, 8256)
    ysp = aa("ysp", (128, NFC * CAP), BF16, 16448)
    ztm = aa("ztm", (128, 2 * D), BF16, 20544)
    zsb = aa("zsb", (128, 2 * CAP), BF16, 28736)
    ostage = aa("ostage", (128, 2 * D), F32, 16384)
    rstage = aa("rstage", (128, 2 * D), BF16, 32768)
    masksb = sa("masksb", (128, 3 * 384), BF16)
    combT = sa("combT", (16, ST), BF16)
    cbb = sa("cbb", (128, ST), BF16)
    wasb = sa("wasb", (128, NDC * E), BF16)
    emsb = sa("emsb", (16, E * 128), BF16)
    idbf = sa("idbf", (128, 128), BF16)
    idf32 = sa("idf32", (128, 128), F32)
    onesb = sa("onesb", (1, 512), BF16)
    ones2b = sa("ones2b", (2, 1024), BF16)
    bqr = sa("bqr", (1, D), BF16)
    bkr = sa("bkr", (1, 512), BF16)
    bvr = sa("bvr", (1, 512), BF16)
    bor = sa("bor", (1, D), BF16)
    rb2sb = sa("rb2sb", (16, D), BF16)
    sb2sb = sa("sb2sb", (2, D), BF16)
    rgbp = sa("rgbp", (128, E * NFC), F32)
    rb1p = sa("rb1p", (128, E * NFC), F32)
    sgbp = sa("sgbp", (128, SH * NFC), F32)
    sb1p = sa("sb1p", (128, SH * NFC), F32)
    lnsum = sa("lnsum", (128, 1), F32)
    lnm = sa("lnm", (128, 2), F32)
    lnsq = sa("lnsq", (128, 2), F32)
    lnva = sa("lnva", (128, 1), F32)
    lnrv = sa("lnrv", (128, 2), F32)
    lnrstd = sa("lnrstd", (128, 2), F32)
    sig = sa("sig", (128, 16), F32)
    rm1 = sa("rm1", (128, 1), F32)
    raff2 = sa("raff2", (128, 16), F32)
    rm2 = sa("rm2", (128, 1), F32)
    rge2 = sa("rge2", (128, 16), F32)
    rss = sa("rss", (128, 1), F32)
    rrs = sa("rrs", (128, 1), F32)
    rcb = sa("rcb", (128, 16), F32)
    smx = sa("smx", (128, 3), F32)
    srs = sa("srs", (128, 3), F32)
    srr = sa("srr", (128, 3), F32)
    xncb = sa("xncb", (128, 2 * 2064), BF16)
    cs0 = sa("cs0", (16, 1024), F32)
    cs1 = sa("cs1", (16, 1024), F32)
    qb0 = sa("qb0", (16, 1024), F32)
    qtm = sa("qtm", (128, 8 * 16), F32)
    p12f = sa("p12f", (128, 2), F32)
    p12i = sa("p12i", (128, 16), mybir.dt.int32)
    mm16 = sa("mm16", (128, 16), F32)
    zerosb = sa("zerosb", (128, 2064), BF16)
    bigisb = sa("bigisb", (128, 32), mybir.dt.int32)
    iotsb = sa("iotsb", (128, 8), mybir.dt.int32)
    idxesb = sa("idxesb", (128, 2 * 2), mybir.dt.int32)
    cbvsb = sa("cbvsb", (128, 2), F32)
    ecapsb = sa("ecapsb", (16, 1), F32)

    psum_ctx = [nc.psum_tensor("pf%d" % i, [128, 512], F32) for i in range(8)]
    pf = [p.__enter__() for p in psum_ctx]
    PFN = ["pf%d" % i for i in range(8)]

    sem_names = ["sp", "sv", "ss", "swp", "sxa", "sxb", "soa", "sob",
                 "sxc", "ssc", "szs", "sla", "slb", "ssd", "sse", "sxg", "sie", "sif",
                 "sra", "srb"] + \
                ["sr%d" % i for i in range(8)]
    sem_ctx = [nc.semaphore(n) for n in sem_names]
    sems = dict(zip(sem_names, [s.__enter__() for s in sem_ctx]))
    C = {k: 0 for k in sems}
    plan = {"sync": [], "gpsimd": [], "tensor": [], "vector": [], "scalar": []}
    FREE = {}

    def em(eng, fn):
        plan[eng].append(fn)

    def wt(eng, sem, val):
        if val and val > 0:
            em(eng, lambda EE, s=sems[sem], v=val: EE.wait_ge(s, v))

    def bump(inc, n=1):
        C[inc] += n
        return C[inc]

    def mm(bank_ap, lhsT_ap, rhs_ap, start, stop, inc=None):
        cnt = bump(inc) if inc else None
        def fn(EE, o=bank_ap, l=lhsT_ap, r=rhs_ap, a=start, b=stop, s=(sems[inc] if inc else None)):
            ins = EE.matmul(o, l, r, start=a, stop=b)
            if s is not None:
                ins.then_inc(s, 1)
        em("tensor", fn)
        return cnt

    def mmtp(out_ap, in_ap, ident_ap):
        return mm(out_ap, in_ap, ident_ap, True, True, inc="sp")

    def vop(builder, inc="sv"):
        cnt = bump(inc)
        em("vector", lambda EE, b=builder, s=sems[inc]: b(EE).then_inc(s, 1))
        return cnt

    def sop(builder, inc="ss"):
        cnt = bump(inc)
        em("scalar", lambda EE, b=builder, s=sems[inc]: b(EE).then_inc(s, 1))
        return cnt

    def dma(eng, dst_ap, src_ap, inc):
        cnt = bump(inc, 16)
        em(eng, lambda EE, o=dst_ap, i=src_ap, s=sems[inc]: EE.dma_start(o, i).then_inc(s, 16))
        return cnt

    def bank_wait_and_claim(key):
        ent = FREE.pop(key, None)
        if ent:
            wt("tensor", ent[0], ent[1])

    def bank_release(key, sem, cnt):
        FREE[key] = (sem, cnt)

    def emit_jobs(jobs):
        for j0 in range(0, len(jobs), 2):
            pair = jobs[j0:j0 + 2]
            for jb in pair:
                for s, v in jb.get("waits", ()):
                    wt("tensor", s, v)
                bank_wait_and_claim(jb["bankkey"])
            nk = max(len(jb["steps"]) for jb in pair)
            for k in range(nk):
                for jb in pair:
                    if k >= len(jb["steps"]):
                        continue
                    l, r = jb["steps"][k]
                    last = k == len(jb["steps"]) - 1
                    c = mm(jb["bank"], l, r, k == 0, last, inc="sp" if last else None)
                    if last:
                        jb["done"] = c
            for jb in pair:
                bank_release(jb["bankkey"], *jb["evac"](jb))

    class Ring:
        def __init__(self, init_release):
            self.i = 0
            self.release = [init_release] * 8

        def take(self):
            s = self.i % 8
            self.i += 1
            return s

    def load_chunk(rg_ap, src_ap, slot_release, slot):
        wt("gpsimd", "sp", slot_release)
        c = dma("gpsimd", rg_ap, src_ap, "sr%d" % slot)
        return ("sr%d" % slot, c)

    def rreg(s, cols=2048):
        return ring[:, s * 2048:s * 2048 + cols]

    rb_cap = nc.gpsimd.alloc_register("rb_cap")
    rb_tok = nc.gpsimd.alloc_register("rb_tok")
    em("gpsimd", lambda EE: EE.reg_mov(rb_cap, 16 * CAP - 1))
    em("gpsimd", lambda EE: EE.reg_mov(rb_tok, 1023))

    pre = [
        (masksb.rearrange("p (m t) -> p m t", t=384), dr["masks"].rearrange("m p t -> p m t")),
        (idbf[:, :], dr["id128"][:, :]),
        (idf32[:, :], dr["id128"][:, :]),
        (emsb[:, :], dr["emat"][:, :]),
        (onesb[:, :], dr["onesr"][:, :]),
        (ones2b[:, :], dr["ones2"][:, :]),
        (bqr[:, :], dr["bq_r"][:, :]),
        (bkr[:, :], dr["bk_r"][:, :]),
        (bvr[:, :], dr["bv_r"][:, :]),
        (bor[:, :], dr["bo_r"][:, :]),
        (rb2sb[:, :], dr["rb2_n"][:, :]),
        (sb2sb[:, :], dr["sb2_n"][:, :]),
        (rgbp[:, :], dr["rgb_p"][:, :]),
        (rb1p[:, :], dr["rb1_p"][:, :]),
        (sgbp[:, :], dr["sgb_p"][:, :]),
        (sb1p[:, :], dr["sb1_p"][:, :]),
    ]
    for o, i in pre:
        dma("gpsimd", o, i, "swp")
    for dc in range(NDC):
        dma("gpsimd", wasb[:, dc * E:(dc + 1) * E], dr["wa_n"][dc * 128:(dc + 1) * 128, :], "swp")
    dma("gpsimd", iotsb[:, :], dr["iotatok"][:, :], "swp")
    dma("gpsimd", ecapsb[:, :], dr["ecap_c"][:, :], "swp")
    SW_PRE = C["swp"]
    for eng in ("tensor", "vector", "scalar"):
        wt(eng, "swp", SW_PRE)
    cz1 = vop(lambda EE: EE.memset(zerosb[:, :], 0.0))
    cz2 = vop(lambda EE: EE.memset(bigisb[:, :], 100000))
    CZ_OK = cz2
    wt("sync", "sv", CZ_OK)
    for zi in range(32):
        dma("sync", xg_d[zi * 128:(zi + 1) * 128, :], zerosb[:, :], "sxc")

    for st in range(2):
        row0 = st * 1024

        if st > 0:
            for eng in ("vector", "scalar", "sync"):
                wt(eng, "sp", C["sp"])
                wt(eng, "soa", C["soa"])
                wt(eng, "sob", C["sob"])
            wt("vector", "ss", C["ss"])
            wt("gpsimd", "sp", C["sp"])

        evtile = {}
        xncb_st = {}

        def p0_stats(ti):
            buf = ti % 2
            xs = xstage[:, buf * D:(buf + 1) * D]
            wt("sync", "sv", C["sv"])
            wt("sync", "ss", C["ss"])
            sxn = "sxa" if buf == 0 else "sxb"
            cx = dma("sync", xs, dr["x_sl"][row0 + ti * 128:row0 + ti * 128 + 128, :], sxn)
            lm = lnm[:, buf:buf + 1]
            lrv = lnrv[:, buf:buf + 1]
            sqa = lnsq[:, buf:buf + 1]
            wt("vector", sxn, cx)
            c1 = vop(lambda EE, i=xs: EE.reduce_sum(lnsum[:, :], i, axis=AX.X))
            wt("scalar", sxn, cx)
            c2 = sop(lambda EE, i=xs, a=sqa: EE.activation(sqstage[:, :], i, AF.Square, accum_out=a))
            wt("vector", "sv", c1)
            c3 = vop(lambda EE, o=lm: EE.tensor_scalar(o, lnsum[:, :], 1.0 / D, None, ALU.mult))
            wt("vector", "sv", c3)
            c4 = vop(lambda EE, m=lm: EE.scalar_tensor_tensor(
                lnva[:, :], m, -1.0, m, ALU.mult, ALU.mult))
            wt("vector", "ss", c2)
            wt("vector", "sv", c4)
            c5 = vop(lambda EE, a=sqa: EE.tensor_scalar(
                lnva[:, :], a, 1.0 / D, lnva[:, 0:1], ALU.mult, ALU.add))
            wt("vector", "sv", c5)
            c6 = vop(lambda EE, o=lrv: EE.tensor_scalar(o, lnva[:, :], 1e-5, None, ALU.add))
            wt("vector", "sv", c6)
            c7 = vop(lambda EE, o=lrv: EE.reciprocal(o, o))
            wt("scalar", "sv", c7)
            c8 = sop(lambda EE, i=lrv, o=lnrstd[:, buf:buf + 1]: EE.activation(o, i, AF.Sqrt))
            return c8

        def p0_xn(ti, c8):
            buf = ti % 2
            xs = xstage[:, buf * D:(buf + 1) * D]
            xn = xnstage[:, buf * D:(buf + 1) * D]
            lm = lnm[:, buf:buf + 1]
            wt("vector", "ss", c8)
            c9 = vop(lambda EE, i=xs, o=xn, m=lm, r=lnrstd[:, buf:buf + 1]: EE.tensor_scalar(
                o, i, m, r, ALU.subtract, ALU.mult))
            wt("tensor", "sv", c9)
            for dc in range(NDC):
                key = PFN[dc % 6]
                reg = pf[dc % 6][:, 0:128]
                bank_wait_and_claim(key)
                cc = mmtp(reg, xn[:, dc * 128:(dc + 1) * 128], idf32[:, :])
                dst = xnT[:, dc * STT + ti * 128:dc * STT + (ti + 1) * 128]
                wt("vector", "sp", cc)
                ce = vop(lambda EE, d=dst, r=reg: EE.tensor_copy(d, r))
                bank_release(key, "sv", ce)
            evtile[ti] = C["sv"]
            if ti >= 2 and not _NO_RT:
                tok0 = ti * 128 - HALO
                pa = pf[6 + ti % 2]
                key = PFN[6 + ti % 2]
                bank_wait_and_claim(key)
                wt("tensor", "sv", evtile[ti])
                ca = None
                for dc in range(NDC):
                    lhs = xnT[:, dc * STT + ti * 128:dc * STT + (ti + 1) * 128]
                    rhs = wasb[:, dc * E:(dc + 1) * E]
                    ca = mm(pa[:, 0:16], lhs, rhs, dc == 0, dc == NDC - 1,
                            inc="sp" if dc == NDC - 1 else None)
                wt("scalar", "sp", ca)
                cs = sop(lambda EE, p=pa: EE.activation(sig[:, :], p[:, 0:16], AF.Sigmoid))
                wt("vector", "ss", cs)
                c = vop(lambda EE: EE.reduce_max(rm1[:, :], sig[:, :], axis=AX.X))
                wt("vector", "sv", c)
                c = vop(lambda EE: EE.tensor_scalar(rge2[:, :], sig[:, :], rm1[:, 0:1], None, ALU.is_ge))
                wt("vector", "sv", c)
                c = vop(lambda EE: EE.scalar_tensor_tensor(
                    raff2[:, :], rge2[:, :], NEG, sig[:, :], ALU.mult, ALU.add))
                wt("vector", "sv", c)
                c = vop(lambda EE: EE.reduce_max(rm2[:, :], raff2[:, :], axis=AX.X))
                wt("vector", "sv", c)
                c = vop(lambda EE: EE.tensor_scalar(rge2[:, :], sig[:, :], rm2[:, 0:1], None, ALU.is_ge))
                c = vop(lambda EE: EE.tensor_add(rss[:, :], rm1[:, :], rm2[:, :]))
                wt("vector", "sv", c)
                c = vop(lambda EE: EE.reciprocal(rrs[:, :], rss[:, :]))
                wt("vector", "sv", c)
                c = vop(lambda EE: EE.tensor_mul(rcb[:, :], sig[:, :], rge2[:, :]))
                wt("vector", "sv", c)
                ccb = vop(lambda EE: EE.tensor_scalar(rcb[:, :], rcb[:, :], rrs[:, 0:1], None, ALU.mult))
                wt("tensor", "sv", ccb)
                key2 = PFN[6 + (ti + 1) % 2]
                pa2 = pf[6 + (ti + 1) % 2]
                bank_wait_and_claim(key2)
                ctp = mm(pa2[0:16, 0:128], rcb[:, 0:16], idf32[:, :], True, True, inc="sp")
                wt("vector", "sp", ctp)
                dst = combT[:, tok0:tok0 + 128]
                ce2 = vop(lambda EE, d=dst, p=pa2: EE.tensor_copy(d, p[0:16, 0:128]))
                bank_release(key2, "sv", ce2)
                bank_release(key, "ss", cs)
                xcb = xncb[:, buf * 2064:(buf + 1) * 2064]
                sost = "soa" if buf == 0 else "sob"
                wt("vector", sost, xncb_st.get(buf, 0))
                cxa = vop(lambda EE, o=xcb, i=xn: EE.tensor_copy(o[:, 0:2048], i))
                cxb = vop(lambda EE, o=xcb: EE.tensor_copy(o[:, 2048:2064], rcb[:, :]))
                wt("sync", "sv", cxb)
                xncb_st[buf] = dma("sync", xncb_d[tok0:tok0 + 128, :], xcb, sost)

        p0_pend = None
        for ti in range(_N_TI):
            c8t = p0_stats(ti)
            if p0_pend is not None:
                p0_xn(*p0_pend)
            p0_pend = (ti, c8t)
        if p0_pend is not None:
            p0_xn(*p0_pend)
            p0_pend = None

        fr = Ring(C["sp"])
        xnT_ok = C["sv"]
        KVHN = _N_KVH

        def cp_evac(dst_ap):
            def cb(jb):
                wt("vector", "sp", jb["done"])
                c = vop(lambda EE, d=dst_ap, b=jb["bank"]: EE.tensor_copy(d, b))
                return ("sv", c)
            return cb

        kjobs = []
        for kvh in range(KVHN):
            s = fr.take()
            ck = load_chunk(rreg(s), dr["wk_p"][kvh], fr.release[s], s)
            for tix, (t0, tw) in enumerate([(0, 512), (512, 512), (1024, 256)]):
                steps = [(rreg(s)[:, dc * 128:(dc + 1) * 128],
                          xnT[:, dc * STT + t0:dc * STT + t0 + tw]) for dc in range(NDC)]
                steps.append((bkr[0:1, kvh * 128:(kvh + 1) * 128], onesb[0:1, 0:tw]))
                bidx = (kvh * 3 + tix) % 4
                kjobs.append(dict(bankkey=PFN[bidx], bank=pf[bidx][:, 0:tw], steps=steps,
                                  waits=[ck, ("sv", xnT_ok)],
                                  evac=cp_evac(KTb[:, kvh * STT + t0:kvh * STT + t0 + tw]),
                                  slot=s))
        emit_jobs(kjobs)
        for kvh in range(KVHN):
            fr.release[kjobs[kvh * 3]["slot"]] = kjobs[kvh * 3 + 2]["done"]

        vslots = []
        for dc2 in range(8 if KVHN else 0):
            s = fr.take()
            src = dr["wv_p"][dc2 * 2:dc2 * 2 + 2].rearrange("c p m -> p c m")
            cv = load_chunk(rreg(s, 1024).rearrange("p (c m) -> p c m", m=512), src,
                            fr.release[s], s)
            vslots.append((s, cv))
        vjobs = []
        for tb in range(NTI if KVHN else 0):
            steps = []
            for dc in range(NDC):
                s, _ = vslots[dc // 2]
                steps.append((xnT[:, dc * STT + tb * 128:dc * STT + (tb + 1) * 128],
                              rreg(s)[:, (dc % 2) * 512:(dc % 2 + 1) * 512]))
            steps.append((onesb[0:1, 0:128], bvr[0:1, :]))
            vjobs.append(dict(bankkey=PFN[tb % 4], bank=pf[tb % 4][:, :], steps=steps,
                              waits=[w for _, w in vslots] + [("sv", xnT_ok)],
                              evac=cp_evac(Vtm[:, tb * 512:(tb + 1) * 512])))
        emit_jobs(vjobs)
        for s, _ in vslots:
            fr.release[s] = vjobs[-1]["done"]

        wt("sync", "szs", C["szs"])
        wt("sync", "sra", C["sra"])
        wt("sync", "srb", C["srb"])
        wt("sync", "sv", CZ_OK)
        cini = dma("sync", idx_d.rearrange("(a b) x -> a (b x)", a=128), bigisb[:, :], "sxc")
        for tc8 in range(8):
            cini = dma("sync", racc_d[tc8 * 128:(tc8 + 1) * 128, :], zerosb[:, 0:2048], "sxc")
        SXC_INIT = C["sxc"]

        comb_ok0 = C["sv"]
        c = vop(lambda EE: EE.tensor_scalar(cs0[:, :], combT[:, :], 0.0, None, ALU.is_gt))
        wt("vector", "sv", c)
        cm1 = vop(lambda EE: EE.tensor_scalar(
            cs1[:, :], cs0[:, :], -1.0, 1.0, ALU.mult, ALU.add))
        cur, oth = cs0, qb0
        step = 1
        while step < 1024:
            wt("vector", "sv", C["sv"])
            ca_ = vop(lambda EE, a=oth, b=cur, st_=step: EE.tensor_tensor(
                a[:, st_:1024], b[:, st_:1024], b[:, 0:1024 - st_], ALU.add))
            cb_ = vop(lambda EE, a=oth, b=cur, st_=step: EE.tensor_copy(
                a[:, 0:st_], b[:, 0:st_]))
            cur, oth = oth, cur
            step *= 2
        wt("vector", "sv", C["sv"])
        cq = vop(lambda EE, a=cur: EE.tensor_scalar(
            a[:, :], a[:, :], -1.0, ecapsb[:, 0:1], ALU.add, ALU.add))
        wt("vector", "sv", cq)
        cg3 = vop(lambda EE, a=cur: EE.scalar_tensor_tensor(
            a[:, :], cs1[:, :], BIGF, a[:, :], ALU.mult, ALU.add))
        wt("vector", "sv", cg3)
        cg_ = vop(lambda EE, a=cur: EE.tensor_scalar(
            cs1[:, :], a[:, :], ecapsb[:, 0:1], CAP - 0.5, ALU.subtract, ALU.is_gt))
        wt("vector", "sv", cg_)
        cg2 = vop(lambda EE, a=cur: EE.scalar_tensor_tensor(
            a[:, :], cs1[:, :], BIGF, a[:, :], ALU.mult, ALU.add))
        q_ok = cg2
        for tc8 in range(8):
            key = PFN[tc8 % 2]
            bank_wait_and_claim(key)
            wt("tensor", "sv", q_ok)
            ctq = mm(pf[tc8 % 2][0:128, 0:16], cur[0:16, tc8 * 128:(tc8 + 1) * 128],
                     idf32[0:16, 0:16], True, True, inc="sp")
            wt("vector", "sp", ctq)
            ceq = vop(lambda EE, t=tc8, p=pf[tc8 % 2]: EE.tensor_copy(
                qtm[:, t * 16:(t + 1) * 16], p[0:128, 0:16]))
            bank_release(key, "sv", ceq)
        for tc8 in range(8):
            sl = qtm[:, tc8 * 16:(tc8 + 1) * 16]
            wt("vector", "sv", C["sv"])
            c1_ = vop(lambda EE, s2=sl: EE.tensor_reduce(p12f[:, 0:1], s2, AX.X, ALU.min))
            wt("vector", "sv", c1_)
            c2_ = vop(lambda EE, s2=sl: EE.tensor_scalar(
                mm16[:, :], s2, p12f[:, 0:1], None, ALU.is_le))
            wt("vector", "sv", c2_)
            c3_ = vop(lambda EE, s2=sl: EE.scalar_tensor_tensor(
                mm16[:, :], mm16[:, :], BIGF, s2, ALU.mult, ALU.add))
            wt("vector", "sv", c3_)
            c4_ = vop(lambda EE: EE.tensor_reduce(p12f[:, 1:2], mm16[:, :], AX.X, ALU.min))
            wt("vector", "sv", c4_)
            c5_ = vop(lambda EE, t=tc8: EE.tensor_copy(p12i[:, 2 * t:2 * t + 2], p12f[:, 0:2]))
        p12_ok = C["sv"]

        scat_last = {}
        for tc8 in range(8):
            buf2 = tc8 % 2
            sln = "sla" if buf2 == 0 else "slb"
            ssn = "ssd" if buf2 == 0 else "sse"
            xcb = xncb[:, buf2 * 2064:(buf2 + 1) * 2064]
            if tc8 >= 2:
                wt("gpsimd", ssn, scat_last[tc8 - 2])
            wt("gpsimd", "soa", C["soa"])
            wt("gpsimd", "sob", C["sob"])
            cl_ = dma("gpsimd", xcb, xncb_d[tc8 * 128:(tc8 + 1) * 128, :], sln)
            wt("gpsimd", sln, cl_)
            wt("gpsimd", "sv", p12_ok)
            wt("gpsimd", "sxc", SXC_INIT)
            last = None
            for k2 in range(2):
                idxap = p12i[:, 2 * tc8 + k2:2 * tc8 + k2 + 1]
                c1_ = bump(ssn, 16)
                em("gpsimd", lambda EE, o=xg_d, ix=idxap, i=xcb, s2=sems[ssn]:
                   EE.indirect_dma_start(o[:, :], bass.IndirectOffsetOnAxis(ix, 0), i, None,
                                         bounds_check=rb_cap, oob_is_err=False
                                         ).then_inc(s2, 16))
                wt("gpsimd", ssn, c1_)
                last = bump(ssn, 16)
                em("gpsimd", lambda EE, o=idx_d, ix=idxap, i=iotsb[:, tc8:tc8 + 1],
                   s2=sems[ssn]:
                   EE.indirect_dma_start(o[:, :], bass.IndirectOffsetOnAxis(ix, 0), i, None,
                                         bounds_check=rb_cap, oob_is_err=False
                                         ).then_inc(s2, 16))
                wt("gpsimd", ssn, last)
            scat_last[tc8] = last
        SSD_ALL = (C["ssd"], C["sse"])

        for kvh in range(KVHN):
            qslot = {}
            qck = {}
            for sh in range(4):
                s = fr.take()
                qslot[sh] = s
                qck[sh] = load_chunk(rreg(s), dr["wq_p"][kvh * 4 + sh], fr.release[s], s)

            def q_evac(dst_ap):
                def cb(jb):
                    wt("vector", "sp", jb["done"])
                    c = vop(lambda EE, d=dst_ap, b=jb["bank"]: EE.tensor_scalar(
                        d, b, SCALE, None, ALU.mult))
                    return ("sv", c)
                return cb

            qjobs = []
            for sh in range(4):
                for tt in range(2):
                    steps = [(rreg(qslot[sh])[:, dc * 128:(dc + 1) * 128],
                              xnT[:, dc * STT + HALO + tt * 512:dc * STT + HALO + (tt + 1) * 512])
                             for dc in range(NDC)]
                    hh = kvh * 4 + sh
                    steps.append((bqr[0:1, hh * 128:(hh + 1) * 128], onesb[0:1, 0:512]))
                    bidx = (sh * 2 + tt) % 4
                    qjobs.append(dict(bankkey=PFN[bidx], bank=pf[bidx][:, :], steps=steps,
                                      waits=[qck[sh]],
                                      evac=q_evac(QTb[:, sh * ST + tt * 512:sh * ST + (tt + 1) * 512])))
            emit_jobs(qjobs)
            for sh in range(4):
                fr.release[qslot[sh]] = qjobs[sh * 2 + 1]["done"]
            qt_ok = C["sv"]

            at_done = {}
            pending = None

            def emit_atav(p):
                it, dbl, qb, sh = p["it"], p["dbl"], p["qb"], p["sh"]
                wt("tensor", "sv", p["ca2"])
                at_ok = None
                at_sem = None
                for t3 in range(3):
                    key = PFN[2 + t3]
                    reg = pf[2 + t3][:, 0:128]
                    bank_wait_and_claim(key)
                    cc = mmtp(reg, A2b[:, dbl * 384 + t3 * 128:dbl * 384 + (t3 + 1) * 128],
                              idbf[:, :])
                    dst = ATb[:, dbl * 384 + t3 * 128:dbl * 384 + (t3 + 1) * 128]
                    if t3 % 2 == 0:
                        wt("scalar", "sp", cc)
                        ce = sop(lambda EE, d=dst, r=reg: EE.copy(d, r))
                        bank_release(key, "ss", ce)
                        at_sem = ("ss", ce)
                    else:
                        wt("vector", "sp", cc)
                        ce = vop(lambda EE, d=dst, r=reg: EE.tensor_copy(d, r))
                        bank_release(key, "sv", ce)
                        at_sem = ("sv", ce)
                    at_ok = ce
                okey = PFN[dbl]
                pot = pf[dbl][:, 0:128]
                bank_wait_and_claim(okey)
                wt("tensor", "ss", C["ss"])
                wt("tensor", "sv", C["sv"])
                cot = None
                for t3 in range(3):
                    tbv = qb + t3
                    cot = mm(pot, Vtm[:, tbv * 512 + kvh * 128:tbv * 512 + (kvh + 1) * 128],
                             ATb[:, dbl * 384 + t3 * 128:dbl * 384 + (t3 + 1) * 128],
                             t3 == 0, t3 == 2, inc="sp" if t3 == 2 else None)
                at_done[it] = cot
                wt("vector", "sp", cot)
                dst = OTk[:, sh * ST + qb * 128:sh * ST + (qb + 1) * 128]
                ceo = vop(lambda EE, d=dst, p=pot: EE.tensor_copy(d, p))
                bank_release(okey, "sv", ceo)

            for qb in range(8):
                mi = 0 if (st == 0 and qb == 0) else (1 if (st == 0 and qb == 1) else 2)
                for sh in range(4):
                    it = qb * 4 + sh
                    dbl = it % 2
                    tri = it % 3
                    pkey = PFN[5 + tri]
                    psc = pf[5 + tri]
                    bank_wait_and_claim(pkey)
                    wt("tensor", "sv", qt_ok)
                    mm(psc[:, 0:384], QTb[:, sh * ST + qb * 128:sh * ST + (qb + 1) * 128],
                       KTb[:, kvh * STT + qb * 128:kvh * STT + qb * 128 + 384], True, False)
                    csc = mm(psc[:, 0:384], idbf[:, :], masksb[:, mi * 384:(mi + 1) * 384],
                             False, True, inc="sp")
                    wt("vector", "sp", csc)
                    cmx = vop(lambda EE, p=psc, d=tri: EE.reduce_max(
                        smx[:, d:d + 1], p[:, 0:384], axis=AX.X, negate=True))
                    wt("scalar", "sv", cmx)
                    cex = sop(lambda EE, p=psc, d=tri, d2=dbl: EE.activation(
                        Asb[:, d2 * 384:(d2 + 1) * 384], p[:, 0:384], AF.Exp,
                        bias=smx[:, d:d + 1], accum_out=srs[:, d:d + 1]))
                    bank_release(pkey, "ss", cex)
                    wt("vector", "ss", cex)
                    crr = vop(lambda EE, d=tri: EE.reciprocal(srr[:, d:d + 1], srs[:, d:d + 1]))
                    wt("vector", "sv", crr)
                    if it >= 2:
                        wt("vector", "sp", at_done.get(it - 2, 0))
                    ca2 = vop(lambda EE, d=tri, d2=dbl: EE.tensor_scalar(
                        A2b[:, d2 * 384:(d2 + 1) * 384], Asb[:, d2 * 384:(d2 + 1) * 384],
                        srr[:, d:d + 1], None, ALU.mult))
                    me = dict(it=it, dbl=dbl, qb=qb, sh=sh, ca2=ca2)
                    if pending is not None:
                        emit_atav(pending)
                    pending = me
            if pending is not None:
                emit_atav(pending)
                pending = None
            ot_ok = C["sv"]

            for dout in range(16):
                s = fr.take()
                src = dr["wo_p"][dout, kvh]
                cw = load_chunk(rreg(s, 512), src, fr.release[s], s)
                jobs = []
                for tt in range(2):
                    steps = [(rreg(s, 512)[:, sh * 128:(sh + 1) * 128],
                              OTk[:, sh * ST + tt * 512:sh * ST + (tt + 1) * 512])
                             for sh in range(4)]
                    if kvh == 0:
                        steps.append((bor[0:1, dout * 128:(dout + 1) * 128], onesb[0:1, 0:512]))
                    aslice = acc[:, dout * ST + tt * 512:dout * ST + (tt + 1) * 512]
                    if kvh == 0:
                        xslice = xnT[:, dout * STT + HALO + tt * 512:dout * STT + HALO + (tt + 1) * 512]
                        def acc_evac(jb, a=aslice, xx=xslice):
                            wt("vector", "sp", jb["done"])
                            c = vop(lambda EE, aa=a, b=jb["bank"], x2=xx:
                                    EE.tensor_tensor(aa, b, x2, ALU.add))
                            return ("sv", c)
                    else:
                        def acc_evac(jb, a=aslice):
                            wt("vector", "sp", jb["done"])
                            c = vop(lambda EE, aa=a, b=jb["bank"]:
                                    EE.tensor_tensor(aa, b, aa, ALU.add))
                            return ("sv", c)
                    bidx = (dout * 2 + tt) % 4
                    jobs.append(dict(bankkey=PFN[bidx], bank=pf[bidx][:, :], steps=steps,
                                     waits=[cw, ("sv", ot_ok)], evac=acc_evac))
                emit_jobs(jobs)
                fr.release[s] = jobs[1]["done"]

        comb_ok = C["sv"]
        def wsrc(kind, e, j):
            if kind == "g":
                t = dr["rg_p"][e, j] if e < E else dr["sg_p"][e - E, j]
            elif kind == "w1":
                t = dr["rw1_p"][e, j] if e < E else dr["sw1_p"][e - E, j]
            else:
                t = dr["rw2_p"][e, j] if e < E else dr["sw2_p"][e - E, j]
            return t

        prev_stt = {0: 0, 1: 0}

        tp_in_done = {-1: 0}
        zsc_last = {-2: 0, -1: 0}
        sie_cnt = {}
        sxg_cnt = {}
        otp_done = {}
        if _N_PAIRS > 0:
            for e in range(E):
                wt("gpsimd", "szs", zsc_last[e - 2])
                if e == 0:
                    wt("gpsimd", "ssd", SSD_ALL[0])
                    wt("gpsimd", "sse", SSD_ALL[1])
                sie_cnt[e] = (("sie" if e % 2 == 0 else "sif"),
                    dma("gpsimd", idxesb[:, (e % 2) * 2:(e % 2) * 2 + 2],
                        idx_d[e * CAP:(e + 1) * CAP].rearrange("(p c) x -> p (c x)", p=128),
                        "sie" if e % 2 == 0 else "sif"))
                wt("gpsimd", sie_cnt[e][0], sie_cnt[e][1])
                wt("gpsimd", "sp", tp_in_done[e - 1])
                sxg_cnt[e] = dma("gpsimd", xgsb[:, :].rearrange("p (c m) -> p c m", m=2064),
                                 xg_d[e * CAP:(e + 1) * CAP, :].rearrange(
                                     "(p c) m -> p c m", p=128), "sxg")
                wt("vector", "sxg", sxg_cnt[e])
                ccb1 = vop(lambda EE, e_=e: EE.tensor_copy(
                    cbvsb[:, 0:1], xgsb[:, 0 * 2064 + 2048 + e_:0 * 2064 + 2048 + e_ + 1]))
                ccb2 = vop(lambda EE, e_=e: EE.tensor_copy(
                    cbvsb[:, 1:2], xgsb[:, 1 * 2064 + 2048 + e_:1 * 2064 + 2048 + e_ + 1]))
                wt("tensor", "sxg", sxg_cnt[e])
                tpi = 0
                last_tp = 0
                for cc in range(2):
                    for dc in range(NDC):
                        key = PFN[tpi % 6]
                        reg = pf[tpi % 6][:, 0:128]
                        bank_wait_and_claim(key)
                        cc_t = mmtp(reg, xgsb[:, cc * 2064 + dc * 128:cc * 2064 + (dc + 1) * 128],
                                    idbf[:, :])
                        dst = xgT[:, dc * CAP + cc * 128:dc * CAP + (cc + 1) * 128]
                        if tpi % 2 == 0:
                            wt("vector", "sp", cc_t)
                            ce_t = vop(lambda EE, d=dst, r=reg: EE.tensor_copy(d, r))
                            bank_release(key, "sv", ce_t)
                        else:
                            wt("scalar", "sp", cc_t)
                            ce_t = sop(lambda EE, d=dst, r=reg: EE.copy(d, r))
                            bank_release(key, "ss", ce_t)
                        tpi += 1
                        last_tp = cc_t
                tp_in_done[e] = last_tp
                xgT_ok = C["sv"]
                bias_gb = rgbp[:, e * NFC:(e + 1) * NFC]
                bias_b1 = rb1p[:, e * NFC:(e + 1) * NFC]
                for fc in range(NFC):
                    sg_ = fr.take()
                    s1_ = fr.take()
                    cg = load_chunk(rreg(sg_), wsrc("g", e, fc), fr.release[sg_], sg_)
                    cw = load_chunk(rreg(s1_), wsrc("w1", e, fc), fr.release[s1_], s1_)
                    bsel = (fc % 3) * 2
                    gb_ap = bias_gb[:, fc:fc + 1]
                    b1_ap = bias_b1[:, fc:fc + 1]
                    ydst = ysp[:, fc * CAP:(fc + 1) * CAP]

                    def gelu_evac(jb, bb=gb_ap, pv=prev_stt):
                        wt("scalar", "sp", jb["done"])
                        wt("scalar", "sv", pv[0])
                        c = sop(lambda EE, b=jb["bank"], b2=bb:
                                EE.activation(ygc[:, 0:CAP], b, AF.Gelu, bias=b2[:, 0:1]))
                        return ("ss", c)

                    def stt_evac(jb, o=ydst, bb=b1_ap, pv=prev_stt):
                        wt("vector", "sp", jb["done"])
                        wt("vector", "ss", C["ss"])
                        c = vop(lambda EE, oo=o, b=jb["bank"], b2=bb:
                                EE.scalar_tensor_tensor(
                                    oo, b, b2[:, 0:1], ygc[:, 0:CAP], ALU.add, ALU.mult))
                        pv[0] = c
                        return ("sv", c)

                    jobs = [
                        dict(bankkey=PFN[bsel], bank=pf[bsel][:, 0:CAP],
                             steps=[(rreg(sg_)[:, dc * 128:(dc + 1) * 128],
                                     xgT[:, dc * CAP:(dc + 1) * CAP]) for dc in range(NDC)],
                             evac=gelu_evac, waits=[cg, ("sv", xgT_ok)]),
                        dict(bankkey=PFN[bsel + 1], bank=pf[bsel + 1][:, 0:CAP],
                             steps=[(rreg(s1_)[:, dc * 128:(dc + 1) * 128],
                                     xgT[:, dc * CAP:(dc + 1) * CAP]) for dc in range(NDC)],
                             evac=stt_evac, waits=[cw]),
                    ]
                    emit_jobs(jobs)
                    fr.release[sg_] = jobs[0]["done"]
                    fr.release[s1_] = jobs[1]["done"]
                y_ok = C["sv"]

                wt("vector", "szs", zsc_last[e - 1])
                wt("scalar", "szs", zsc_last[e - 1])
                otp_pend = None

                def emit_otp(items):
                    for (dout, cc2, zap, zev) in items:
                        key = PFN[6 + (dout * 2 + cc2) % 2]
                        reg = pf[6 + (dout * 2 + cc2) % 2][:, 0:128]
                        bank_wait_and_claim(key)
                        wt("tensor", "sv", zev)
                        ct2 = mmtp(reg, zap, idbf[:, :])
                        otp_done[dout % 2] = ct2
                        dst = ztm[:, cc2 * D + dout * 128:cc2 * D + (dout + 1) * 128]
                        if cc2 == 0:
                            wt("vector", "sp", ct2)
                            cz2 = vop(lambda EE, d=dst, r=reg: EE.tensor_copy(d, r))
                            bank_release(key, "sv", cz2)
                        else:
                            wt("scalar", "sp", ct2)
                            cz2 = sop(lambda EE, d=dst, r=reg: EE.copy(d, r))
                            bank_release(key, "ss", cz2)

                for dp in range(8):
                    d0, d1 = 2 * dp, 2 * dp + 1
                    jobs = []
                    for dout in (d0, d1):
                        sw2 = fr.take()
                        cw2 = load_chunk(rreg(sw2, 1024), wsrc("w2", e, dout),
                                         fr.release[sw2], sw2)
                        zslice = zsb[:, (dout % 2) * CAP:(dout % 2) * CAP + CAP]

                        def z_evac(jb, zz=zslice, dd=dout):
                            wt("vector", "sp", jb["done"])
                            wt("vector", "sp", otp_done.get(dd % 2, 0))
                            c = vop(lambda EE, o=zz, b=jb["bank"]: EE.tensor_copy(o, b))
                            jb["zev"] = c
                            return ("sv", c)

                        jobs.append(dict(bankkey=PFN[4 + dout % 2],
                                         bank=pf[4 + dout % 2][:, 0:CAP],
                                         steps=[(rreg(sw2, 1024)[:, fc * 128:(fc + 1) * 128],
                                                 ysp[:, fc * CAP:(fc + 1) * CAP])
                                                for fc in range(NFC)],
                                         evac=z_evac, waits=[cw2, ("sv", y_ok)],
                                         slot=sw2, dout=dout))
                    if otp_pend is not None:
                        emit_otp(otp_pend)
                    emit_jobs(jobs)
                    for jb in jobs:
                        fr.release[jb["slot"]] = jb["done"]
                    otp_pend = []
                    for jb in jobs:
                        dd = jb["dout"]
                        otp_pend.append((dd, 0, zsb[:, (dd % 2) * CAP:(dd % 2) * CAP + 128],
                                         jb["zev"]))
                        otp_pend.append((dd, 1, zsb[:, (dd % 2) * CAP + 128:(dd % 2) * CAP + 256],
                                         jb["zev"]))
                if otp_pend is not None:
                    emit_otp(otp_pend)
                wt("vector", "sv", C["sv"])
                wt("vector", "ss", C["ss"])
                wt("vector", "sv", max(ccb1, ccb2))
                csc1 = vop(lambda EE: EE.tensor_scalar(
                    ztm[:, 0:D], ztm[:, 0:D], cbvsb[:, 0:1], None, ALU.mult))
                wt("vector", "sv", csc1)
                csc2 = vop(lambda EE: EE.tensor_scalar(
                    ztm[:, D:2 * D], ztm[:, D:2 * D], cbvsb[:, 1:2], None, ALU.mult))
                wt("gpsimd", "sv", csc2)
                wt("gpsimd", sie_cnt[e][0], sie_cnt[e][1])
                wt("gpsimd", "sxc", SXC_INIT)
                zl = 0
                for cc in range(2):
                    zl = bump("szs", 16)
                    wt("gpsimd", "szs", zl - 16)
                    em("gpsimd", lambda EE, i=ztm[:, cc * D:(cc + 1) * D],
                       ix=idxesb[:, (e % 2) * 2 + cc:(e % 2) * 2 + cc + 1],
                       s2=sems["szs"]:
                       EE.indirect_dma_start(racc_d[:, :], bass.IndirectOffsetOnAxis(ix, 0),
                                             i, None, bounds_check=rb_tok, oob_is_err=False,
                                             compute_op=ALU.add).then_inc(s2, 16))
                zsc_last[e] = zl
        Z_ALL = C["szs"]

        if _N_PAIRS > 0:
            wt("vector", "szs", C["szs"])
            wt("scalar", "szs", C["szs"])
            pair_es = [E, E + 1]
            ybufs = {E: y0, E + 1: y1}
            for e in pair_es:
                bias_gb = sgbp[:, (e - E) * NFC:(e - E + 1) * NFC]
                bias_b1 = sb1p[:, (e - E) * NFC:(e - E + 1) * NFC]
                ybuf = ybufs[e]
                for fc in range(NFC):
                    sg_ = fr.take()
                    s1_ = fr.take()
                    cg = load_chunk(rreg(sg_), wsrc("g", e, fc), fr.release[sg_], sg_)
                    cw = load_chunk(rreg(s1_), wsrc("w1", e, fc), fr.release[s1_], s1_)
                    bsel = (fc % 2) * 4
                    banks = [bsel, bsel + 1, bsel + 2, bsel + 3]
                    jobs = []
                    for part, (slot, cload) in enumerate([(sg_, cg), (s1_, cw)]):
                        for tt in range(2):
                            steps = [(rreg(slot)[:, dc * 128:(dc + 1) * 128],
                                      xnT[:, dc * STT + HALO + tt * 512:
                                          dc * STT + HALO + (tt + 1) * 512])
                                     for dc in range(NDC)]
                            bidx = banks[part * 2 + tt]
                            gb_ap = bias_gb[:, fc:fc + 1]
                            b1_ap = bias_b1[:, fc:fc + 1]
                            yg_ap = ygc[:, tt * 512:(tt + 1) * 512]
                            ydst = ybuf[:, fc * ST + tt * 512:fc * ST + (tt + 1) * 512]
                            if part == 0:
                                def gelu_evac(jb, o=yg_ap, bb=gb_ap, pv=prev_stt, t=tt):
                                    wt("scalar", "sp", jb["done"])
                                    wt("scalar", "sv", pv[t])
                                    c = sop(lambda EE, oo=o, b=jb["bank"], b2=bb:
                                            EE.activation(oo, b, AF.Gelu, bias=b2[:, 0:1]))
                                    return ("ss", c)
                                jobs.append(dict(bankkey=PFN[bidx], bank=pf[bidx][:, :],
                                                 steps=steps, evac=gelu_evac,
                                                 waits=[cload, ("sv", xnT_ok)]))
                            else:
                                def stt_evac(jb, o=ydst, bb=b1_ap, yg=yg_ap, pv=prev_stt, t=tt):
                                    wt("vector", "sp", jb["done"])
                                    wt("vector", "ss", jb["gelu_ss"])
                                    c = vop(lambda EE, oo=o, b=jb["bank"], b2=bb, yy=yg:
                                            EE.scalar_tensor_tensor(
                                                oo, b, b2[:, 0:1], yy, ALU.add, ALU.mult))
                                    pv[t] = c
                                    return ("sv", c)
                                jobs.append(dict(bankkey=PFN[bidx], bank=pf[bidx][:, :],
                                                 steps=steps, evac=stt_evac, tt=tt,
                                                 waits=[cload]))
                    emit_jobs(jobs[0:2])
                    jobs[2]["gelu_ss"] = C["ss"]
                    jobs[3]["gelu_ss"] = C["ss"]
                    emit_jobs(jobs[2:4])
                    fr.release[sg_] = jobs[1]["done"]
                    fr.release[s1_] = jobs[3]["done"]
            y_ok = C["sv"]

            for dout in range(16):
                slots = {}
                cls = {}
                for e in pair_es:
                    s2_ = fr.take()
                    slots[e] = s2_
                    cls[e] = load_chunk(rreg(s2_, 1024), wsrc("w2", e, dout),
                                        fr.release[s2_], s2_)
                jobs = []
                for tt in range(2):
                    steps = []
                    for e in pair_es:
                        for fc in range(NFC):
                            steps.append((rreg(slots[e], 1024)[:, fc * 128:(fc + 1) * 128],
                                          ybufs[e][:, fc * ST + tt * 512:fc * ST + (tt + 1) * 512]))
                    steps.append((rb2sb[:, dout * 128:(dout + 1) * 128],
                                  combT[:, tt * 512:(tt + 1) * 512]))
                    steps.append((sb2sb[:, dout * 128:(dout + 1) * 128],
                                  ones2b[:, tt * 512:(tt + 1) * 512]))
                    aslice = acc[:, dout * ST + tt * 512:dout * ST + (tt + 1) * 512]
                    def add_evac(jb, a=aslice):
                        wt("vector", "sp", jb["done"])
                        c = vop(lambda EE, aa=a, b=jb["bank"]: EE.tensor_tensor(aa, b, aa, ALU.add))
                        return ("sv", c)
                    bidx = (dout * 2 + tt) % 4
                    jobs.append(dict(bankkey=PFN[bidx], bank=pf[bidx][:, :], steps=steps,
                                     waits=[cls[pair_es[0]], cls[pair_es[1]], ("sv", y_ok)],
                                     evac=add_evac))
                emit_jobs(jobs)
                for e in pair_es:
                    fr.release[slots[e]] = jobs[1]["done"]

        acc_ok = C["sv"]
        ost_cnt = {}
        for ti in range(0 if _NO_P4 else 8):
            buf = ti % 2
            xs = xstage[:, buf * D:(buf + 1) * D]
            osb = ostage[:, buf * D:(buf + 1) * D]
            wt("sync", "sv", C["sv"])
            wt("sync", "ss", C["ss"])
            wt("sync", "sp", C["sp"])
            sxn = "sxa" if buf == 0 else "sxb"
            cx = dma("sync", xs, dr["x_sl"][row0 + HALO + ti * 128:row0 + HALO + ti * 128 + 128, :],
                     sxn)
            rsb = rstage[:, buf * D:(buf + 1) * D]
            wt("sync", "szs", Z_ALL if _N_PAIRS > 0 else 0)
            srn = "sra" if buf == 0 else "srb"
            crc = dma("sync", rsb, racc_d[ti * 128:(ti + 1) * 128, :], srn)
            wt("vector", sxn, cx)
            wt("vector", srn, crc)
            if ti >= 2:
                wt("vector", "soa" if buf == 0 else "sob", ost_cnt[ti - 2])
            cra = vop(lambda EE, o=osb, i=xs, r=rsb: EE.tensor_tensor(o, i, r, ALU.add))
            wt("vector", "sv", cra)
            wt("tensor", "sv", acc_ok)
            for dc in range(NDC):
                key = PFN[dc % 6]
                reg = pf[dc % 6][:, 0:128]
                bank_wait_and_claim(key)
                cc = mmtp(reg, acc[:, dc * ST + ti * 128:dc * ST + (ti + 1) * 128], idbf[:, :])
                wt("vector", "sp", cc)
                ce = vop(lambda EE, o=osb, d=dc, r=reg: EE.tensor_tensor(
                    o[:, d * 128:(d + 1) * 128], r, o[:, d * 128:(d + 1) * 128], ALU.add))
                bank_release(key, "sv", ce)
            wt("sync", "sv", C["sv"])
            ost_cnt[ti] = dma("sync", out_h[st * 1024 + ti * 128:st * 1024 + (ti + 1) * 128, :],
                              osb, "soa" if buf == 0 else "sob")

    wt("sync", "soa", C["soa"])
    wt("sync", "sob", C["sob"])

    with nc.Block() as block:
        @block.sync
        def _(eng):
            for fn in plan["sync"]:
                fn(eng)

        @block.gpsimd
        def _(eng):
            for fn in plan["gpsimd"]:
                fn(eng)

        @block.tensor
        def _(eng):
            for fn in plan["tensor"]:
                fn(eng)

        @block.vector
        def _(eng):
            for fn in plan["vector"]:
                fn(eng)

        @block.scalar
        def _(eng):
            for fn in plan["scalar"]:
                fn(eng)

    for s in reversed(sem_ctx):
        s.__exit__(None, None, None)
    for p in reversed(psum_ctx):
        p.__exit__(None, None, None)
    return nc


def prep_inputs(L0_attn_x, L0_attn_Wq, L0_attn_bq, L0_attn_Wk, L0_attn_bk,
                L0_attn_Wv, L0_attn_bv, L0_attn_Wo, L0_attn_bo, L0_ffn_wa,
                L0_ffn_rg, L0_ffn_rgb, L0_ffn_rw1, L0_ffn_rb1, L0_ffn_rw2,
                L0_ffn_rb2, L0_ffn_sg, L0_ffn_sgb, L0_ffn_sw1, L0_ffn_sb1,
                L0_ffn_sw2, L0_ffn_sb2):
    f32 = np.float32
    asnp = lambda a: np.asarray(a, dtype=f32)
    x = asnp(L0_attn_x)
    sh = {}
    sh["wq_p"] = np.ascontiguousarray(
        asnp(L0_attn_Wq).reshape(16, 128, H, 128).transpose(2, 1, 0, 3).reshape(H, 128, D))
    sh["wk_p"] = np.ascontiguousarray(
        asnp(L0_attn_Wk).reshape(16, 128, KVH, 128).transpose(2, 1, 0, 3).reshape(KVH, 128, D))
    sh["wv_p"] = asnp(L0_attn_Wv).reshape(16, 128, 512)
    sh["wo_p"] = np.ascontiguousarray(
        asnp(L0_attn_Wo).reshape(KVH, 4, 128, 16, 128).transpose(3, 0, 2, 1, 4)
        .reshape(16, KVH, 128, 512))
    sh["bq_r"] = asnp(L0_attn_bq).reshape(1, D)
    sh["bk_r"] = asnp(L0_attn_bk).reshape(1, 512)
    sh["bv_r"] = asnp(L0_attn_bv).reshape(1, 512)
    sh["bo_r"] = asnp(L0_attn_bo).reshape(1, D)
    sh["wa_n"] = asnp(L0_ffn_wa)
    def dfp(w, ne):
        return np.ascontiguousarray(
            w.reshape(ne, 16, 128, NFC, 128).transpose(0, 3, 2, 1, 4).reshape(ne, NFC, 128, D))
    def fdp(w, ne):
        return np.ascontiguousarray(
            w.reshape(ne, NFC, 128, 16, 128).transpose(0, 3, 2, 1, 4).reshape(ne, 16, 128, F))
    sh["rg_p"] = dfp(asnp(L0_ffn_rg), E)
    sh["rw1_p"] = dfp(asnp(L0_ffn_rw1), E)
    sh["rw2_p"] = fdp(asnp(L0_ffn_rw2), E)
    sh["sg_p"] = dfp(asnp(L0_ffn_sg), SH)
    sh["sw1_p"] = fdp(asnp(L0_ffn_sw2), SH) if False else dfp(asnp(L0_ffn_sw1), SH)
    sh["sw2_p"] = fdp(asnp(L0_ffn_sw2), SH)
    sh["rgb_p"] = np.ascontiguousarray(
        asnp(L0_ffn_rgb).reshape(E, NFC, 128).transpose(2, 0, 1).reshape(128, E * NFC))
    sh["rb1_p"] = np.ascontiguousarray(
        asnp(L0_ffn_rb1).reshape(E, NFC, 128).transpose(2, 0, 1).reshape(128, E * NFC))
    sh["sgb_p"] = np.ascontiguousarray(
        asnp(L0_ffn_sgb).reshape(SH, NFC, 128).transpose(2, 0, 1).reshape(128, SH * NFC))
    sh["sb1_p"] = np.ascontiguousarray(
        asnp(L0_ffn_sb1).reshape(SH, NFC, 128).transpose(2, 0, 1).reshape(128, SH * NFC))
    sh["rb2_n"] = asnp(L0_ffn_rb2)
    sh["sb2_n"] = asnp(L0_ffn_sb2)
    sh["id128"] = np.eye(128, dtype=f32)
    em = np.zeros((E, E * 128), dtype=ml_dtypes.bfloat16)
    for e in range(E):
        em[e, e * 128:(e + 1) * 128] = 1
    sh["emat"] = em
    sh["onesr"] = np.ones((1, 512), dtype=ml_dtypes.bfloat16)
    sh["ecap_c"] = (np.arange(16) * CAP).reshape(16, 1).astype(f32)
    sh["iotatok"] = (np.arange(8)[None, :] * 128 + np.arange(128)[:, None]).astype(np.int32)
    sh["ones2"] = np.ones((2, 1024), dtype=ml_dtypes.bfloat16)

    qi = np.arange(128)[:, None]
    tj = np.arange(384)[None, :]
    band = (tj >= qi + 1) & (tj <= qi + 256)
    mk = lambda m: np.where(m, 0.0, NEG).astype(f32)
    m_band = mk(band)
    m_e0 = mk(band & (tj >= 256))
    m_e1 = mk(band & (tj >= 128))

    in_maps = []
    for c in range(8):
        b, h = c // 2, c % 2
        sl = np.zeros((2304, D), dtype=f32)
        lo = h * 2048 - 256
        src_lo = max(lo, 0)
        sl[src_lo - lo:, :] = x[b, src_lo:h * 2048 + 2048, :]
        m = dict(sh)
        m["x_sl"] = sl
        m["masks"] = (np.stack([m_e0, m_e1, m_band]) if h == 0
                      else np.stack([m_band, m_band, m_band]))
        in_maps.append(m)
    return in_maps


def kernel(**inputs):
    in_maps = prep_inputs(**inputs)
    nc = build_program()
    res = run_bass_kernel_spmd(nc, in_maps, core_ids=list(range(8)))
    out = np.empty((4, 4096, D), dtype=np.float32)
    for c in range(8):
        b, h = c // 2, c % 2
        out[b, h * 2048:(h + 1) * 2048, :] = res.results[c]["out"]
    return out
